# version 18
# baseline (speedup 1.0000x reference)
import numpy as np
import ml_dtypes
from contextlib import ExitStack

import concourse.bass as bass
import concourse.mybir as mybir
from concourse.bass_utils import run_bass_kernel_spmd
from concourse.bass_core_rust import IndirectOffsetOnAxis

F32 = mybir.dt.float32
BF16 = mybir.dt.bfloat16
I32 = mybir.dt.int32
AF = mybir.ActivationFunctionType
ALU = mybir.AluOpType
AX = mybir.AxisListType
BFNP = ml_dtypes.bfloat16

ENGINES = ("tensor", "vector", "scalar", "gpsimd", "sync")

DBG = {"no_igs": False, "no_gx": False, "no_dsc": False, "no_acc": False,
       "ffn_only": False}

REAL_CFG = dict(B=2, N=2048, D=1024, DFF=4096, E=8,
                TILE=1024, NCH=512, CH=256, SC=128, FCC=256, CAPR=256)


class Builder:

    def __init__(self, nc, ctx):
        self.nc = nc
        self.prog = {e: [] for e in ENGINES}
        self.semh = {}
        self.semc = {}
        self.floors = {e: {} for e in ENGINES}
        self.buf = {}
        self._ctx = ctx
        for name in ["s_tensor", "s_vector", "s_scalar", "s_gpsimd", "s_sync"]:
            self.semh[name] = ctx.enter_context(nc.semaphore(name))
            self.semc[name] = 0

    def op(self, eng, emit, reads=(), writes=(), dma=False, dsem=None):
        if dma:
            assert dsem is not None
            sem = "dma_" + dsem
            if sem not in self.semh:
                self.semh[sem] = self._ctx.enter_context(self.nc.semaphore(sem))
                self.semc[sem] = 0
            inc = 16
        else:
            sem = "s_" + eng
            inc = 1
        needs = {}

        def want(mark):
            s, v = mark
            if eng == "tensor" and s == "s_tensor":
                return
            if v > needs.get(s, 0):
                needs[s] = v

        for k in reads:
            st = self.buf.get(k)
            if st and st["w"]:
                want(st["w"])
        for k in writes:
            st = self.buf.get(k)
            if st:
                if st["w"]:
                    want(st["w"])
                for r in st["r"]:
                    want(r)
        lst = self.prog[eng]
        fl = self.floors[eng]
        for s in sorted(needs):
            v = needs[s]
            if fl.get(s, 0) >= v:
                continue
            fl[s] = v
            h = self.semh[s]
            lst.append(lambda E, h=h, v=v: E.wait_ge(h, v))
        self.semc[sem] += inc
        h = self.semh[sem]
        lst.append(lambda E, h=h, emit=emit, inc=inc: emit(E).then_inc(h, inc))
        mark = (sem, self.semc[sem])
        for k in writes:
            self.buf[k] = {"w": mark, "r": []}
        for k in reads:
            if k in writes:
                continue
            st = self.buf.setdefault(k, {"w": None, "r": []})
            st["r"].append(mark)
        return mark

    def barrier(self):
        targets = [(s, c) for s, c in self.semc.items() if c > 0]
        for e in ENGINES:
            fl = self.floors[e]
            for s, v in targets:
                if fl.get(s, 0) >= v:
                    continue
                fl[s] = v
                h = self.semh[s]
                self.prog[e].append(lambda E, h=h, v=v: E.wait_ge(h, v))
        self.buf.clear()

    def play(self, ctx):
        block = ctx.enter_context(self.nc.Block())
        prog = self.prog

        @block.tensor
        def _(E):
            for f in prog["tensor"]:
                f(E)

        @block.vector
        def _(E):
            for f in prog["vector"]:
                f(E)

        @block.scalar
        def _(E):
            for f in prog["scalar"]:
                f(E)

        @block.gpsimd
        def _(E):
            for f in prog["gpsimd"]:
                f(E)

        @block.sync
        def _(E):
            for f in prog["sync"]:
                f(E)


def build_program(cfg):
    B, N, D, DFF, E8 = cfg["B"], cfg["N"], cfg["D"], cfg["DFF"], cfg["E"]
    TILE, NCH, CH, SC, FCC = (cfg["TILE"], cfg["NCH"], cfg["CH"], cfg["SC"],
                              cfg["FCC"])
    CAPR = cfg["CAPR"]
    CAP2 = 2 * CAPR
    NG = CAPR // 128
    GPE = 2 * NG
    C8 = TILE // 128
    SLOT_TOT = (cfg["E"]) * CAP2
    assert CAPR % 128 == 0 and CAP2 <= NCH
    TQ = 4 if (D // 128) % 4 == 0 else D // 128
    CQ = 4 if C8 % 4 == 0 else min(4, C8)
    DP = D // 128
    DFP = DFF // 128
    NT = N // TILE
    NSC = CH // SC
    assert TILE % NCH == 0 and N % CH == 0 and CH % SC == 0
    assert DFF % FCC == 0 and FCC % 128 == 0 and N % NCH == 0

    nc = bass.Bass("TRN2", target_bir_lowering=False)

    xd = nc.dram_tensor("x", [B, D, N], F32, kind="ExternalInput").ap()
    wqd = nc.dram_tensor("wq", [D, D], F32, kind="ExternalInput").ap()
    wkd = nc.dram_tensor("wk", [D, D], F32, kind="ExternalInput").ap()
    wvd = nc.dram_tensor("wv", [D, D], F32, kind="ExternalInput").ap()
    wod = nc.dram_tensor("wo", [D, D], F32, kind="ExternalInput").ap()
    nwd = nc.dram_tensor("nw", [128, DP], F32, kind="ExternalInput").ap()
    rtd = nc.dram_tensor("rt", [D, E8], F32, kind="ExternalInput").ap()
    fct = nc.dram_tensor("fct", [E8 + 1, D, DFF], F32, kind="ExternalInput").ap()
    pjt = nc.dram_tensor("pjt", [E8 + 1, DFF, D], F32, kind="ExternalInput").ap()
    identb = nc.dram_tensor("identb", [128, 128], BF16, kind="ExternalInput").ap()
    identf = nc.dram_tensor("identf", [128, 128], F32, kind="ExternalInput").ap()
    onescb = nc.dram_tensor("onescb", [128, 1], BF16, kind="ExternalInput").ap()
    onescf = nc.dram_tensor("onescf", [128, 1], F32, kind="ExternalInput").ap()
    onesrf = nc.dram_tensor("onesrf", [1, 128], F32, kind="ExternalInput").ap()
    halfrf = nc.dram_tensor("halfrf", [1, 128], F32, kind="ExternalInput").ap()
    halfrb = nc.dram_tensor("halfrb", [1, 128], BF16, kind="ExternalInput").ap()
    maskbd = nc.dram_tensor("maskb", [SC, SC], BF16, kind="ExternalInput").ap()
    onesrb = nc.dram_tensor("onesrb", [1, 128], BF16, kind="ExternalInput").ap()
    ecolsd = nc.dram_tensor("ecols", [128, C8 * E8], F32, kind="ExternalInput").ap()
    tvld = nc.dram_tensor("tvl", [128, C8], F32, kind="ExternalInput").ap()
    oobsd = nc.dram_tensor("oobs", [128, SLOT_TOT * 2 // 128], F32,
                           kind="ExternalInput").ap()
    outd = nc.dram_tensor("out", [B, D, N], F32, kind="ExternalOutput").ap()
    NDUMP = 4 * (CAPR // 128)
    xtmd_t = nc.dram_tensor("xtmd", [TILE + NDUMP, D], BF16, kind="Internal")
    igbd_t = nc.dram_tensor("igbd", [SLOT_TOT + 1, 2], F32, kind="Internal")
    dltA_t = nc.dram_tensor("dltA", [TILE + NDUMP, D], BF16, kind="Internal")
    dltB_t = nc.dram_tensor("dltB", [TILE + NDUMP, D], BF16, kind="Internal")
    fctb = nc.dram_tensor("fctb", [E8 + 1, DFF // FCC, 128, (D // 128) * FCC],
                          BF16, kind="Internal").ap()
    pjtb = nc.dram_tensor("pjtb", [E8 + 1, D // 128, 128, (DFF // 128) * 128],
                          BF16, kind="Internal").ap()
    awtb = nc.dram_tensor("awtb", [4, 128, (D // 128) * D], BF16,
                          kind="Internal").ap()

    top = ExitStack()
    with top:
        b = Builder(nc, top)
        sb = lambda nm, sh, dt: top.enter_context(nc.sbuf_tensor(nm, sh, dt))

        xs = sb("xs", [128, DP * N], F32)
        statA = sb("statA", [1, N], BF16)
        statB = sb("statB", [1, N], BF16)
        idb = sb("idb", [128, 128], BF16)
        idf = sb("idf", [128, 128], F32)
        ocb = sb("ocb", [128, 1], BF16)
        orf = sb("orf", [1, 128], F32)
        hrf = sb("hrf", [1, 128], F32)
        hrb = sb("hrb", [1, 128], BF16)
        mkb = sb("mkb", [SC, SC], BF16)
        nws = sb("nws", [128, DP], F32)
        rts = sb("rts", [128, DP * E8], BF16)
        rtsF = sb("rtsF", [128, DP * E8], F32)
        ocf = sb("ocf", [128, 1], F32)
        sbc = sb("sbc", [128, E8], F32)
        srow = sb("srow", [1, E8], F32)
        zcol = sb("zcol", [128, max(SC, NCH)], F32)
        orb = sb("orb", [1, 128], BF16)
        ecols = sb("ecols_s", [128, C8 * E8], F32)
        tvl = sb("tvl_s", [128, C8], F32)
        oobs = sb("oobs_s", [128, SLOT_TOT * 2 // 128], F32)
        eps5 = sb("eps5", [128, 1], F32)
        eps6 = sb("eps6", [128, 1], F32)
        hcolP = sb("hcolP", [128, 1], F32)
        hcolN = sb("hcolN", [128, 1], F32)

        XBLK = min(512, N)

        def xk(pt, blk=None):
            if blk is None:
                return [("x", pt, i) for i in range(N // XBLK)]
            return ("x", pt, blk)

        def xks(pt, t0, t1):
            return [("x", pt, i) for i in range(t0 // XBLK,
                                                (t1 + XBLK - 1) // XBLK)]

        def xsl(pt, t0, t1):
            return xs[:, pt * N + t0:pt * N + t1]

        b.op("sync", lambda E: E.dma_start(out=idb[:, :], in_=identb),
             writes=["idb"], dma=True, dsem="idb")
        b.op("sync", lambda E: E.dma_start(out=idf[:, :], in_=identf),
             writes=["idf"], dma=True, dsem="idf")
        b.op("sync", lambda E: E.dma_start(out=ocb[:, :], in_=onescb),
             writes=["ocb"], dma=True, dsem="ocb")
        b.op("sync", lambda E: E.dma_start(out=ocf[:, :], in_=onescf),
             writes=["ocf"], dma=True, dsem="ocf")
        b.op("sync", lambda E: E.dma_start(out=orf[:, :], in_=onesrf),
             writes=["orf"], dma=True, dsem="orf")
        b.op("sync", lambda E: E.dma_start(out=hrf[:, :], in_=halfrf),
             writes=["hrf"], dma=True, dsem="hrf")
        b.op("sync", lambda E: E.dma_start(out=hrb[:, :], in_=halfrb),
             writes=["hrb"], dma=True, dsem="hrb")
        b.op("sync", lambda E: E.dma_start(out=mkb[:, :], in_=maskbd),
             writes=["mkb"], dma=True, dsem="mkb")
        b.op("sync", lambda E: E.dma_start(out=nws[:, :], in_=nwd),
             writes=["nws"], dma=True, dsem="nws")
        b.op("sync", lambda E: E.dma_start(
            out=rtsF[:, :].rearrange("p (k e) -> p k e", k=DP),
            in_=rtd.rearrange("(k p) e -> p k e", p=128)),
            writes=["rtsF"], dma=True, dsem="rts")
        b.op("gpsimd", lambda E: E.tensor_copy(
            rts[:, :], rtsF[:, :]),
            reads=["rtsF"], writes=["rts"])
        b.op("gpsimd", lambda E: E.memset(zcol[:, :], 0.0), writes=["zcol"])
        b.op("sync", lambda E: E.dma_start(out=orb[:, :], in_=onesrb),
             writes=["orb"], dma=True, dsem="orb")
        b.op("sync", lambda E: E.dma_start(out=ecols[:, :], in_=ecolsd),
             writes=["ecols"], dma=True, dsem="ecols")
        b.op("sync", lambda E: E.dma_start(out=tvl[:, :], in_=tvld),
             writes=["tvl"], dma=True, dsem="tvl")
        b.op("sync", lambda E: E.dma_start(out=oobs[:, :], in_=oobsd),
             writes=["oobs"], dma=True, dsem="oobs")
        b.op("gpsimd", lambda E: E.memset(eps5[:, :], 1e-5), writes=["eps5"])
        b.op("gpsimd", lambda E: E.memset(eps6[:, :], 1e-6), writes=["eps6"])
        b.op("gpsimd", lambda E: E.memset(hcolP[:, :], 44.36), writes=["hcolP"])
        b.op("gpsimd", lambda E: E.memset(hcolN[:, :], -44.36), writes=["hcolN"])
        ZW = max(SC, NCH)
        for gg in range(NDUMP):
            for off in range(0, D, ZW):
                w = min(ZW, D - off)
                b.op("gpsimd", lambda E, gg=gg, off=off, w=w: E.dma_start(
                    out=xtmd_t.ap()[TILE + gg:TILE + gg + 1, off:off + w],
                    in_=zcol[0:1, 0:w]),
                    reads=["zcol"], writes=[("xtmd", "dump", gg, off)],
                    dma=True, dsem="xdump")

        def ln_stats(tag, xbB, sqB, xbK, sqK, psA, psAk, psB, psBk, rowt, rowtK,
                     eps):
            for c in range(N // NCH):
                t0 = c * NCH
                for pt in range(DP):
                    p = pt % 2
                    xb, sq = xbB[p], sqB[p]
                    kxb, ksq = xbK[p], sqK[p]
                    b.op("vector", lambda E, xb=xb, pt=pt, t0=t0:
                         E.tensor_copy(xb[:, 0:NCH], xsl(pt, t0, t0 + NCH)),
                         reads=xks(pt, t0, t0 + NCH), writes=kxb)
                    b.op("vector", lambda E, sq=sq, xb=xb: E.tensor_tensor(
                        sq[:, 0:NCH], xb[:, 0:NCH], xb[:, 0:NCH], ALU.mult),
                        reads=kxb, writes=ksq)
                    st, sp = (pt == 0), (pt == DP - 1)
                    b.op("tensor", lambda E, xb=xb, st=st, sp=sp: E.matmul(
                        psA[0:1, 0:NCH], ocb[:, :], xb[:, 0:NCH],
                        start=st, stop=sp),
                        reads=list(kxb) + ["ocb"], writes=[psAk])
                    b.op("tensor", lambda E, sq=sq, st=st, sp=sp: E.matmul(
                        psB[0:1, 0:NCH], ocb[:, :], sq[:, 0:NCH],
                        start=st, stop=sp),
                        reads=list(ksq) + ["ocb"], writes=[psBk])
                b.op("scalar", lambda E, t0=t0: E.activation(
                    statA[0:1, t0:t0 + NCH], psA[0:1, 0:NCH], AF.Copy,
                    scale=1.0 / D),
                    reads=[psAk], writes=["statA"])
                b.op("scalar", lambda E, t0=t0: E.activation(
                    statB[0:1, t0:t0 + NCH], psB[0:1, 0:NCH], AF.Copy,
                    scale=1.0 / D),
                    reads=[psBk], writes=["statB"])
                b.op("vector", lambda E, t0=t0: E.tensor_tensor(
                    rowt[0:1, 0:NCH], statA[0:1, t0:t0 + NCH],
                    statA[0:1, t0:t0 + NCH], ALU.mult),
                    reads=["statA"], writes=[rowtK])
                b.op("vector", lambda E, t0=t0: E.tensor_tensor(
                    statB[0:1, t0:t0 + NCH], statB[0:1, t0:t0 + NCH],
                    rowt[0:1, 0:NCH], ALU.subtract),
                    reads=[rowtK, "statB"], writes=["statB"])
                b.op("scalar", lambda E, t0=t0: E.activation(
                    statB[0:1, t0:t0 + NCH], statB[0:1, t0:t0 + NCH], AF.Ln,
                    bias=eps[0:1, 0:1]),
                    reads=["statB", "eps5"], writes=["statB"])
                b.op("scalar", lambda E, t0=t0: E.activation(
                    statB[0:1, t0:t0 + NCH], statB[0:1, t0:t0 + NCH], AF.Exp,
                    scale=-0.5),
                    reads=["statB"], writes=["statB"])

        weight_cached = set()

        def ffn_phase(tag):
            fc_ctx = ExitStack()
            with fc_ctx:
                fs = lambda nm, sh, dt: fc_ctx.enter_context(
                    nc.sbuf_tensor(tag + nm, sh, dt))
                fp = lambda nm, sh, dt: fc_ctx.enter_context(
                    nc.psum_tensor(tag + nm, sh, dt))
                K = lambda *a: (tag,) + a
                half = DFP // 2

                xln = fs("xln", [128, DP * TILE], BF16)
                xtm = fs("xtm", [128, C8 * D], BF16)
                hg = fs("hg", [128, DFP * NCH], BF16)
                xgog = [fs(f"xgog{i}", [128, GPE * D], BF16)
                        for i in range(2)]
                xg = fs("xg", [128, DP * CAP2], BF16)
                ogf = xg
                fcb = [fs(f"fcb{i}", [128, DP * FCC], BF16) for i in range(2)]
                pjb = [fs(f"pjb{i}", [128, DFP * 128], BF16) for i in range(2)]
                h1 = [fs(f"h1{i}", [128, NCH], BF16) for i in range(2)]
                evt = fs("evt", [128, NCH], F32)
                SEL = [fs(f"SEL{i}", [E8, TILE], BF16) for i in range(2)]
                pos = fs("pos", [E8, TILE], F32)
                oha = [fs(f"oha{i}", [128, C8 * E8], F32) for i in range(2)]
                posm = fs("posm", [128, C8 * E8], F32)
                ea_ = [fs(f"ea{i}", [128, C8], F32) for i in range(2)]
                pa_ = [fs(f"pa{i}", [128, C8], F32) for i in range(2)]
                si_ = [fs(f"si{i}", [128, C8], I32) for i in range(2)]
                ii_ = [fs(f"ii{i}", [128, C8], I32) for i in range(4)]
                sval = [fs(f"sval{i}", [128, 2 * C8], F32) for i in range(2)]
                idg = fs("idg", [128, GPE * E8 * 2], F32)
                ido = fs("ido", [128, GPE * E8], I32)
                lgb = fs("lgb", [128, E8], F32)
                tmp8 = fs("tmp8", [128, E8], F32)
                mcols = fs("mcols", [128, 2 * C8], F32)
                rcols = fs("rcols", [128, 2 * C8], F32)
                m1b = fs("m1b", [128, 1], F32)
                m2b = fs("m2b", [128, 1], F32)
                nm2b = fs("nm2b", [128, 1], F32)

                hP = [fp(f"hP{i}", [128, NCH], F32) for i in range(2)]
                ogP = [fp(f"ogP{i}", [128, CAP2], F32) for i in range(2)]
                trP = [fp(f"trP{i}", [128, 512], BF16) for i in range(2)]
                bcP = [fp(f"bcP{i}", [128, NCH], F32) for i in range(2)]

                ln_stats(tag, [h1[0][:, 0:NCH], h1[1][:, 0:NCH]],
                         [xgog[0][:, 0:NCH], xgog[1][:, 0:NCH]],
                         [[K("h1", 0)], [K("h1", 1)]],
                         [[K("xgog", 0, 0)], [K("xgog", 1, 0)]],
                         bcP[0], K("bcP", 0), bcP[1], K("bcP", 1),
                         evt, K("evt"), eps5)

                if ("srow",) not in weight_cached:
                    weight_cached.add(("srow",))
                    for kpt in range(DP):
                        b.op("tensor", lambda E, kpt=kpt: E.matmul(
                            bcP[0][0:1, 0:E8], ocf[:, :],
                            rtsF[:, kpt * E8:(kpt + 1) * E8],
                            start=(kpt == 0), stop=(kpt == DP - 1)),
                            reads=["ocf", "rtsF"], writes=[K("bcP", 0)])
                    b.op("scalar", lambda E: E.copy(
                        srow[0:1, :], bcP[0][0:1, 0:E8]),
                        reads=[K("bcP", 0)], writes=["srow"])
                    b.op("tensor", lambda E: E.matmul(
                        bcP[0][:, 0:E8], orf[:, :], srow[0:1, :],
                        start=True, stop=True),
                        reads=["srow", "orf"], writes=[K("bcP", 0)])
                    b.op("scalar", lambda E: E.copy(
                        sbc[:, :], bcP[0][:, 0:E8]),
                        reads=[K("bcP", 0)], writes=["sbc"])

                for par in range(2):
                    b.op("gpsimd", lambda E, par=par: E.memset(
                        xgog[par][:, :], 0.0),
                        writes=[K("xgog", par, g) for g in range(GPE)])
                for r in range(2):
                    b.op("vector", lambda E, r=r: E.tensor_copy(
                        sval[r][:, :].rearrange("p (c v) -> p c v", v=2)
                        [:, :, 0:1],
                        tvl[:, :].rearrange("p (c one) -> p c one", one=1)),
                        reads=["tvl"], writes=[K("sval", r)])

                def stream_fc(e, fcc, buf, bufk):
                    ck = ("wfc", e, fcc)
                    if ck not in weight_cached:
                        weight_cached.add(ck)
                        PW = NCH
                        kpp = max(1, PW // FCC)
                        for pc in range(DP * FCC // PW):
                            k0 = pc * kpp
                            b.op("sync", lambda E, e=e, fcc=fcc, k0=k0,
                                 kpp=kpp: E.dma_start(
                                     out=evt[:, :].rearrange(
                                         "p (k c) -> p k c", k=kpp),
                                     in_=fct.rearrange(
                                         "e (k p) c -> e p k c", p=128)
                                     [e, :, k0:k0 + kpp,
                                      fcc * FCC:(fcc + 1) * FCC]),
                                 reads=[K("evt")], writes=[K("evt")],
                                 dma=True, dsem="wst")
                            b.op("gpsimd", lambda E, buf=buf, pc=pc, PW=PW:
                                 E.tensor_copy(
                                     buf[:, pc * PW:(pc + 1) * PW],
                                     evt[:, :]),
                                 reads=[K("evt")], writes=[bufk])
                        b.op("sync", lambda E, e=e, fcc=fcc, buf=buf:
                             E.dma_start(out=fctb[e, fcc], in_=buf[:, :]),
                             reads=[bufk], writes=[("fctb", e, fcc)],
                             dma=True, dsem=f"fcw{fcc % 2}")
                    else:
                        b.op("gpsimd", lambda E, e=e, fcc=fcc, buf=buf:
                             E.dma_start(out=buf[:, :], in_=fctb[e, fcc]),
                             reads=[("fctb", e, fcc)], writes=[bufk],
                             dma=True, dsem=f"fcr{fcc % 2}")

                def stream_pj(e, mo, buf, bufk, par):
                    ck = ("wpj", e, mo)
                    if ck not in weight_cached:
                        weight_cached.add(ck)
                        PW = NCH
                        kpj = PW // 128
                        for pc in range(DFP * 128 // PW):
                            k0 = pc * kpj
                            b.op("sync", lambda E, e=e, mo=mo, k0=k0,
                                 kpj=kpj: E.dma_start(
                                     out=evt[:, :].rearrange(
                                         "p (k c) -> p k c", k=kpj),
                                     in_=pjt.rearrange(
                                         "e (k p) c -> e p k c", p=128)
                                     [e, :, k0:k0 + kpj,
                                      mo * 128:(mo + 1) * 128]),
                                 reads=[K("evt")], writes=[K("evt")],
                                 dma=True, dsem="wst")
                            b.op("gpsimd", lambda E, buf=buf, pc=pc, PW=PW:
                                 E.tensor_copy(
                                     buf[:, pc * PW:(pc + 1) * PW],
                                     evt[:, :]),
                                 reads=[K("evt")], writes=[bufk])
                        b.op("sync", lambda E, e=e, mo=mo, buf=buf:
                             E.dma_start(
                                 out=pjtb[e, mo], in_=buf[:, :]),
                             reads=[bufk], writes=[("pjtb", e, mo)],
                             dma=True, dsem=f"pjw{par}")
                    else:
                        b.op("sync", lambda E, e=e, mo=mo, buf=buf:
                             E.dma_start(
                                 out=buf[:, :], in_=pjtb[e, mo]),
                             reads=[("pjtb", e, mo)], writes=[bufk],
                             dma=True, dsem=f"pjr{par}")

                nchn = TILE // NCH
                for tile in range(NT):
                    tt0 = tile * TILE
                    for tc in range(C8):
                        t0c = tt0 + tc * 128
                        b.op("tensor", lambda E, t0c=t0c, tc=tc: E.transpose(
                            trP[0][0:128, 2 * tc:2 * tc + 1],
                            statA[0:1, t0c:t0c + 128], idb[0:1, 0:1]),
                            reads=["statA", "idb"], writes=[K("trP", 0)])
                        b.op("tensor", lambda E, t0c=t0c, tc=tc: E.transpose(
                            trP[1][0:128, 2 * tc:2 * tc + 1],
                            statB[0:1, t0c:t0c + 128], idb[0:1, 0:1]),
                            reads=["statB", "idb"], writes=[K("trP", 1)])
                    for tc in range(C8):
                        b.op("scalar", lambda E, tc=tc: E.copy(
                            mcols[:, 2 * tc:2 * tc + 1],
                            trP[0][0:128, 2 * tc:2 * tc + 1]),
                            reads=[K("trP", 0)], writes=[K("mcols")])
                        b.op("scalar", lambda E, tc=tc: E.copy(
                            rcols[:, 2 * tc:2 * tc + 1],
                            trP[1][0:128, 2 * tc:2 * tc + 1]),
                            reads=[K("trP", 1)], writes=[K("rcols")])
                    for tc in range(C8):
                        t0c = tt0 + tc * 128
                        for pt in range(DP):
                            b.op("tensor", lambda E, pt=pt, t0c=t0c:
                                 E.matmul(
                                     bcP[0][:, 0:E8],
                                     xsl(pt, t0c, t0c + 128),
                                     rtsF[:, pt * E8:(pt + 1) * E8],
                                     start=(pt == 0), stop=(pt == DP - 1)),
                                 reads=xks(pt, t0c, t0c + 128) + ["rtsF"],
                                 writes=[K("bcP", 0)])
                        b.op("vector", lambda E, tc=tc: E.tensor_scalar(
                            tmp8[:, :], sbc[:, :], mcols[:, 2 * tc:2 * tc + 1], None,
                            ALU.mult),
                            reads=["sbc", K("mcols")], writes=[K("tmp8")])
                        b.op("vector", lambda E: E.tensor_tensor(
                            lgb[:, :], bcP[0][:, 0:E8], tmp8[:, :],
                            ALU.subtract),
                            reads=[K("bcP", 0), K("tmp8")], writes=[K("lgb")])
                        b.op("vector", lambda E, tc=tc: E.tensor_scalar(
                            lgb[:, :], lgb[:, :], rcols[:, 2 * tc:2 * tc + 1], None,
                            ALU.mult),
                            reads=[K("lgb"), K("rcols")], writes=[K("lgb")])
                        b.op("vector", lambda E: E.reduce_max(
                            m1b[:, :], lgb[:, :], AX.X),
                            reads=[K("lgb")], writes=[K("m1b")])
                        o1 = oha[0][:, tc * E8:(tc + 1) * E8]
                        o2 = oha[1][:, tc * E8:(tc + 1) * E8]
                        b.op("vector", lambda E, o1=o1: E.tensor_scalar(
                            o1, lgb[:, :], m1b[:, 0:1], None, ALU.is_ge),
                            reads=[K("lgb"), K("m1b")], writes=[K("oha", 0)])
                        b.op("vector", lambda E, o1=o1: E.scalar_tensor_tensor(
                            tmp8[:, :], o1, -1e30, lgb[:, :],
                            ALU.mult, ALU.add),
                            reads=[K("oha", 0), K("lgb")], writes=[K("tmp8")])
                        b.op("vector", lambda E: E.reduce_max(
                            m2b[:, :], tmp8[:, :], AX.X),
                            reads=[K("tmp8")], writes=[K("m2b")])
                        b.op("vector", lambda E, o2=o2: E.tensor_scalar(
                            o2, tmp8[:, :], m2b[:, 0:1], None, ALU.is_ge),
                            reads=[K("tmp8"), K("m2b")], writes=[K("oha", 1)])
                        b.op("scalar", lambda E: E.activation(
                            nm2b[:, :], m2b[:, 0:1], AF.Copy, scale=-1.0),
                            reads=[K("m2b")], writes=[K("nm2b")])
                        b.op("scalar", lambda E, tc=tc: E.activation(
                            sval[0][:, 2 * tc + 1:2 * tc + 2], m1b[:, 0:1],
                            AF.Sigmoid, bias=nm2b[:, 0:1]),
                            reads=[K("m1b"), K("nm2b")], writes=[K("sval", 0)])
                        b.op("scalar", lambda E, tc=tc: E.activation(
                            sval[1][:, 2 * tc + 1:2 * tc + 2], m1b[:, 0:1],
                            AF.Sigmoid, bias=m2b[:, 0:1], scale=-1.0),
                            reads=[K("m1b"), K("m2b")], writes=[K("sval", 1)])
                        for r in range(2):
                            oo = oha[r][:, tc * E8:(tc + 1) * E8]
                            b.op("tensor", lambda E, oo=oo: E.transpose(
                                bcP[1][0:E8, 0:128], oo, idf[:, :]),
                                reads=[K("oha", r), "idf"],
                                writes=[K("bcP", 1)])
                            b.op("scalar", lambda E, r=r, tc=tc: E.copy(
                                SEL[r][:, tc * 128:(tc + 1) * 128],
                                bcP[1][0:E8, 0:128]),
                                reads=[K("bcP", 1)], writes=[K("SEL", r)])

                    for c in range(nchn):
                        t0 = tt0 + c * NCH
                        b.op("tensor", lambda E, t0=t0: E.matmul(
                            bcP[0][:, :], orb[:, :], statA[0:1, t0:t0 + NCH],
                            start=True, stop=True),
                            reads=["statA", "orb"], writes=[K("bcP", 0)])
                        b.op("tensor", lambda E, t0=t0: E.matmul(
                            bcP[1][:, :], orb[:, :], statB[0:1, t0:t0 + NCH],
                            start=True, stop=True),
                            reads=["statB", "orb"], writes=[K("bcP", 1)])
                        for pt in range(DP):
                            b.op("vector", lambda E, pt=pt, t0=t0:
                                 E.tensor_tensor(
                                     evt[:, :], xsl(pt, t0, t0 + NCH),
                                     bcP[0][:, :], ALU.subtract),
                                 reads=xks(pt, t0, t0 + NCH) + [K("bcP", 0)],
                                 writes=[K("evt")])
                            b.op("vector", lambda E, pt=pt, c=c:
                                 E.tensor_tensor(
                                     xln[:, pt * TILE + c * NCH:
                                         pt * TILE + (c + 1) * NCH],
                                     evt[:, :], bcP[1][:, :], ALU.mult),
                                 reads=[K("evt"), K("bcP", 1)],
                                 writes=[K("xln", pt)])

                    for c in range(C8):
                        for q in range(DP // TQ):
                            par = (c * (DP // TQ) + q) % 2
                            for j in range(TQ):
                                pt = q * TQ + j
                                b.op("tensor", lambda E, c=c, pt=pt, j=j,
                                     par=par: E.transpose(
                                         trP[par][:, j * 128:(j + 1) * 128],
                                         xln[:, pt * TILE + c * 128:
                                             pt * TILE + (c + 1) * 128],
                                         idb[:, :]),
                                     reads=[K("xln", pt), "idb"],
                                     writes=[K("trP", par)])
                            b.op("scalar", lambda E, c=c, q=q, par=par:
                                 E.copy(xtm[:, c * D + q * TQ * 128:
                                            c * D + (q + 1) * TQ * 128],
                                        trP[par][:, 0:TQ * 128]),
                                 reads=[K("trP", par)], writes=[K("xtm")])
                    b.op("sync", lambda E: E.dma_start(
                        out=xtmd_t.ap()[0:TILE, :].rearrange("(p c) d -> p c d", p=128),
                        in_=xtm[:, :].rearrange("p (c d) -> p c d", c=C8)),
                        reads=[K("xtm")], writes=[("xtmd",)], dma=True,
                        dsem="xtmd")

                    for r in range(2):
                        for c in range(TILE // NCH):
                            init = (0.0 if c == 0 else
                                    pos[:, c * NCH - 1:c * NCH])
                            b.op("vector", lambda E, c=c, r=r, init=init:
                                 E.tensor_tensor_scan(
                                     pos[0:E8, c * NCH:(c + 1) * NCH],
                                     SEL[r][:, c * NCH:(c + 1) * NCH],
                                     zcol[0:E8, 0:NCH], init,
                                     ALU.add, ALU.add),
                                 reads=[K("SEL", r), "zcol", K("pos")],
                                 writes=[K("pos")])
                        for tc in range(C8):
                            b.op("tensor", lambda E, tc=tc: E.transpose(
                                bcP[1][0:128, 0:E8],
                                pos[:, tc * 128:(tc + 1) * 128],
                                idf[0:E8, 0:E8]),
                                reads=[K("pos"), "idf"], writes=[K("bcP", 1)])
                            b.op("scalar", lambda E, tc=tc: E.copy(
                                posm[:, tc * E8:(tc + 1) * E8],
                                bcP[1][0:128, 0:E8]),
                                reads=[K("bcP", 1)], writes=[K("posm")])
                        b.op("vector", lambda E, r=r: E.tensor_tensor(
                            posm[:, :], posm[:, :], oha[r][:, :], ALU.mult),
                            reads=[K("posm"), K("oha", r)], writes=[K("posm")])
                        b.op("vector", lambda E, r=r: E.tensor_reduce(
                            pa_[r][:, :].rearrange("p (c one) -> p c one", one=1),
                            posm[:, :].rearrange("p (c e) -> p c e", e=E8),
                            AX.X, ALU.add),
                            reads=[K("posm")], writes=[K("pa", r)])
                        b.op("vector", lambda E, r=r: E.tensor_scalar(
                            pa_[r][:, :], pa_[r][:, :], -1.0, None, ALU.add),
                            reads=[K("pa", r)], writes=[K("pa", r)])
                        b.op("vector", lambda E, r=r: E.tensor_tensor(
                            posm[:, :], ecols[:, :], oha[r][:, :], ALU.mult),
                            reads=["ecols", K("oha", r), K("posm")],
                            writes=[K("posm")])
                        b.op("vector", lambda E, r=r: E.tensor_reduce(
                            ea_[r][:, :].rearrange("p (c one) -> p c one", one=1),
                            posm[:, :].rearrange("p (c e) -> p c e", e=E8),
                            AX.X, ALU.add),
                            reads=[K("posm")], writes=[K("ea", r)])
                        ST = 2 * E8 * NG
                        b.op("vector", lambda E, r=r: E.tensor_copy(
                            ii_[0][:, :], pa_[r][:, :]),
                            reads=[K("pa", r)], writes=[K("ii", 0)])
                        b.op("vector", lambda E: E.tensor_scalar(
                            ii_[1][:, :], ii_[0][:, :], 7, None,
                            ALU.arith_shift_right),
                            reads=[K("ii", 0)], writes=[K("ii", 1)])
                        b.op("vector", lambda E: E.tensor_scalar(
                            ii_[0][:, :], ii_[0][:, :], 127, None,
                            ALU.bitwise_and),
                            reads=[K("ii", 0)], writes=[K("ii", 0)])
                        b.op("vector", lambda E, r=r: E.tensor_copy(
                            ii_[2][:, :], ea_[r][:, :]),
                            reads=[K("ea", r)], writes=[K("ii", 2)])
                        b.op("vector", lambda E, r=r: E.tensor_scalar(
                            ii_[2][:, :], ii_[2][:, :], 2 * NG, r * NG,
                            ALU.mult, ALU.add),
                            reads=[K("ii", 2)], writes=[K("ii", 2)])
                        b.op("vector", lambda E: E.tensor_tensor(
                            ii_[2][:, :], ii_[2][:, :], ii_[1][:, :],
                            ALU.add),
                            reads=[K("ii", 2), K("ii", 1)], writes=[K("ii", 2)])
                        b.op("vector", lambda E, ST=ST: E.tensor_scalar(
                            ii_[0][:, :], ii_[0][:, :], ST, None, ALU.mult),
                            reads=[K("ii", 0)], writes=[K("ii", 0)])
                        b.op("vector", lambda E: E.tensor_tensor(
                            ii_[2][:, :], ii_[2][:, :], ii_[0][:, :],
                            ALU.add),
                            reads=[K("ii", 2), K("ii", 0)], writes=[K("ii", 2)])
                        b.op("vector", lambda E, r=r: E.tensor_scalar(
                            pa_[r][:, :], pa_[r][:, :], float(CAPR), None,
                            ALU.is_ge),
                            reads=[K("pa", r)], writes=[K("pa", r)])
                        b.op("vector", lambda E, r=r: E.tensor_copy(
                            ii_[3][:, :], pa_[r][:, :]),
                            reads=[K("pa", r)], writes=[K("ii", 3)])
                        b.op("vector", lambda E: E.tensor_scalar(
                            ii_[3][:, :], ii_[3][:, :], SLOT_TOT, None,
                            ALU.mult),
                            reads=[K("ii", 3)], writes=[K("ii", 3)])
                        b.op("vector", lambda E: E.tensor_tensor(
                            ii_[2][:, :], ii_[2][:, :], ii_[3][:, :],
                            ALU.add),
                            reads=[K("ii", 2), K("ii", 3)], writes=[K("ii", 2)])
                        b.op("vector", lambda E, r=r: E.tensor_scalar_min(
                            si_[r][:, :], ii_[2][:, :], SLOT_TOT),
                            reads=[K("ii", 2)], writes=[K("si", r)])

                    b.op("sync", lambda E: E.dma_start(
                        out=igbd_t.ap()[0:SLOT_TOT, :].rearrange("(p a) v -> p (a v)", p=128),
                        in_=oobs[:, :]),
                        reads=["oobs"], writes=[("igbd",)], dma=True,
                        dsem="igi")
                    for tc in range(C8):
                        for r in range(2):
                            if DBG["no_igs"]:
                                continue
                            b.op("gpsimd", lambda E, tc=tc, r=r:
                                 E.indirect_dma_start(
                                     igbd_t.ap(), IndirectOffsetOnAxis(
                                         si_[r][:, tc:tc + 1], 0),
                                     sval[r][:, 2 * tc:2 * tc + 2], None),
                                 reads=[("igbd",), K("si", r), K("sval", r)],
                                 dma=True, dsem=f"igs{r}")
                    b.op("sync", lambda E: E.dma_start(
                        out=idg[:, :].rearrange("p (a v) -> p a v", v=2),
                        in_=igbd_t.ap()[0:SLOT_TOT, :].rearrange("(p a) v -> p a v", p=128)),
                        reads=[("igbd",)], writes=[("igbd",), K("idg")],
                        dma=True, dsem="igr")
                    b.op("vector", lambda E: E.tensor_copy(
                        ido[:, :].rearrange("p (a one) -> p a one", one=1),
                        idg[:, :].rearrange("p (a v) -> p a v", v=2)
                        [:, :, 0:1]),
                        reads=[K("idg")], writes=[K("ido")])

                    e = E8
                    nfc = DFF // FCC
                    mpc = FCC // 128
                    for c2 in range(nchn):
                        for fcc in range(nfc):
                            fb_ = fcb[fcc % 2]
                            fbk = K("fcb", fcc % 2)
                            stream_fc(e, fcc, fb_, fbk)
                            for m in range(mpc):
                                mpt = fcc * mpc + m
                                pp = mpt % 2
                                hPk = K("hP", pp)
                                for kpt in range(DP):
                                    b.op("tensor", lambda E, fb_=fb_, m=m,
                                         c2=c2, kpt=kpt, pp=pp: E.matmul(
                                             hP[pp][:, :],
                                             fb_[:, kpt * FCC + m * 128:
                                                 kpt * FCC + (m + 1) * 128],
                                             xln[:, kpt * TILE + c2 * NCH:
                                                 kpt * TILE + (c2 + 1) * NCH],
                                             start=(kpt == 0),
                                             stop=(kpt == DP - 1)),
                                         reads=[fbk, K("xln", kpt)],
                                         writes=[hPk])
                                hb = h1[pp]
                                b.op("scalar", lambda E, hb=hb, pp=pp:
                                     E.activation(hb[:, 0:NCH],
                                                  hP[pp][:, :], AF.Relu),
                                     reads=[hPk], writes=[K("h1", pp)])
                                b.op("vector", lambda E, hb=hb, mpt=mpt:
                                     E.tensor_tensor(
                                         hg[:, mpt * NCH:(mpt + 1) * NCH],
                                         hb[:, 0:NCH], hb[:, 0:NCH],
                                         ALU.mult),
                                     reads=[K("h1", pp)],
                                     writes=[K("hg", mpt)])
                        for mo in range(DP):
                            pp = mo % 2
                            ogPk = K("ogP", pp)
                            pb_ = pjb[pp]
                            pbk = K("pjb", pp)
                            stream_pj(e, mo, pb_, pbk, pp)
                            for kk in range(DFP):
                                b.op("tensor", lambda E, pb_=pb_, kk=kk,
                                     pp=pp: E.matmul(
                                         ogP[pp][:, 0:NCH],
                                         pb_[:, kk * 128:(kk + 1) * 128],
                                         hg[:, kk * NCH:(kk + 1) * NCH],
                                         start=(kk == 0),
                                         stop=(kk == DFP - 1)),
                                     reads=[pbk, K("hg", kk)],
                                     writes=[ogPk])
                            t0 = tt0 + c2 * NCH
                            b.op("vector", lambda E, mo=mo, t0=t0, pp=pp:
                                 E.scalar_tensor_tensor(
                                     xsl(mo, t0, t0 + NCH),
                                     ogP[pp][:, 0:NCH], 0.5,
                                     xsl(mo, t0, t0 + NCH),
                                     ALU.mult, ALU.add),
                                 reads=[ogPk, xk(mo, t0 // XBLK)], writes=[xk(mo, t0 // XBLK)])

                    for e in range(E8):
                        epar = e % 2
                        for g in range(GPE):
                            if DBG["no_gx"]:
                                continue
                            b.op("gpsimd", lambda E, e=e, g=g, epar=epar:
                                 E.indirect_dma_start(
                                     xgog[epar][:, g * D:(g + 1) * D], None,
                                     xtmd_t.ap(), IndirectOffsetOnAxis(
                                         ido[:, e * GPE + g:
                                             e * GPE + g + 1], 0)),
                                 reads=[("xtmd",), K("ido"),
                                        K("xgog", epar, g)] +
                                       [("xtmd", "dump", gg, off)
                                        for gg in range(NDUMP)
                                        for off in range(0, D,
                                                         max(SC, NCH))],
                                 writes=[K("xgog", epar, g)], dma=True,
                                 dsem=f"gx{epar}{g}")
                        for g in range(GPE):
                            for q in range(DP // TQ):
                                par = (g * (DP // TQ) + q) % 2
                                for j in range(TQ):
                                    pt = q * TQ + j
                                    b.op("tensor", lambda E, g=g, pt=pt, j=j,
                                         par=par, epar=epar: E.transpose(
                                             trP[par][:, j * 128:
                                                      (j + 1) * 128],
                                             xgog[epar][:, g * D + pt * 128:
                                                  g * D + (pt + 1) * 128],
                                             idb[:, :]),
                                         reads=[K("xgog", epar, g), "idb"],
                                         writes=[K("trP", par)])
                                b.op("scalar", lambda E, g=g, q=q, par=par:
                                     E.copy(
                                         xg[:, :].rearrange(
                                             "p (k s) -> p k s", k=DP)
                                         [:, q * TQ:(q + 1) * TQ,
                                          g * 128:(g + 1) * 128],
                                         trP[par][:, 0:TQ * 128].rearrange(
                                             "p (k s) -> p k s", k=TQ)),
                                     reads=[K("trP", par)],
                                     writes=[K("xg")])
                        for fcc in range(nfc):
                            fb_ = fcb[fcc % 2]
                            fbk = K("fcb", fcc % 2)
                            stream_fc(e, fcc, fb_, fbk)
                            for m in range(mpc):
                                mpt = fcc * mpc + m
                                pp = mpt % 2
                                hPk = K("hP", pp)
                                for kpt in range(DP):
                                    b.op("tensor", lambda E, fb_=fb_, m=m,
                                         kpt=kpt, pp=pp: E.matmul(
                                             hP[pp][:, 0:CAP2],
                                             fb_[:, kpt * FCC + m * 128:
                                                 kpt * FCC + (m + 1) * 128],
                                             xg[:, kpt * CAP2:
                                                (kpt + 1) * CAP2],
                                             start=(kpt == 0),
                                             stop=(kpt == DP - 1)),
                                         reads=[fbk, K("xg")],
                                         writes=[hPk])
                                hb = h1[pp]
                                b.op("scalar", lambda E, hb=hb, pp=pp:
                                     E.activation(hb[:, 0:CAP2],
                                                  hP[pp][:, 0:CAP2], AF.Relu),
                                     reads=[hPk], writes=[K("h1", pp)])
                                b.op("vector", lambda E, hb=hb, mpt=mpt:
                                     E.tensor_tensor(
                                         hg[:, mpt * NCH:mpt * NCH + CAP2],
                                         hb[:, 0:CAP2], hb[:, 0:CAP2],
                                         ALU.mult),
                                     reads=[K("h1", pp)],
                                     writes=[K("hg", mpt)])
                        for mo in range(DP):
                            pp = mo % 2
                            ogPk = K("ogP", pp)
                            pb_ = pjb[pp]
                            pbk = K("pjb", pp)
                            stream_pj(e, mo, pb_, pbk, pp)
                            for kk in range(DFP):
                                b.op("tensor", lambda E, pb_=pb_, kk=kk,
                                     pp=pp: E.matmul(
                                         ogP[pp][:, 0:CAP2],
                                         pb_[:, kk * 128:(kk + 1) * 128],
                                         hg[:, kk * NCH:kk * NCH + CAP2],
                                         start=(kk == 0),
                                         stop=(kk == DFP - 1)),
                                     reads=[pbk, K("hg", kk)],
                                     writes=[ogPk])
                            b.op("scalar", lambda E, mo=mo, pp=pp: E.copy(
                                ogf[:, mo * CAP2:(mo + 1) * CAP2],
                                ogP[pp][:, 0:CAP2]),
                                reads=[ogPk], writes=[K("xg")])
                        for g in range(GPE):
                            for q in range(DP // TQ):
                                par = (g * (DP // TQ) + q) % 2
                                for j in range(TQ):
                                    pt = q * TQ + j
                                    b.op("tensor", lambda E, g=g, pt=pt, j=j,
                                         par=par: E.transpose(
                                             trP[par][:, j * 128:
                                                      (j + 1) * 128],
                                             ogf[:, pt * CAP2 + g * 128:
                                                 pt * CAP2 + (g + 1) * 128],
                                             idb[:, :]),
                                         reads=[K("xg"), "idb"],
                                         writes=[K("trP", par)])
                                gc = idg[:, (e * GPE + g) * 2 + 1:
                                         (e * GPE + g) * 2 + 2]
                                b.op("vector", lambda E, g=g, q=q, par=par,
                                     gc=gc, epar=epar: E.tensor_scalar(
                                         xgog[epar][:, g * D + q * TQ * 128:
                                              g * D + (q + 1) * TQ * 128],
                                         trP[par][:, 0:TQ * 128],
                                         gc[:, 0:1], None, ALU.mult),
                                     reads=[K("trP", par), K("idg")],
                                     writes=[K("xgog", epar, g)])
                        for g in range(GPE):
                            if DBG["no_dsc"]:
                                continue
                            dst = dltA_t if g < NG else dltB_t
                            dk = ("dltA",) if g < NG else ("dltB",)
                            b.op("gpsimd", lambda E, e=e, g=g, dst=dst,
                                 epar=epar: E.indirect_dma_start(
                                     dst.ap(), IndirectOffsetOnAxis(
                                         ido[:, e * GPE + g:
                                             e * GPE + g + 1], 0),
                                     xgog[epar][:, g * D:(g + 1) * D], None),
                                 reads=[dk, K("ido"), K("xgog", epar, g)],
                                 dma=True, dsem=f"dsc{epar}{g}")

                    for dt_, dkk, dsm in ((dltA_t, ("dltA",), "dra"),
                                          (dltB_t, ("dltB",), "drb")):
                        b.op("sync", lambda E, dt_=dt_: E.dma_start(
                            out=xtm[:, :].rearrange("p (c d) -> p c d", c=C8),
                            in_=dt_.ap()[0:TILE, :].rearrange(
                                "(p c) d -> p c d", p=128)),
                            reads=[dkk], writes=[dkk, K("xtm")],
                            dma=True, dsem=dsm)
                        for pt in range(DP):
                            for cq in range(C8 // CQ):
                                par = (pt * (C8 // CQ) + cq) % 2
                                for j in range(CQ):
                                    c = cq * CQ + j
                                    b.op("tensor", lambda E, c=c, pt=pt, j=j,
                                         par=par: E.transpose(
                                             trP[par][:, j * 128:
                                                      (j + 1) * 128],
                                             xtm[:, c * D + pt * 128:
                                                 c * D + (pt + 1) * 128],
                                             idb[:, :]),
                                         reads=[K("xtm"), "idb"],
                                         writes=[K("trP", par)])
                                t0 = tt0 + cq * CQ * 128
                                b.op("vector", lambda E, pt=pt, t0=t0,
                                     par=par: E.scalar_tensor_tensor(
                                         xsl(pt, t0, t0 + CQ * 128),
                                         trP[par][:, 0:CQ * 128], 0.5,
                                         xsl(pt, t0, t0 + CQ * 128),
                                         ALU.mult, ALU.add),
                                     reads=[K("trP", par),
                                            xk(pt, t0 // XBLK)],
                                     writes=[xk(pt, t0 // XBLK)])
                b.barrier()

        def attn_phase(tag):
            at_ctx = ExitStack()
            with at_ctx:
                fs = lambda nm, sh, dt: at_ctx.enter_context(
                    nc.sbuf_tensor(tag + nm, sh, dt))
                fp = lambda nm, sh, dt: at_ctx.enter_context(
                    nc.psum_tensor(tag + nm, sh, dt))
                K = lambda *a: (tag,) + a

                ws = {}
                for wi, (nm, dram) in enumerate([("wq", wqd), ("wk", wkd),
                                                 ("wv", wvd), ("wo", wod)]):
                    ws[nm] = fs(nm, [128, DP * D], BF16)
                    ck = ("watt", nm)
                    if ck not in weight_cached:
                        weight_cached.add(ck)
                        PW = NCH
                        mp = max(1, D // PW)
                        for pc in range(DP * D // PW):
                            kk_, mo_ = pc // mp, (pc % mp) * PW
                            b.op("sync", lambda E, dram=dram, kk_=kk_,
                                 mo_=mo_, PW=PW: E.dma_start(
                                     out=aevt[:, 0:PW],
                                     in_=dram.rearrange(
                                         "(k p) m -> p k m", p=128)
                                     [:, kk_, mo_:mo_ + PW]),
                                 reads=[K("aevt")], writes=[K("aevt")],
                                 dma=True, dsem="wst")
                            b.op("gpsimd", lambda E, s=ws[nm], pc=pc, PW=PW:
                                 E.tensor_copy(
                                     s[:, pc * PW:(pc + 1) * PW],
                                     aevt[:, 0:PW]),
                                 reads=[K("aevt")], writes=[K(nm)])
                        b.op("sync", lambda E, s=ws[nm], wi=wi: E.dma_start(
                            out=awtb[wi], in_=s[:, :]),
                            reads=[K(nm)], writes=[("awtb", wi)],
                            dma=True, dsem="wsw")
                    else:
                        b.op("sync", lambda E, s=ws[nm], wi=wi: E.dma_start(
                            out=s[:, :], in_=awtb[wi]),
                            reads=[("awtb", wi)], writes=[K(nm)],
                            dma=True, dsem="w_" + nm)

                xlnc = fs("xlnc", [128, DP * CH], BF16)
                bcMc = fs("bcMc", [128, CH], F32)
                bcRc = fs("bcRc", [128, CH], F32)
                qb = fs("qb", [128, DP * CH], BF16)
                fb = fs("fb", [128, DP * CH], BF16)
                vb = fs("vb", [128, DP * CH], BF16)
                ob = fs("ob", [128, DP * CH], BF16)
                lfb = fs("lfb", [128, DP * CH], F32)
                cb_ = fs("cb_", [128, DP * CH], F32)
                QB = fs("QB", [128, DP * CH], BF16)
                qRb = fs("qRb", [128, DP * CH], BF16)
                son = fs("son", [128, DP * CH], BF16)
                rbc = fs("rbc", [128, CH], F32)
                rrow = fs("rrow", [1, CH], F32)
                Sb_ = fs("Sb_", [128, DP * 128], BF16)
                qt = [fs(f"qt{i}", [128, SC], BF16) for i in range(2)]
                kt = [fs(f"kt{i}", [128, SC], BF16) for i in range(2)]
                kh = [fs(f"kh{i}", [128, SC], BF16) for i in range(2)]
                atm = [fs(f"atm{i}", [SC, SC], BF16) for i in range(2)]
                vk = [fs(f"vk{i}", [SC, 256], BF16) for i in range(2)]
                rcol = [fs(f"rcol{i}", [128, 1], F32) for i in range(2)]
                aevt = fs("aevt", [128, NCH], F32)

                qkvP = [fp(f"qkvP{i}", [128, max(CH, NCH)], F32)
                        for i in range(2)]
                glaA = [fp(f"glaA{i}", [128, 2 * SC], F32) for i in range(2)]
                glaT = [fp(f"glaT{i}", [SC, 2 * 128], BF16) for i in range(2)]
                glaS = [fp(f"glaS{i}", [128, 128], F32) for i in range(2)]

                nks = (NCH + CH - 1) // CH
                ln_stats(tag, [qb[:, 0:NCH], vb[:, 0:NCH]],
                         [fb[:, 0:NCH], ob[:, 0:NCH]],
                         [[K("qb", i) for i in range(nks)],
                          [K("vb", i) for i in range(nks)]],
                         [[K("fb", i) for i in range(nks)],
                          [K("ob", i) for i in range(nks)]],
                         qkvP[0], K("qkvP", 0), qkvP[1], K("qkvP", 1),
                         aevt, K("aevt"), eps5)

                b.op("gpsimd", lambda E: E.memset(Sb_[:, :], 0.0),
                     writes=[K("Sb_", h) for h in range(DP)])

                for ch in range(N // CH):
                    t0 = ch * CH
                    b.op("tensor", lambda E, t0=t0: E.matmul(
                        qkvP[0][:, 0:CH], orb[:, :], statA[0:1, t0:t0 + CH],
                        start=True, stop=True),
                        reads=["statA", "orb"], writes=[K("qkvP", 0)])
                    b.op("tensor", lambda E, t0=t0: E.matmul(
                        qkvP[1][:, 0:CH], orb[:, :], statB[0:1, t0:t0 + CH],
                        start=True, stop=True),
                        reads=["statB", "orb"], writes=[K("qkvP", 1)])
                    b.op("vector", lambda E: E.tensor_copy(
                        bcMc[:, :], qkvP[0][:, 0:CH]),
                        reads=[K("qkvP", 0)], writes=[K("bcMc")])
                    b.op("vector", lambda E: E.tensor_copy(
                        bcRc[:, :], qkvP[1][:, 0:CH]),
                        reads=[K("qkvP", 1)], writes=[K("bcRc")])
                    for pt in range(DP):
                        b.op("vector", lambda E, pt=pt, t0=t0:
                             E.tensor_tensor(aevt[:, 0:CH],
                                             xsl(pt, t0, t0 + CH),
                                             bcMc[:, :], ALU.subtract),
                             reads=xks(pt, t0, t0 + CH) + [K("bcMc")], writes=[K("aevt")])
                        b.op("vector", lambda E, pt=pt: E.tensor_tensor(
                            xlnc[:, pt * CH:(pt + 1) * CH], aevt[:, 0:CH],
                            bcRc[:, :], ALU.mult),
                            reads=[K("aevt"), K("bcRc")],
                            writes=[K("xlnc", pt)])
                    for nm, dst, dstk, act in [("wq", qb, "qb", AF.Silu),
                                               ("wk", fb, "fb", AF.Sigmoid),
                                               ("wv", vb, "vb", AF.Copy)]:
                        w = ws[nm]
                        for mo in range(DP):
                            pp = (mo // 2) % 2
                            co = (mo % 2) * CH
                            for kpt in range(DP):
                                b.op("tensor", lambda E, w=w, mo=mo, kpt=kpt,
                                     pp=pp, co=co: E.matmul(
                                         qkvP[pp][:, co:co + CH],
                                         w[:, kpt * D + mo * 128:
                                           kpt * D + (mo + 1) * 128],
                                         xlnc[:, kpt * CH:(kpt + 1) * CH],
                                         start=(kpt == 0),
                                         stop=(kpt == DP - 1)),
                                     reads=[K(nm), K("xlnc", kpt)],
                                     writes=[K("qkvP", pp)])
                            if mo % 2 == 1:
                                b.op("scalar", lambda E, dst=dst, mo=mo,
                                     pp=pp, act=act: E.activation(
                                         dst[:, (mo - 1) * CH:(mo + 1) * CH],
                                         qkvP[pp][:, 0:2 * CH], act),
                                     reads=[K("qkvP", pp)],
                                     writes=[K(dstk, mo - 1), K(dstk, mo)])
                    allf = [K("fb", mo) for mo in range(DP)]
                    alll = [K("lfb", mo) for mo in range(DP)]
                    b.op("scalar", lambda E: E.activation(
                        lfb[:, :], fb[:, :], AF.Sigmoid),
                        reads=allf, writes=alll)
                    b.op("scalar", lambda E: E.activation(
                        lfb[:, :], lfb[:, :], AF.Ln),
                        reads=alll, writes=alll)
                    mid = SC // 2
                    EMH = float(np.exp(-44.36))
                    allc = [K("cb", h) for h in range(DP)]
                    for h in range(DP):
                        for sc in range(NSC):
                            o0 = h * CH + sc * SC
                            b.op("vector", lambda E, o0=o0:
                                 E.tensor_tensor_scan(
                                     cb_[:, o0:o0 + SC], lfb[:, o0:o0 + SC],
                                     zcol[:, 0:SC], 0.0, ALU.add, ALU.add),
                                 reads=[K("lfb", h), "zcol"],
                                 writes=[K("cb", h)])
                    b.op("scalar", lambda E: E.activation(
                        QB[:, :], cb_[:, :], AF.Exp, scale=-1.0,
                        bias=hcolN[:, 0:1]),
                        reads=allc + ["hcolN"], writes=[K("QB")])
                    alll = [K("lfb", mo) for mo in range(DP)]
                    b.op("scalar", lambda E: E.activation(
                        lfb[:, :], cb_[:, :], AF.Exp, bias=hcolP[:, 0:1]),
                        reads=allc + ["hcolP"], writes=alll)
                    b.op("vector", lambda E: E.tensor_tensor(
                        qRb[:, :], qb[:, :], lfb[:, :], ALU.mult),
                        reads=[K("qb", h) for h in range(DP)] + alll,
                        writes=[K("qRb")])
                    b.op("vector", lambda E: E.tensor_tensor(
                        cb_[:, :], fb[:, :], QB[:, :], ALU.mult),
                        reads=[K("fb", h) for h in range(DP)] + [K("QB")],
                        writes=allc)
                    b.op("vector", lambda E: E.tensor_tensor(
                        QB[:, :], QB[:, :], cb_[:, :], ALU.subtract),
                        reads=allc + [K("QB")], writes=[K("QB")])
                    for sc in range(NSC):
                        for h in range(DP):
                            i = h % 2
                            o0 = h * CH + sc * SC
                            rmid = lfb[:, o0 + mid:o0 + mid + 1]
                            rend = lfb[:, o0 + SC - 1:o0 + SC]
                            b.op("vector", lambda E, rmid=rmid, i=i:
                                 E.reciprocal(rcol[i][:, :], rmid),
                                 reads=[K("lfb", h)], writes=[K("rcol", i)])
                            b.op("vector", lambda E, o0=o0, i=i:
                                 E.tensor_scalar(
                                     qt[i][:, :], qRb[:, o0:o0 + SC],
                                     rcol[i][:, 0:1], None, ALU.mult),
                                 reads=[K("qRb"), K("rcol", i)],
                                 writes=[K("qt", i)])
                            b.op("vector", lambda E, o0=o0, i=i, rmid=rmid:
                                 E.tensor_scalar(
                                     kt[i][:, :], QB[:, o0:o0 + SC],
                                     rmid, None, ALU.mult),
                                 reads=[K("QB"), K("lfb", h)],
                                 writes=[K("kt", i)])
                            b.op("vector", lambda E, o0=o0, i=i, rend=rend:
                                 E.tensor_scalar(
                                     kh[i][:, :], QB[:, o0:o0 + SC],
                                     rend, None, ALU.mult),
                                 reads=[K("QB"), K("lfb", h)],
                                 writes=[K("kh", i)])
                            b.op("tensor", lambda E, i=i: E.matmul(
                                glaA[i][:, 0:SC], kt[i][:, :], qt[i][:, :],
                                start=True, stop=True),
                                reads=[K("kt", i), K("qt", i)],
                                writes=[K("glaA", i)])
                            b.op("vector", lambda E, i=i: E.tensor_tensor(
                                atm[i][:, :], glaA[i][0:SC, 0:SC], mkb[:, :],
                                ALU.mult),
                                reads=[K("glaA", i), "mkb"],
                                writes=[K("atm", i)])
                            b.op("tensor", lambda E, o0=o0, i=i: E.transpose(
                                glaT[i][:, 0:128], vb[:, o0:o0 + SC],
                                idb[:, :]),
                                reads=[K("vb", h), "idb"],
                                writes=[K("glaT", i)])
                            b.op("tensor", lambda E, i=i: E.transpose(
                                glaT[i][:, 128:256], kh[i][:, :], idb[:, :]),
                                reads=[K("kh", i), "idb"],
                                writes=[K("glaT", i)])
                            b.op("scalar", lambda E, i=i: E.copy(
                                vk[i][:, :], glaT[i][0:SC, 0:256]),
                                reads=[K("glaT", i)], writes=[K("vk", i)])
                            b.op("tensor", lambda E, h=h, i=i, o0=o0:
                                 E.matmul(
                                     glaA[i][:, SC:2 * SC],
                                     Sb_[:, h * 128:(h + 1) * 128],
                                     qRb[:, o0:o0 + SC],
                                     start=True, stop=False),
                                 reads=[K("Sb_", h), K("qRb")],
                                 writes=[K("glaA", i)])
                            b.op("tensor", lambda E, i=i: E.matmul(
                                glaA[i][:, SC:2 * SC], vk[i][:, 0:128],
                                atm[i][:, :], start=False, stop=True),
                                reads=[K("vk", i), K("atm", i)],
                                writes=[K("glaA", i)])
                            b.op("scalar", lambda E, o0=o0, i=i: E.copy(
                                ob[:, o0:o0 + SC], glaA[i][:, SC:2 * SC]),
                                reads=[K("glaA", i)], writes=[K("ob", h)])
                            b.op("tensor", lambda E, i=i: E.matmul(
                                glaS[i][:, :], vk[i][:, 128:256],
                                vk[i][:, 0:128], start=True, stop=True),
                                reads=[K("vk", i)], writes=[K("glaS", i)])
                            b.op("vector", lambda E, h=h, i=i, rend=rend:
                                 E.tensor_scalar(
                                     Sb_[:, h * 128:(h + 1) * 128],
                                     Sb_[:, h * 128:(h + 1) * 128],
                                     rend, EMH, ALU.mult, ALU.mult),
                                 reads=[K("Sb_", h), K("lfb", h)],
                                 writes=[K("Sb_", h)])
                            b.op("vector", lambda E, h=h, i=i:
                                 E.scalar_tensor_tensor(
                                     Sb_[:, h * 128:(h + 1) * 128],
                                     glaS[i][:, :], EMH,
                                     Sb_[:, h * 128:(h + 1) * 128],
                                     ALU.mult, ALU.add),
                                 reads=[K("glaS", i), K("Sb_", h)],
                                 writes=[K("Sb_", h)])
                    b.op("vector", lambda E: E.tensor_tensor(
                        son[:, :], ob[:, :], ob[:, :], ALU.mult),
                        reads=[K("ob", hh) for hh in range(DP)],
                        writes=[K("son")])
                    for kpt in range(DP):
                        b.op("tensor", lambda E, kpt=kpt: E.matmul(
                            qkvP[0][0:1, 0:CH], ocb[:, :],
                            son[:, kpt * CH:(kpt + 1) * CH],
                            start=(kpt == 0), stop=(kpt == DP - 1)),
                            reads=[K("son"), "ocb"], writes=[K("qkvP", 0)])
                    b.op("scalar", lambda E: E.activation(
                        rrow[0:1, :], qkvP[0][0:1, 0:CH], AF.Ln,
                        scale=1.0 / D, bias=eps6[0:1, 0:1]),
                        reads=[K("qkvP", 0), "eps6"], writes=[K("rrow")])
                    b.op("scalar", lambda E: E.activation(
                        rrow[0:1, :], rrow[0:1, :], AF.Exp, scale=-0.5),
                        reads=[K("rrow")], writes=[K("rrow")])
                    b.op("tensor", lambda E: E.matmul(
                        qkvP[1][:, 0:CH], orf[:, :], rrow[0:1, :],
                        start=True, stop=True),
                        reads=[K("rrow"), "orf"], writes=[K("qkvP", 1)])
                    b.op("vector", lambda E: E.tensor_copy(
                        rbc[:, :], qkvP[1][:, 0:CH]),
                        reads=[K("qkvP", 1)], writes=[K("rbc")])
                    for mo in range(DP):
                        b.op("vector", lambda E, mo=mo: E.scalar_tensor_tensor(
                            son[:, mo * CH:(mo + 1) * CH],
                            ob[:, mo * CH:(mo + 1) * CH],
                            nws[:, mo:mo + 1], rbc[:, :],
                            ALU.mult, ALU.mult),
                            reads=[K("ob", mo), "nws", K("rbc")],
                            writes=[K("son")])
                    for mo in range(DP):
                        pp = (mo // 2) % 2
                        co = (mo % 2) * CH
                        for kpt in range(DP):
                            b.op("tensor", lambda E, mo=mo, kpt=kpt, pp=pp,
                                 co=co: E.matmul(
                                     qkvP[pp][:, co:co + CH],
                                     ws["wo"][:, kpt * D + mo * 128:
                                              kpt * D + (mo + 1) * 128],
                                     son[:, kpt * CH:(kpt + 1) * CH],
                                     start=(kpt == 0), stop=(kpt == DP - 1)),
                                 reads=[K("wo"), K("son")],
                                 writes=[K("qkvP", pp)])
                        if mo % 2 == 1:
                            xpair = xs[:, :].rearrange(
                                "p (k n) -> p k n", k=DP)[
                                :, mo - 1:mo + 1, t0:t0 + CH]
                            b.op("vector", lambda E, xpair=xpair, pp=pp:
                                 E.tensor_tensor(
                                     xpair, qkvP[pp][:, 0:2 * CH],
                                     xpair, ALU.add),
                                 reads=[K("qkvP", pp), xk(mo - 1, t0 // XBLK), xk(mo, t0 // XBLK)],
                                 writes=[xk(mo - 1, t0 // XBLK), xk(mo, t0 // XBLK)])
                b.barrier()

        for bb in range(B):
            b.op("sync", lambda E, bb=bb: E.dma_start(
                out=xs[:, :].rearrange("p (k n) -> p k n", k=DP),
                in_=xd.rearrange("b (k p) n -> b p k n", p=128)[bb]),
                writes=[k for pt in range(DP) for k in xk(pt)], dma=True, dsem="xload")
            ffn_phase(f"F{bb}a_")
            if not DBG["ffn_only"]:
                attn_phase(f"A{bb}_")
                ffn_phase(f"F{bb}b_")
            b.op("sync", lambda E, bb=bb: E.dma_start(
                out=outd.rearrange("b (k p) n -> b p k n", p=128)[bb],
                in_=xs[:, :].rearrange("p (k n) -> p k n", k=DP)),
                reads=[k for pt in range(DP) for k in xk(pt)], dma=True, dsem="xstore")
        b.barrier()
        b.play(top)
    return nc


def _oobs_const(cfg):
    E8 = cfg["E"]
    NG = cfg["CAPR"] // 128
    ST = 2 * E8 * NG
    a = np.arange(ST)
    epar = (a // (2 * NG)) % 2
    gg = epar * (2 * NG) + a % (2 * NG)
    ids = cfg["TILE"] + gg.astype(np.float32)
    pat = np.zeros((ST, 2), dtype=np.float32)
    pat[:, 0] = ids
    return np.tile(pat.reshape(1, ST * 2), (128, 1))


def prep_inputs(L0_attn_x, L0_attn_Wq, L0_attn_Wk, L0_attn_Wv, L0_attn_Wo,
                L0_attn_norm_w, L0_ffn_router_w, L0_ffn_w_fc, L0_ffn_w_proj,
                L0_ffn_shared_fc, L0_ffn_shared_proj, cfg, n_cores=8):
    B, N, D, DFF, E8 = cfg["B"], cfg["N"], cfg["D"], cfg["DFF"], cfg["E"]
    SC = cfg["SC"]
    cc = np.ascontiguousarray
    ascc = cc
    x = np.asarray(L0_attn_x, dtype=np.float32)
    fctT = np.concatenate(
        [np.transpose(np.asarray(L0_ffn_w_fc, np.float32), (0, 2, 1)),
         np.transpose(np.asarray(L0_ffn_shared_fc, np.float32))[None]],
        axis=0)
    pjtT = np.concatenate(
        [np.transpose(np.asarray(L0_ffn_w_proj, np.float32), (0, 2, 1)),
         np.transpose(np.asarray(L0_ffn_shared_proj, np.float32))[None]],
        axis=0)
    common = {
        "wq": cc(np.asarray(L0_attn_Wq, np.float32).T),
        "wk": cc(np.asarray(L0_attn_Wk, np.float32).T),
        "wv": cc(np.asarray(L0_attn_Wv, np.float32).T),
        "wo": cc(np.asarray(L0_attn_Wo, np.float32).T),
        "nw": cc(np.asarray(L0_attn_norm_w, np.float32).reshape(D // 128, 128).T),
        "rt": cc(np.asarray(L0_ffn_router_w, np.float32).T),
        "fct": cc(fctT),
        "pjt": cc(pjtT),
        "identb": np.eye(128, dtype=BFNP),
        "identf": np.eye(128, dtype=np.float32),
        "onescb": np.ones((128, 1), dtype=BFNP),
        "onescf": np.ones((128, 1), dtype=np.float32),
        "onesrf": np.ones((1, 128), dtype=np.float32),
        "halfrf": np.full((1, 128), 0.5, dtype=np.float32),
        "halfrb": np.full((1, 128), 0.5, dtype=BFNP),
        "maskb": np.triu(np.ones((SC, SC), dtype=np.float32)).astype(BFNP),
        "onesrb": np.ones((1, 128), dtype=BFNP),
        "ecols": np.tile(np.arange(E8, dtype=np.float32),
                         (128, cfg["TILE"] // 128)),
        "tvl": (np.arange(128, dtype=np.float32)[:, None] *
                (cfg["TILE"] // 128) +
                np.arange(cfg["TILE"] // 128, dtype=np.float32)[None, :]),
        "oobs": _oobs_const(cfg),
    }
    in_maps = []
    for c in range(n_cores):
        m = dict(common)
        m["x"] = cc(np.transpose(x[c * B:(c + 1) * B], (0, 2, 1)))
        in_maps.append(m)
    return in_maps


def kernel(**inputs):
    cfg = REAL_CFG
    n_cores = 8
    in_maps = prep_inputs(n_cores=n_cores, cfg=cfg, **inputs)
    nc = build_program(cfg)
    res = run_bass_kernel_spmd(nc, in_maps, core_ids=list(range(n_cores)))
    outs = [np.transpose(r["out"], (0, 2, 1)) for r in res.results]
    return np.concatenate(outs, axis=0)


# version 19
# speedup vs baseline: 1.1127x; 1.1127x over previous
import numpy as np
import ml_dtypes
from contextlib import ExitStack

import concourse.bass as bass
import concourse.mybir as mybir
from concourse.bass_utils import run_bass_kernel_spmd
from concourse.bass_core_rust import IndirectOffsetOnAxis

F32 = mybir.dt.float32
BF16 = mybir.dt.bfloat16
I32 = mybir.dt.int32
AF = mybir.ActivationFunctionType
ALU = mybir.AluOpType
AX = mybir.AxisListType
BFNP = ml_dtypes.bfloat16

ENGINES = ("tensor", "vector", "scalar", "gpsimd", "sync")

DBG = {"no_igs": False, "no_gx": False, "no_dsc": False, "no_acc": False,
       "ffn_only": False}

REAL_CFG = dict(B=2, N=2048, D=1024, DFF=4096, E=8,
                TILE=1024, NCH=512, CH=256, SC=128, FCC=256, CAPR=256)


class Builder:

    def __init__(self, nc, ctx):
        self.nc = nc
        self.prog = {e: [] for e in ENGINES}
        self.semh = {}
        self.semc = {}
        self.floors = {e: {} for e in ENGINES}
        self.buf = {}
        self._ctx = ctx
        for name in ["s_tensor", "s_vector", "s_scalar", "s_gpsimd", "s_sync"]:
            self.semh[name] = ctx.enter_context(nc.semaphore(name))
            self.semc[name] = 0

    def op(self, eng, emit, reads=(), writes=(), dma=False, dsem=None):
        if dma:
            assert dsem is not None
            sem = "dma_" + dsem
            if sem not in self.semh:
                self.semh[sem] = self._ctx.enter_context(self.nc.semaphore(sem))
                self.semc[sem] = 0
            inc = 16
        else:
            sem = "s_" + eng
            inc = 1
        needs = {}

        def want(mark):
            s, v = mark
            if eng == "tensor" and s == "s_tensor":
                return
            if v > needs.get(s, 0):
                needs[s] = v

        for k in reads:
            st = self.buf.get(k)
            if st and st["w"]:
                want(st["w"])
        for k in writes:
            st = self.buf.get(k)
            if st:
                if st["w"]:
                    want(st["w"])
                for r in st["r"]:
                    want(r)
        lst = self.prog[eng]
        fl = self.floors[eng]
        for s in sorted(needs):
            v = needs[s]
            if fl.get(s, 0) >= v:
                continue
            fl[s] = v
            h = self.semh[s]
            lst.append(lambda E, h=h, v=v: E.wait_ge(h, v))
        self.semc[sem] += inc
        h = self.semh[sem]
        lst.append(lambda E, h=h, emit=emit, inc=inc: emit(E).then_inc(h, inc))
        mark = (sem, self.semc[sem])
        for k in writes:
            self.buf[k] = {"w": mark, "r": []}
        for k in reads:
            if k in writes:
                continue
            st = self.buf.setdefault(k, {"w": None, "r": []})
            st["r"].append(mark)
        return mark

    def barrier(self):
        targets = [(s, c) for s, c in self.semc.items() if c > 0]
        for e in ENGINES:
            fl = self.floors[e]
            for s, v in targets:
                if fl.get(s, 0) >= v:
                    continue
                fl[s] = v
                h = self.semh[s]
                self.prog[e].append(lambda E, h=h, v=v: E.wait_ge(h, v))
        self.buf.clear()

    def play(self, ctx):
        block = ctx.enter_context(self.nc.Block())
        prog = self.prog

        @block.tensor
        def _(E):
            for f in prog["tensor"]:
                f(E)

        @block.vector
        def _(E):
            for f in prog["vector"]:
                f(E)

        @block.scalar
        def _(E):
            for f in prog["scalar"]:
                f(E)

        @block.gpsimd
        def _(E):
            for f in prog["gpsimd"]:
                f(E)

        @block.sync
        def _(E):
            for f in prog["sync"]:
                f(E)


def build_program(cfg):
    B, N, D, DFF, E8 = cfg["B"], cfg["N"], cfg["D"], cfg["DFF"], cfg["E"]
    TILE, NCH, CH, SC, FCC = (cfg["TILE"], cfg["NCH"], cfg["CH"], cfg["SC"],
                              cfg["FCC"])
    CAPR = cfg["CAPR"]
    CAP2 = 2 * CAPR
    NG = CAPR // 128
    GPE = 2 * NG
    C8 = TILE // 128
    SLOT_TOT = (cfg["E"]) * CAP2
    assert CAPR % 128 == 0 and CAP2 <= NCH
    TQ = 4 if (D // 128) % 4 == 0 else D // 128
    CQ = 4 if C8 % 4 == 0 else min(4, C8)
    DP = D // 128
    DFP = DFF // 128
    NT = N // TILE
    NSC = CH // SC
    assert TILE % NCH == 0 and N % CH == 0 and CH % SC == 0
    assert DFF % FCC == 0 and FCC % 128 == 0 and N % NCH == 0

    nc = bass.Bass("TRN2", target_bir_lowering=False)

    xd = nc.dram_tensor("x", [B, D, N], F32, kind="ExternalInput").ap()
    wqd = nc.dram_tensor("wq", [D, D], F32, kind="ExternalInput").ap()
    wkd = nc.dram_tensor("wk", [D, D], F32, kind="ExternalInput").ap()
    wvd = nc.dram_tensor("wv", [D, D], F32, kind="ExternalInput").ap()
    wod = nc.dram_tensor("wo", [D, D], F32, kind="ExternalInput").ap()
    nwd = nc.dram_tensor("nw", [128, DP], F32, kind="ExternalInput").ap()
    rtd = nc.dram_tensor("rt", [D, E8], F32, kind="ExternalInput").ap()
    fct = nc.dram_tensor("fct", [E8 + 1, D, DFF], F32, kind="ExternalInput").ap()
    pjt = nc.dram_tensor("pjt", [E8 + 1, DFF, D], F32, kind="ExternalInput").ap()
    identb = nc.dram_tensor("identb", [128, 128], BF16, kind="ExternalInput").ap()
    identf = nc.dram_tensor("identf", [128, 128], F32, kind="ExternalInput").ap()
    onescb = nc.dram_tensor("onescb", [128, 1], BF16, kind="ExternalInput").ap()
    onescf = nc.dram_tensor("onescf", [128, 1], F32, kind="ExternalInput").ap()
    onesrf = nc.dram_tensor("onesrf", [1, 128], F32, kind="ExternalInput").ap()
    halfrf = nc.dram_tensor("halfrf", [1, 128], F32, kind="ExternalInput").ap()
    halfrb = nc.dram_tensor("halfrb", [1, 128], BF16, kind="ExternalInput").ap()
    maskbd = nc.dram_tensor("maskb", [SC, SC], BF16, kind="ExternalInput").ap()
    onesrb = nc.dram_tensor("onesrb", [1, 128], BF16, kind="ExternalInput").ap()
    ecolsd = nc.dram_tensor("ecols", [128, C8 * E8], F32, kind="ExternalInput").ap()
    tvld = nc.dram_tensor("tvl", [128, C8], F32, kind="ExternalInput").ap()
    oobsd = nc.dram_tensor("oobs", [128, SLOT_TOT * 2 // 128], F32,
                           kind="ExternalInput").ap()
    outd = nc.dram_tensor("out", [B, D, N], F32, kind="ExternalOutput").ap()
    NDUMP = 4 * (CAPR // 128)
    xtmd_t = nc.dram_tensor("xtmd", [TILE + NDUMP, D], BF16, kind="Internal")
    igbd_t = nc.dram_tensor("igbd", [SLOT_TOT + 1, 2], F32, kind="Internal")
    dltA_t = nc.dram_tensor("dltA", [TILE + NDUMP, D], BF16, kind="Internal")
    dltB_t = nc.dram_tensor("dltB", [TILE + NDUMP, D], BF16, kind="Internal")
    fctb = nc.dram_tensor("fctb", [E8 + 1, DFF // FCC, 128, (D // 128) * FCC],
                          BF16, kind="Internal").ap()
    pjtb = nc.dram_tensor("pjtb", [E8 + 1, D // 128, 128, (DFF // 128) * 128],
                          BF16, kind="Internal").ap()
    awtb = nc.dram_tensor("awtb", [4, 128, (D // 128) * D], BF16,
                          kind="Internal").ap()

    top = ExitStack()
    with top:
        b = Builder(nc, top)
        sb = lambda nm, sh, dt: top.enter_context(nc.sbuf_tensor(nm, sh, dt))

        xs = sb("xs", [128, DP * N], F32)
        statA = sb("statA", [1, N], BF16)
        statB = sb("statB", [1, N], BF16)
        idb = sb("idb", [128, 128], BF16)
        idf = sb("idf", [128, 128], F32)
        ocb = sb("ocb", [128, 1], BF16)
        orf = sb("orf", [1, 128], F32)
        hrf = sb("hrf", [1, 128], F32)
        hrb = sb("hrb", [1, 128], BF16)
        mkb = sb("mkb", [SC, SC], BF16)
        nws = sb("nws", [128, DP], F32)
        rts = sb("rts", [128, DP * E8], BF16)
        rtsF = sb("rtsF", [128, DP * E8], F32)
        ocf = sb("ocf", [128, 1], F32)
        sbc = sb("sbc", [128, E8], F32)
        srow = sb("srow", [1, E8], F32)
        zcol = sb("zcol", [128, max(SC, NCH)], F32)
        orb = sb("orb", [1, 128], BF16)
        ecols = sb("ecols_s", [128, C8 * E8], F32)
        tvl = sb("tvl_s", [128, C8], F32)
        oobs = sb("oobs_s", [128, SLOT_TOT * 2 // 128], F32)
        eps5 = sb("eps5", [128, 1], F32)
        eps6 = sb("eps6", [128, 1], F32)
        hcolP = sb("hcolP", [128, 1], F32)
        hcolN = sb("hcolN", [128, 1], F32)

        XBLK = min(512, N)

        def ecast(E, out_ap, in_ap):
            if hasattr(E, "tensor_copy"):
                return E.tensor_copy(out_ap, in_ap)
            return E.copy(out_ap, in_ap)

        def xk(pt, blk=None):
            if blk is None:
                return [("x", pt, i) for i in range(N // XBLK)]
            return ("x", pt, blk)

        def xks(pt, t0, t1):
            return [("x", pt, i) for i in range(t0 // XBLK,
                                                (t1 + XBLK - 1) // XBLK)]

        def xsl(pt, t0, t1):
            return xs[:, pt * N + t0:pt * N + t1]

        b.op("sync", lambda E: E.dma_start(out=idb[:, :], in_=identb),
             writes=["idb"], dma=True, dsem="idb")
        b.op("sync", lambda E: E.dma_start(out=idf[:, :], in_=identf),
             writes=["idf"], dma=True, dsem="idf")
        b.op("sync", lambda E: E.dma_start(out=ocb[:, :], in_=onescb),
             writes=["ocb"], dma=True, dsem="ocb")
        b.op("sync", lambda E: E.dma_start(out=ocf[:, :], in_=onescf),
             writes=["ocf"], dma=True, dsem="ocf")
        b.op("sync", lambda E: E.dma_start(out=orf[:, :], in_=onesrf),
             writes=["orf"], dma=True, dsem="orf")
        b.op("sync", lambda E: E.dma_start(out=hrf[:, :], in_=halfrf),
             writes=["hrf"], dma=True, dsem="hrf")
        b.op("sync", lambda E: E.dma_start(out=hrb[:, :], in_=halfrb),
             writes=["hrb"], dma=True, dsem="hrb")
        b.op("sync", lambda E: E.dma_start(out=mkb[:, :], in_=maskbd),
             writes=["mkb"], dma=True, dsem="mkb")
        b.op("sync", lambda E: E.dma_start(out=nws[:, :], in_=nwd),
             writes=["nws"], dma=True, dsem="nws")
        b.op("sync", lambda E: E.dma_start(
            out=rtsF[:, :].rearrange("p (k e) -> p k e", k=DP),
            in_=rtd.rearrange("(k p) e -> p k e", p=128)),
            writes=["rtsF"], dma=True, dsem="rts")
        b.op("gpsimd", lambda E: E.tensor_copy(
            rts[:, :], rtsF[:, :]),
            reads=["rtsF"], writes=["rts"])
        b.op("gpsimd", lambda E: E.memset(zcol[:, :], 0.0), writes=["zcol"])
        b.op("sync", lambda E: E.dma_start(out=orb[:, :], in_=onesrb),
             writes=["orb"], dma=True, dsem="orb")
        b.op("sync", lambda E: E.dma_start(out=ecols[:, :], in_=ecolsd),
             writes=["ecols"], dma=True, dsem="ecols")
        b.op("sync", lambda E: E.dma_start(out=tvl[:, :], in_=tvld),
             writes=["tvl"], dma=True, dsem="tvl")
        b.op("sync", lambda E: E.dma_start(out=oobs[:, :], in_=oobsd),
             writes=["oobs"], dma=True, dsem="oobs")
        b.op("gpsimd", lambda E: E.memset(eps5[:, :], 1e-5), writes=["eps5"])
        b.op("gpsimd", lambda E: E.memset(eps6[:, :], 1e-6), writes=["eps6"])
        b.op("gpsimd", lambda E: E.memset(hcolP[:, :], 44.36), writes=["hcolP"])
        b.op("gpsimd", lambda E: E.memset(hcolN[:, :], -44.36), writes=["hcolN"])
        ZW = max(SC, NCH)
        for gg in range(NDUMP):
            for off in range(0, D, ZW):
                w = min(ZW, D - off)
                b.op("gpsimd", lambda E, gg=gg, off=off, w=w: E.dma_start(
                    out=xtmd_t.ap()[TILE + gg:TILE + gg + 1, off:off + w],
                    in_=zcol[0:1, 0:w]),
                    reads=["zcol"], writes=[("xtmd", "dump", gg, off)],
                    dma=True, dsem="xdump")

        def ln_stats(tag, xbB, sqB, xbK, sqK, psA, psAk, psB, psBk, rowt, rowtK,
                     eps):
            for c in range(N // NCH):
                t0 = c * NCH
                for pt in range(DP):
                    p = pt % 2
                    xb, sq = xbB[p], sqB[p]
                    kxb, ksq = xbK[p], sqK[p]
                    b.op("vector", lambda E, xb=xb, pt=pt, t0=t0:
                         E.tensor_copy(xb[:, 0:NCH], xsl(pt, t0, t0 + NCH)),
                         reads=xks(pt, t0, t0 + NCH), writes=kxb)
                    b.op("vector", lambda E, sq=sq, xb=xb: E.tensor_tensor(
                        sq[:, 0:NCH], xb[:, 0:NCH], xb[:, 0:NCH], ALU.mult),
                        reads=kxb, writes=ksq)
                    st, sp = (pt == 0), (pt == DP - 1)
                    b.op("tensor", lambda E, xb=xb, st=st, sp=sp: E.matmul(
                        psA[0:1, 0:NCH], ocb[:, :], xb[:, 0:NCH],
                        start=st, stop=sp),
                        reads=list(kxb) + ["ocb"], writes=[psAk])
                    b.op("tensor", lambda E, sq=sq, st=st, sp=sp: E.matmul(
                        psB[0:1, 0:NCH], ocb[:, :], sq[:, 0:NCH],
                        start=st, stop=sp),
                        reads=list(ksq) + ["ocb"], writes=[psBk])
                b.op("scalar", lambda E, t0=t0: E.activation(
                    statA[0:1, t0:t0 + NCH], psA[0:1, 0:NCH], AF.Copy,
                    scale=1.0 / D),
                    reads=[psAk], writes=["statA"])
                b.op("scalar", lambda E, t0=t0: E.activation(
                    statB[0:1, t0:t0 + NCH], psB[0:1, 0:NCH], AF.Copy,
                    scale=1.0 / D),
                    reads=[psBk], writes=["statB"])
                b.op("vector", lambda E, t0=t0: E.tensor_tensor(
                    rowt[0:1, 0:NCH], statA[0:1, t0:t0 + NCH],
                    statA[0:1, t0:t0 + NCH], ALU.mult),
                    reads=["statA"], writes=[rowtK])
                b.op("vector", lambda E, t0=t0: E.tensor_tensor(
                    statB[0:1, t0:t0 + NCH], statB[0:1, t0:t0 + NCH],
                    rowt[0:1, 0:NCH], ALU.subtract),
                    reads=[rowtK, "statB"], writes=["statB"])
                b.op("scalar", lambda E, t0=t0: E.activation(
                    statB[0:1, t0:t0 + NCH], statB[0:1, t0:t0 + NCH], AF.Ln,
                    bias=eps[0:1, 0:1]),
                    reads=["statB", "eps5"], writes=["statB"])
                b.op("scalar", lambda E, t0=t0: E.activation(
                    statB[0:1, t0:t0 + NCH], statB[0:1, t0:t0 + NCH], AF.Exp,
                    scale=-0.5),
                    reads=["statB"], writes=["statB"])

        weight_cached = set()

        def ffn_phase(tag):
            fc_ctx = ExitStack()
            with fc_ctx:
                fs = lambda nm, sh, dt: fc_ctx.enter_context(
                    nc.sbuf_tensor(tag + nm, sh, dt))
                fp = lambda nm, sh, dt: fc_ctx.enter_context(
                    nc.psum_tensor(tag + nm, sh, dt))
                K = lambda *a: (tag,) + a
                half = DFP // 2

                xln = fs("xln", [128, DP * TILE], BF16)
                xtm = fs("xtm", [128, C8 * D], BF16)
                hg = fs("hg", [128, DFP * NCH], BF16)
                xgog = [fs(f"xgog{i}", [128, GPE * D], BF16)
                        for i in range(2)]
                xg = fs("xg", [128, DP * CAP2], BF16)
                ogf = xg
                fcb = [fs(f"fcb{i}", [128, DP * FCC], BF16) for i in range(2)]
                pjb = [fs(f"pjb{i}", [128, DFP * 128], BF16) for i in range(2)]
                h1 = [fs(f"h1{i}", [128, NCH], BF16) for i in range(2)]
                evt = fs("evt", [128, NCH], F32)
                evt2 = fs("evt2", [128, NCH], F32)
                SEL = [fs(f"SEL{i}", [E8, TILE], BF16) for i in range(2)]
                pos = fs("pos", [E8, TILE], F32)
                oha = [fs(f"oha{i}", [128, C8 * E8], F32) for i in range(2)]
                posm = fs("posm", [128, C8 * E8], F32)
                ea_ = [fs(f"ea{i}", [128, C8], F32) for i in range(2)]
                pa_ = [fs(f"pa{i}", [128, C8], F32) for i in range(2)]
                si_ = [fs(f"si{i}", [128, C8], I32) for i in range(2)]
                ii_ = [fs(f"ii{i}", [128, C8], I32) for i in range(4)]
                sval = [fs(f"sval{i}", [128, 2 * C8], F32) for i in range(2)]
                idg = fs("idg", [128, GPE * E8 * 2], F32)
                ido = fs("ido", [128, GPE * E8], I32)
                lgb = fs("lgb", [128, E8], F32)
                tmp8 = fs("tmp8", [128, E8], F32)
                mcols = fs("mcols", [128, 2 * C8], F32)
                rcols = fs("rcols", [128, 2 * C8], F32)
                m1b = fs("m1b", [128, 1], F32)
                m2b = fs("m2b", [128, 1], F32)
                nm2b = fs("nm2b", [128, 1], F32)

                hP = [fp(f"hP{i}", [128, NCH], F32) for i in range(2)]
                ogP = [fp(f"ogP{i}", [128, CAP2], F32) for i in range(2)]
                trP = [fp(f"trP{i}", [128, 512], BF16) for i in range(2)]
                bcP = [fp(f"bcP{i}", [128, NCH], F32) for i in range(2)]

                ln_stats(tag, [h1[0][:, 0:NCH], h1[1][:, 0:NCH]],
                         [xgog[0][:, 0:NCH], xgog[1][:, 0:NCH]],
                         [[K("h1", 0)], [K("h1", 1)]],
                         [[K("xgog", 0, 0)], [K("xgog", 1, 0)]],
                         bcP[0], K("bcP", 0), bcP[1], K("bcP", 1),
                         evt, K("evt"), eps5)

                if ("srow",) not in weight_cached:
                    weight_cached.add(("srow",))
                    for kpt in range(DP):
                        b.op("tensor", lambda E, kpt=kpt: E.matmul(
                            bcP[0][0:1, 0:E8], ocf[:, :],
                            rtsF[:, kpt * E8:(kpt + 1) * E8],
                            start=(kpt == 0), stop=(kpt == DP - 1)),
                            reads=["ocf", "rtsF"], writes=[K("bcP", 0)])
                    b.op("scalar", lambda E: E.copy(
                        srow[0:1, :], bcP[0][0:1, 0:E8]),
                        reads=[K("bcP", 0)], writes=["srow"])
                    b.op("tensor", lambda E: E.matmul(
                        bcP[0][:, 0:E8], orf[:, :], srow[0:1, :],
                        start=True, stop=True),
                        reads=["srow", "orf"], writes=[K("bcP", 0)])
                    b.op("scalar", lambda E: E.copy(
                        sbc[:, :], bcP[0][:, 0:E8]),
                        reads=[K("bcP", 0)], writes=["sbc"])

                for par in range(2):
                    b.op("gpsimd", lambda E, par=par: E.memset(
                        xgog[par][:, :], 0.0),
                        writes=[K("xgog", par, g) for g in range(GPE)])
                for r in range(2):
                    b.op("vector", lambda E, r=r: E.tensor_copy(
                        sval[r][:, :].rearrange("p (c v) -> p c v", v=2)
                        [:, :, 0:1],
                        tvl[:, :].rearrange("p (c one) -> p c one", one=1)),
                        reads=["tvl"], writes=[K("sval", r)])

                def stream_fc(e, fcc, buf, bufk):
                    ck = ("wfc", e, fcc)
                    if ck not in weight_cached:
                        weight_cached.add(ck)
                        PW = NCH
                        kpp = max(1, PW // FCC)
                        for pc in range(DP * FCC // PW):
                            k0 = pc * kpp
                            stg = evt if pc % 2 == 0 else evt2
                            stgk = K("evt") if pc % 2 == 0 else K("evt2")
                            din = "sync" if (fcc + pc) % 2 == 0 else "gpsimd"
                            ce = ("vector", "scalar", "gpsimd")[pc % 3]
                            b.op(din, lambda E, e=e, fcc=fcc, k0=k0,
                                 kpp=kpp, stg=stg: E.dma_start(
                                     out=stg[:, :].rearrange(
                                         "p (k c) -> p k c", k=kpp),
                                     in_=fct.rearrange(
                                         "e (k p) c -> e p k c", p=128)
                                     [e, :, k0:k0 + kpp,
                                      fcc * FCC:(fcc + 1) * FCC]),
                                 reads=[stgk], writes=[stgk],
                                 dma=True, dsem="wst" + din + str(pc % 2))
                            b.op(ce, lambda E, buf=buf, pc=pc, PW=PW,
                                 stg=stg: ecast(E,
                                     buf[:, pc * PW:(pc + 1) * PW],
                                     stg[:, :]),
                                 reads=[stgk], writes=[bufk])
                        b.op("sync", lambda E, e=e, fcc=fcc, buf=buf:
                             E.dma_start(out=fctb[e, fcc], in_=buf[:, :]),
                             reads=[bufk], writes=[("fctb", e, fcc)],
                             dma=True, dsem=f"fcw{fcc % 2}")
                    else:
                        b.op("gpsimd", lambda E, e=e, fcc=fcc, buf=buf:
                             E.dma_start(out=buf[:, :], in_=fctb[e, fcc]),
                             reads=[("fctb", e, fcc)], writes=[bufk],
                             dma=True, dsem=f"fcr{fcc % 2}")

                def stream_pj(e, mo, buf, bufk, par):
                    ck = ("wpj", e, mo)
                    if ck not in weight_cached:
                        weight_cached.add(ck)
                        PW = NCH
                        kpj = PW // 128
                        for pc in range(DFP * 128 // PW):
                            k0 = pc * kpj
                            stg = evt if pc % 2 == 0 else evt2
                            stgk = K("evt") if pc % 2 == 0 else K("evt2")
                            din = "sync" if (mo + pc) % 2 == 0 else "gpsimd"
                            ce = ("vector", "scalar", "gpsimd")[pc % 3]
                            b.op(din, lambda E, e=e, mo=mo, k0=k0,
                                 kpj=kpj, stg=stg: E.dma_start(
                                     out=stg[:, :].rearrange(
                                         "p (k c) -> p k c", k=kpj),
                                     in_=pjt.rearrange(
                                         "e (k p) c -> e p k c", p=128)
                                     [e, :, k0:k0 + kpj,
                                      mo * 128:(mo + 1) * 128]),
                                 reads=[stgk], writes=[stgk],
                                 dma=True, dsem="wst" + din + str(pc % 2))
                            b.op(ce, lambda E, buf=buf, pc=pc, PW=PW,
                                 stg=stg: ecast(E,
                                     buf[:, pc * PW:(pc + 1) * PW],
                                     stg[:, :]),
                                 reads=[stgk], writes=[bufk])
                        b.op("sync", lambda E, e=e, mo=mo, buf=buf:
                             E.dma_start(
                                 out=pjtb[e, mo], in_=buf[:, :]),
                             reads=[bufk], writes=[("pjtb", e, mo)],
                             dma=True, dsem=f"pjw{par}")
                    else:
                        b.op("sync", lambda E, e=e, mo=mo, buf=buf:
                             E.dma_start(
                                 out=buf[:, :], in_=pjtb[e, mo]),
                             reads=[("pjtb", e, mo)], writes=[bufk],
                             dma=True, dsem=f"pjr{par}")

                nchn = TILE // NCH
                for tile in range(NT):
                    tt0 = tile * TILE
                    for tc in range(C8):
                        t0c = tt0 + tc * 128
                        b.op("tensor", lambda E, t0c=t0c, tc=tc: E.transpose(
                            trP[0][0:128, 2 * tc:2 * tc + 1],
                            statA[0:1, t0c:t0c + 128], idb[0:1, 0:1]),
                            reads=["statA", "idb"], writes=[K("trP", 0)])
                        b.op("tensor", lambda E, t0c=t0c, tc=tc: E.transpose(
                            trP[1][0:128, 2 * tc:2 * tc + 1],
                            statB[0:1, t0c:t0c + 128], idb[0:1, 0:1]),
                            reads=["statB", "idb"], writes=[K("trP", 1)])
                    for tc in range(C8):
                        b.op("scalar", lambda E, tc=tc: E.copy(
                            mcols[:, 2 * tc:2 * tc + 1],
                            trP[0][0:128, 2 * tc:2 * tc + 1]),
                            reads=[K("trP", 0)], writes=[K("mcols")])
                        b.op("scalar", lambda E, tc=tc: E.copy(
                            rcols[:, 2 * tc:2 * tc + 1],
                            trP[1][0:128, 2 * tc:2 * tc + 1]),
                            reads=[K("trP", 1)], writes=[K("rcols")])
                    for tc in range(C8):
                        t0c = tt0 + tc * 128
                        for pt in range(DP):
                            b.op("tensor", lambda E, pt=pt, t0c=t0c:
                                 E.matmul(
                                     bcP[0][:, 0:E8],
                                     xsl(pt, t0c, t0c + 128),
                                     rtsF[:, pt * E8:(pt + 1) * E8],
                                     start=(pt == 0), stop=(pt == DP - 1)),
                                 reads=xks(pt, t0c, t0c + 128) + ["rtsF"],
                                 writes=[K("bcP", 0)])
                        b.op("vector", lambda E, tc=tc: E.tensor_scalar(
                            tmp8[:, :], sbc[:, :], mcols[:, 2 * tc:2 * tc + 1], None,
                            ALU.mult),
                            reads=["sbc", K("mcols")], writes=[K("tmp8")])
                        b.op("vector", lambda E: E.tensor_tensor(
                            lgb[:, :], bcP[0][:, 0:E8], tmp8[:, :],
                            ALU.subtract),
                            reads=[K("bcP", 0), K("tmp8")], writes=[K("lgb")])
                        b.op("vector", lambda E, tc=tc: E.tensor_scalar(
                            lgb[:, :], lgb[:, :], rcols[:, 2 * tc:2 * tc + 1], None,
                            ALU.mult),
                            reads=[K("lgb"), K("rcols")], writes=[K("lgb")])
                        b.op("vector", lambda E: E.reduce_max(
                            m1b[:, :], lgb[:, :], AX.X),
                            reads=[K("lgb")], writes=[K("m1b")])
                        o1 = oha[0][:, tc * E8:(tc + 1) * E8]
                        o2 = oha[1][:, tc * E8:(tc + 1) * E8]
                        b.op("vector", lambda E, o1=o1: E.tensor_scalar(
                            o1, lgb[:, :], m1b[:, 0:1], None, ALU.is_ge),
                            reads=[K("lgb"), K("m1b")], writes=[K("oha", 0)])
                        b.op("vector", lambda E, o1=o1: E.scalar_tensor_tensor(
                            tmp8[:, :], o1, -1e30, lgb[:, :],
                            ALU.mult, ALU.add),
                            reads=[K("oha", 0), K("lgb")], writes=[K("tmp8")])
                        b.op("vector", lambda E: E.reduce_max(
                            m2b[:, :], tmp8[:, :], AX.X),
                            reads=[K("tmp8")], writes=[K("m2b")])
                        b.op("vector", lambda E, o2=o2: E.tensor_scalar(
                            o2, tmp8[:, :], m2b[:, 0:1], None, ALU.is_ge),
                            reads=[K("tmp8"), K("m2b")], writes=[K("oha", 1)])
                        b.op("scalar", lambda E: E.activation(
                            nm2b[:, :], m2b[:, 0:1], AF.Copy, scale=-1.0),
                            reads=[K("m2b")], writes=[K("nm2b")])
                        b.op("scalar", lambda E, tc=tc: E.activation(
                            sval[0][:, 2 * tc + 1:2 * tc + 2], m1b[:, 0:1],
                            AF.Sigmoid, bias=nm2b[:, 0:1]),
                            reads=[K("m1b"), K("nm2b")], writes=[K("sval", 0)])
                        b.op("scalar", lambda E, tc=tc: E.activation(
                            sval[1][:, 2 * tc + 1:2 * tc + 2], m1b[:, 0:1],
                            AF.Sigmoid, bias=m2b[:, 0:1], scale=-1.0),
                            reads=[K("m1b"), K("m2b")], writes=[K("sval", 1)])
                        for r in range(2):
                            oo = oha[r][:, tc * E8:(tc + 1) * E8]
                            b.op("tensor", lambda E, oo=oo: E.transpose(
                                bcP[1][0:E8, 0:128], oo, idf[:, :]),
                                reads=[K("oha", r), "idf"],
                                writes=[K("bcP", 1)])
                            b.op("scalar", lambda E, r=r, tc=tc: E.copy(
                                SEL[r][:, tc * 128:(tc + 1) * 128],
                                bcP[1][0:E8, 0:128]),
                                reads=[K("bcP", 1)], writes=[K("SEL", r)])

                    for c in range(nchn):
                        t0 = tt0 + c * NCH
                        b.op("tensor", lambda E, t0=t0: E.matmul(
                            bcP[0][:, :], orb[:, :], statA[0:1, t0:t0 + NCH],
                            start=True, stop=True),
                            reads=["statA", "orb"], writes=[K("bcP", 0)])
                        b.op("tensor", lambda E, t0=t0: E.matmul(
                            bcP[1][:, :], orb[:, :], statB[0:1, t0:t0 + NCH],
                            start=True, stop=True),
                            reads=["statB", "orb"], writes=[K("bcP", 1)])
                        for pt in range(DP):
                            b.op("vector", lambda E, pt=pt, t0=t0:
                                 E.tensor_tensor(
                                     evt[:, :], xsl(pt, t0, t0 + NCH),
                                     bcP[0][:, :], ALU.subtract),
                                 reads=xks(pt, t0, t0 + NCH) + [K("bcP", 0)],
                                 writes=[K("evt")])
                            b.op("vector", lambda E, pt=pt, c=c:
                                 E.tensor_tensor(
                                     xln[:, pt * TILE + c * NCH:
                                         pt * TILE + (c + 1) * NCH],
                                     evt[:, :], bcP[1][:, :], ALU.mult),
                                 reads=[K("evt"), K("bcP", 1)],
                                 writes=[K("xln", pt)])

                    for c in range(C8):
                        for q in range(DP // TQ):
                            par = (c * (DP // TQ) + q) % 2
                            for j in range(TQ):
                                pt = q * TQ + j
                                b.op("tensor", lambda E, c=c, pt=pt, j=j,
                                     par=par: E.transpose(
                                         trP[par][:, j * 128:(j + 1) * 128],
                                         xln[:, pt * TILE + c * 128:
                                             pt * TILE + (c + 1) * 128],
                                         idb[:, :]),
                                     reads=[K("xln", pt), "idb"],
                                     writes=[K("trP", par)])
                            b.op("scalar", lambda E, c=c, q=q, par=par:
                                 E.copy(xtm[:, c * D + q * TQ * 128:
                                            c * D + (q + 1) * TQ * 128],
                                        trP[par][:, 0:TQ * 128]),
                                 reads=[K("trP", par)], writes=[K("xtm")])
                    b.op("sync", lambda E: E.dma_start(
                        out=xtmd_t.ap()[0:TILE, :].rearrange("(p c) d -> p c d", p=128),
                        in_=xtm[:, :].rearrange("p (c d) -> p c d", c=C8)),
                        reads=[K("xtm")], writes=[("xtmd",)], dma=True,
                        dsem="xtmd")

                    for r in range(2):
                        for c in range(TILE // NCH):
                            init = (0.0 if c == 0 else
                                    pos[:, c * NCH - 1:c * NCH])
                            b.op("vector", lambda E, c=c, r=r, init=init:
                                 E.tensor_tensor_scan(
                                     pos[0:E8, c * NCH:(c + 1) * NCH],
                                     SEL[r][:, c * NCH:(c + 1) * NCH],
                                     zcol[0:E8, 0:NCH], init,
                                     ALU.add, ALU.add),
                                 reads=[K("SEL", r), "zcol", K("pos")],
                                 writes=[K("pos")])
                        for tc in range(C8):
                            b.op("tensor", lambda E, tc=tc: E.transpose(
                                bcP[1][0:128, 0:E8],
                                pos[:, tc * 128:(tc + 1) * 128],
                                idf[0:E8, 0:E8]),
                                reads=[K("pos"), "idf"], writes=[K("bcP", 1)])
                            b.op("scalar", lambda E, tc=tc: E.copy(
                                posm[:, tc * E8:(tc + 1) * E8],
                                bcP[1][0:128, 0:E8]),
                                reads=[K("bcP", 1)], writes=[K("posm")])
                        b.op("vector", lambda E, r=r: E.tensor_tensor(
                            posm[:, :], posm[:, :], oha[r][:, :], ALU.mult),
                            reads=[K("posm"), K("oha", r)], writes=[K("posm")])
                        b.op("vector", lambda E, r=r: E.tensor_reduce(
                            pa_[r][:, :].rearrange("p (c one) -> p c one", one=1),
                            posm[:, :].rearrange("p (c e) -> p c e", e=E8),
                            AX.X, ALU.add),
                            reads=[K("posm")], writes=[K("pa", r)])
                        b.op("vector", lambda E, r=r: E.tensor_scalar(
                            pa_[r][:, :], pa_[r][:, :], -1.0, None, ALU.add),
                            reads=[K("pa", r)], writes=[K("pa", r)])
                        b.op("vector", lambda E, r=r: E.tensor_tensor(
                            posm[:, :], ecols[:, :], oha[r][:, :], ALU.mult),
                            reads=["ecols", K("oha", r), K("posm")],
                            writes=[K("posm")])
                        b.op("vector", lambda E, r=r: E.tensor_reduce(
                            ea_[r][:, :].rearrange("p (c one) -> p c one", one=1),
                            posm[:, :].rearrange("p (c e) -> p c e", e=E8),
                            AX.X, ALU.add),
                            reads=[K("posm")], writes=[K("ea", r)])
                        ST = 2 * E8 * NG
                        b.op("vector", lambda E, r=r: E.tensor_copy(
                            ii_[0][:, :], pa_[r][:, :]),
                            reads=[K("pa", r)], writes=[K("ii", 0)])
                        b.op("vector", lambda E: E.tensor_scalar(
                            ii_[1][:, :], ii_[0][:, :], 7, None,
                            ALU.arith_shift_right),
                            reads=[K("ii", 0)], writes=[K("ii", 1)])
                        b.op("vector", lambda E: E.tensor_scalar(
                            ii_[0][:, :], ii_[0][:, :], 127, None,
                            ALU.bitwise_and),
                            reads=[K("ii", 0)], writes=[K("ii", 0)])
                        b.op("vector", lambda E, r=r: E.tensor_copy(
                            ii_[2][:, :], ea_[r][:, :]),
                            reads=[K("ea", r)], writes=[K("ii", 2)])
                        b.op("vector", lambda E, r=r: E.tensor_scalar(
                            ii_[2][:, :], ii_[2][:, :], 2 * NG, r * NG,
                            ALU.mult, ALU.add),
                            reads=[K("ii", 2)], writes=[K("ii", 2)])
                        b.op("vector", lambda E: E.tensor_tensor(
                            ii_[2][:, :], ii_[2][:, :], ii_[1][:, :],
                            ALU.add),
                            reads=[K("ii", 2), K("ii", 1)], writes=[K("ii", 2)])
                        b.op("vector", lambda E, ST=ST: E.tensor_scalar(
                            ii_[0][:, :], ii_[0][:, :], ST, None, ALU.mult),
                            reads=[K("ii", 0)], writes=[K("ii", 0)])
                        b.op("vector", lambda E: E.tensor_tensor(
                            ii_[2][:, :], ii_[2][:, :], ii_[0][:, :],
                            ALU.add),
                            reads=[K("ii", 2), K("ii", 0)], writes=[K("ii", 2)])
                        b.op("vector", lambda E, r=r: E.tensor_scalar(
                            pa_[r][:, :], pa_[r][:, :], float(CAPR), None,
                            ALU.is_ge),
                            reads=[K("pa", r)], writes=[K("pa", r)])
                        b.op("vector", lambda E, r=r: E.tensor_copy(
                            ii_[3][:, :], pa_[r][:, :]),
                            reads=[K("pa", r)], writes=[K("ii", 3)])
                        b.op("vector", lambda E: E.tensor_scalar(
                            ii_[3][:, :], ii_[3][:, :], SLOT_TOT, None,
                            ALU.mult),
                            reads=[K("ii", 3)], writes=[K("ii", 3)])
                        b.op("vector", lambda E: E.tensor_tensor(
                            ii_[2][:, :], ii_[2][:, :], ii_[3][:, :],
                            ALU.add),
                            reads=[K("ii", 2), K("ii", 3)], writes=[K("ii", 2)])
                        b.op("vector", lambda E, r=r: E.tensor_scalar_min(
                            si_[r][:, :], ii_[2][:, :], SLOT_TOT),
                            reads=[K("ii", 2)], writes=[K("si", r)])

                    b.op("sync", lambda E: E.dma_start(
                        out=igbd_t.ap()[0:SLOT_TOT, :].rearrange("(p a) v -> p (a v)", p=128),
                        in_=oobs[:, :]),
                        reads=["oobs"], writes=[("igbd",)], dma=True,
                        dsem="igi")
                    for tc in range(C8):
                        for r in range(2):
                            if DBG["no_igs"]:
                                continue
                            b.op("gpsimd", lambda E, tc=tc, r=r:
                                 E.indirect_dma_start(
                                     igbd_t.ap(), IndirectOffsetOnAxis(
                                         si_[r][:, tc:tc + 1], 0),
                                     sval[r][:, 2 * tc:2 * tc + 2], None),
                                 reads=[("igbd",), K("si", r), K("sval", r)],
                                 dma=True, dsem=f"igs{r}")
                    b.op("sync", lambda E: E.dma_start(
                        out=idg[:, :].rearrange("p (a v) -> p a v", v=2),
                        in_=igbd_t.ap()[0:SLOT_TOT, :].rearrange("(p a) v -> p a v", p=128)),
                        reads=[("igbd",)], writes=[("igbd",), K("idg")],
                        dma=True, dsem="igr")
                    b.op("vector", lambda E: E.tensor_copy(
                        ido[:, :].rearrange("p (a one) -> p a one", one=1),
                        idg[:, :].rearrange("p (a v) -> p a v", v=2)
                        [:, :, 0:1]),
                        reads=[K("idg")], writes=[K("ido")])

                    e = E8
                    nfc = DFF // FCC
                    mpc = FCC // 128
                    for c2 in range(nchn):
                        for fcc in range(nfc):
                            fb_ = fcb[fcc % 2]
                            fbk = K("fcb", fcc % 2)
                            stream_fc(e, fcc, fb_, fbk)
                            for m in range(mpc):
                                mpt = fcc * mpc + m
                                pp = mpt % 2
                                hPk = K("hP", pp)
                                for kpt in range(DP):
                                    b.op("tensor", lambda E, fb_=fb_, m=m,
                                         c2=c2, kpt=kpt, pp=pp: E.matmul(
                                             hP[pp][:, :],
                                             fb_[:, kpt * FCC + m * 128:
                                                 kpt * FCC + (m + 1) * 128],
                                             xln[:, kpt * TILE + c2 * NCH:
                                                 kpt * TILE + (c2 + 1) * NCH],
                                             start=(kpt == 0),
                                             stop=(kpt == DP - 1)),
                                         reads=[fbk, K("xln", kpt)],
                                         writes=[hPk])
                                hb = h1[pp]
                                b.op("scalar", lambda E, hb=hb, pp=pp:
                                     E.activation(hb[:, 0:NCH],
                                                  hP[pp][:, :], AF.Relu),
                                     reads=[hPk], writes=[K("h1", pp)])
                                b.op("vector", lambda E, hb=hb, mpt=mpt:
                                     E.tensor_tensor(
                                         hg[:, mpt * NCH:(mpt + 1) * NCH],
                                         hb[:, 0:NCH], hb[:, 0:NCH],
                                         ALU.mult),
                                     reads=[K("h1", pp)],
                                     writes=[K("hg", mpt)])
                        for mo in range(DP):
                            pp = mo % 2
                            ogPk = K("ogP", pp)
                            pb_ = pjb[pp]
                            pbk = K("pjb", pp)
                            stream_pj(e, mo, pb_, pbk, pp)
                            for kk in range(DFP):
                                b.op("tensor", lambda E, pb_=pb_, kk=kk,
                                     pp=pp: E.matmul(
                                         ogP[pp][:, 0:NCH],
                                         pb_[:, kk * 128:(kk + 1) * 128],
                                         hg[:, kk * NCH:(kk + 1) * NCH],
                                         start=(kk == 0),
                                         stop=(kk == DFP - 1)),
                                     reads=[pbk, K("hg", kk)],
                                     writes=[ogPk])
                            t0 = tt0 + c2 * NCH
                            b.op("vector", lambda E, mo=mo, t0=t0, pp=pp:
                                 E.scalar_tensor_tensor(
                                     xsl(mo, t0, t0 + NCH),
                                     ogP[pp][:, 0:NCH], 0.5,
                                     xsl(mo, t0, t0 + NCH),
                                     ALU.mult, ALU.add),
                                 reads=[ogPk, xk(mo, t0 // XBLK)], writes=[xk(mo, t0 // XBLK)])

                    for e in range(E8):
                        epar = e % 2
                        for g in range(GPE):
                            if DBG["no_gx"]:
                                continue
                            b.op("gpsimd", lambda E, e=e, g=g, epar=epar:
                                 E.indirect_dma_start(
                                     xgog[epar][:, g * D:(g + 1) * D], None,
                                     xtmd_t.ap(), IndirectOffsetOnAxis(
                                         ido[:, e * GPE + g:
                                             e * GPE + g + 1], 0)),
                                 reads=[("xtmd",), K("ido"),
                                        K("xgog", epar, g)] +
                                       [("xtmd", "dump", gg, off)
                                        for gg in range(NDUMP)
                                        for off in range(0, D,
                                                         max(SC, NCH))],
                                 writes=[K("xgog", epar, g)], dma=True,
                                 dsem=f"gx{epar}{g}")
                        for g in range(GPE):
                            for q in range(DP // TQ):
                                par = (g * (DP // TQ) + q) % 2
                                for j in range(TQ):
                                    pt = q * TQ + j
                                    b.op("tensor", lambda E, g=g, pt=pt, j=j,
                                         par=par, epar=epar: E.transpose(
                                             trP[par][:, j * 128:
                                                      (j + 1) * 128],
                                             xgog[epar][:, g * D + pt * 128:
                                                  g * D + (pt + 1) * 128],
                                             idb[:, :]),
                                         reads=[K("xgog", epar, g), "idb"],
                                         writes=[K("trP", par)])
                                b.op("scalar", lambda E, g=g, q=q, par=par:
                                     E.copy(
                                         xg[:, :].rearrange(
                                             "p (k s) -> p k s", k=DP)
                                         [:, q * TQ:(q + 1) * TQ,
                                          g * 128:(g + 1) * 128],
                                         trP[par][:, 0:TQ * 128].rearrange(
                                             "p (k s) -> p k s", k=TQ)),
                                     reads=[K("trP", par)],
                                     writes=[K("xg")])
                        for fcc in range(nfc):
                            fb_ = fcb[fcc % 2]
                            fbk = K("fcb", fcc % 2)
                            stream_fc(e, fcc, fb_, fbk)
                            for m in range(mpc):
                                mpt = fcc * mpc + m
                                pp = mpt % 2
                                hPk = K("hP", pp)
                                for kpt in range(DP):
                                    b.op("tensor", lambda E, fb_=fb_, m=m,
                                         kpt=kpt, pp=pp: E.matmul(
                                             hP[pp][:, 0:CAP2],
                                             fb_[:, kpt * FCC + m * 128:
                                                 kpt * FCC + (m + 1) * 128],
                                             xg[:, kpt * CAP2:
                                                (kpt + 1) * CAP2],
                                             start=(kpt == 0),
                                             stop=(kpt == DP - 1)),
                                         reads=[fbk, K("xg")],
                                         writes=[hPk])
                                hb = h1[pp]
                                b.op("scalar", lambda E, hb=hb, pp=pp:
                                     E.activation(hb[:, 0:CAP2],
                                                  hP[pp][:, 0:CAP2], AF.Relu),
                                     reads=[hPk], writes=[K("h1", pp)])
                                b.op("vector", lambda E, hb=hb, mpt=mpt:
                                     E.tensor_tensor(
                                         hg[:, mpt * NCH:mpt * NCH + CAP2],
                                         hb[:, 0:CAP2], hb[:, 0:CAP2],
                                         ALU.mult),
                                     reads=[K("h1", pp)],
                                     writes=[K("hg", mpt)])
                        for mo in range(DP):
                            pp = mo % 2
                            ogPk = K("ogP", pp)
                            pb_ = pjb[pp]
                            pbk = K("pjb", pp)
                            stream_pj(e, mo, pb_, pbk, pp)
                            for kk in range(DFP):
                                b.op("tensor", lambda E, pb_=pb_, kk=kk,
                                     pp=pp: E.matmul(
                                         ogP[pp][:, 0:CAP2],
                                         pb_[:, kk * 128:(kk + 1) * 128],
                                         hg[:, kk * NCH:kk * NCH + CAP2],
                                         start=(kk == 0),
                                         stop=(kk == DFP - 1)),
                                     reads=[pbk, K("hg", kk)],
                                     writes=[ogPk])
                            b.op("scalar", lambda E, mo=mo, pp=pp: E.copy(
                                ogf[:, mo * CAP2:(mo + 1) * CAP2],
                                ogP[pp][:, 0:CAP2]),
                                reads=[ogPk], writes=[K("xg")])
                        for g in range(GPE):
                            for q in range(DP // TQ):
                                par = (g * (DP // TQ) + q) % 2
                                for j in range(TQ):
                                    pt = q * TQ + j
                                    b.op("tensor", lambda E, g=g, pt=pt, j=j,
                                         par=par: E.transpose(
                                             trP[par][:, j * 128:
                                                      (j + 1) * 128],
                                             ogf[:, pt * CAP2 + g * 128:
                                                 pt * CAP2 + (g + 1) * 128],
                                             idb[:, :]),
                                         reads=[K("xg"), "idb"],
                                         writes=[K("trP", par)])
                                gc = idg[:, (e * GPE + g) * 2 + 1:
                                         (e * GPE + g) * 2 + 2]
                                b.op("vector", lambda E, g=g, q=q, par=par,
                                     gc=gc, epar=epar: E.tensor_scalar(
                                         xgog[epar][:, g * D + q * TQ * 128:
                                              g * D + (q + 1) * TQ * 128],
                                         trP[par][:, 0:TQ * 128],
                                         gc[:, 0:1], None, ALU.mult),
                                     reads=[K("trP", par), K("idg")],
                                     writes=[K("xgog", epar, g)])
                        for g in range(GPE):
                            if DBG["no_dsc"]:
                                continue
                            dst = dltA_t if g < NG else dltB_t
                            dk = ("dltA",) if g < NG else ("dltB",)
                            b.op("gpsimd", lambda E, e=e, g=g, dst=dst,
                                 epar=epar: E.indirect_dma_start(
                                     dst.ap(), IndirectOffsetOnAxis(
                                         ido[:, e * GPE + g:
                                             e * GPE + g + 1], 0),
                                     xgog[epar][:, g * D:(g + 1) * D], None),
                                 reads=[dk, K("ido"), K("xgog", epar, g)],
                                 dma=True, dsem=f"dsc{epar}{g}")

                    for dt_, dkk, dsm in ((dltA_t, ("dltA",), "dra"),
                                          (dltB_t, ("dltB",), "drb")):
                        b.op("sync", lambda E, dt_=dt_: E.dma_start(
                            out=xtm[:, :].rearrange("p (c d) -> p c d", c=C8),
                            in_=dt_.ap()[0:TILE, :].rearrange(
                                "(p c) d -> p c d", p=128)),
                            reads=[dkk], writes=[dkk, K("xtm")],
                            dma=True, dsem=dsm)
                        for pt in range(DP):
                            for cq in range(C8 // CQ):
                                par = (pt * (C8 // CQ) + cq) % 2
                                for j in range(CQ):
                                    c = cq * CQ + j
                                    b.op("tensor", lambda E, c=c, pt=pt, j=j,
                                         par=par: E.transpose(
                                             trP[par][:, j * 128:
                                                      (j + 1) * 128],
                                             xtm[:, c * D + pt * 128:
                                                 c * D + (pt + 1) * 128],
                                             idb[:, :]),
                                         reads=[K("xtm"), "idb"],
                                         writes=[K("trP", par)])
                                t0 = tt0 + cq * CQ * 128
                                b.op("vector", lambda E, pt=pt, t0=t0,
                                     par=par: E.scalar_tensor_tensor(
                                         xsl(pt, t0, t0 + CQ * 128),
                                         trP[par][:, 0:CQ * 128], 0.5,
                                         xsl(pt, t0, t0 + CQ * 128),
                                         ALU.mult, ALU.add),
                                     reads=[K("trP", par),
                                            xk(pt, t0 // XBLK)],
                                     writes=[xk(pt, t0 // XBLK)])
                b.barrier()

        def attn_phase(tag):
            at_ctx = ExitStack()
            with at_ctx:
                fs = lambda nm, sh, dt: at_ctx.enter_context(
                    nc.sbuf_tensor(tag + nm, sh, dt))
                fp = lambda nm, sh, dt: at_ctx.enter_context(
                    nc.psum_tensor(tag + nm, sh, dt))
                K = lambda *a: (tag,) + a

                ws = {}
                for wi, (nm, dram) in enumerate([("wq", wqd), ("wk", wkd),
                                                 ("wv", wvd), ("wo", wod)]):
                    ws[nm] = fs(nm, [128, DP * D], BF16)
                    ck = ("watt", nm)
                    if ck not in weight_cached:
                        weight_cached.add(ck)
                        PW = NCH
                        mp = max(1, D // PW)
                        for pc in range(DP * D // PW):
                            kk_, mo_ = pc // mp, (pc % mp) * PW
                            ce = ("vector", "scalar", "gpsimd")[pc % 3]
                            b.op("sync", lambda E, dram=dram, kk_=kk_,
                                 mo_=mo_, PW=PW: E.dma_start(
                                     out=aevt[:, 0:PW],
                                     in_=dram.rearrange(
                                         "(k p) m -> p k m", p=128)
                                     [:, kk_, mo_:mo_ + PW]),
                                 reads=[K("aevt")], writes=[K("aevt")],
                                 dma=True, dsem="wst")
                            b.op(ce, lambda E, s=ws[nm], pc=pc, PW=PW:
                                 ecast(E,
                                     s[:, pc * PW:(pc + 1) * PW],
                                     aevt[:, 0:PW]),
                                 reads=[K("aevt")], writes=[K(nm)])
                        b.op("sync", lambda E, s=ws[nm], wi=wi: E.dma_start(
                            out=awtb[wi], in_=s[:, :]),
                            reads=[K(nm)], writes=[("awtb", wi)],
                            dma=True, dsem="wsw")
                    else:
                        b.op("sync", lambda E, s=ws[nm], wi=wi: E.dma_start(
                            out=s[:, :], in_=awtb[wi]),
                            reads=[("awtb", wi)], writes=[K(nm)],
                            dma=True, dsem="w_" + nm)

                xlnc = fs("xlnc", [128, DP * CH], BF16)
                bcMc = fs("bcMc", [128, CH], F32)
                bcRc = fs("bcRc", [128, CH], F32)
                qb = fs("qb", [128, DP * CH], BF16)
                fb = fs("fb", [128, DP * CH], BF16)
                vb = fs("vb", [128, DP * CH], BF16)
                ob = fs("ob", [128, DP * CH], BF16)
                lfb = fs("lfb", [128, DP * CH], F32)
                cb_ = fs("cb_", [128, DP * CH], F32)
                QB = fs("QB", [128, DP * CH], BF16)
                qRb = fs("qRb", [128, DP * CH], BF16)
                son = fs("son", [128, DP * CH], BF16)
                rbc = fs("rbc", [128, CH], F32)
                rrow = fs("rrow", [1, CH], F32)
                Sb_ = fs("Sb_", [128, DP * 128], BF16)
                qt = [fs(f"qt{i}", [128, SC], BF16) for i in range(2)]
                kt = [fs(f"kt{i}", [128, SC], BF16) for i in range(2)]
                kh = [fs(f"kh{i}", [128, SC], BF16) for i in range(2)]
                atm = [fs(f"atm{i}", [SC, SC], BF16) for i in range(2)]
                vk = [fs(f"vk{i}", [SC, 256], BF16) for i in range(2)]
                rcol = [fs(f"rcol{i}", [128, 1], F32) for i in range(2)]
                aevt = fs("aevt", [128, NCH], F32)

                qkvP = [fp(f"qkvP{i}", [128, max(CH, NCH)], F32)
                        for i in range(2)]
                glaA = [fp(f"glaA{i}", [128, 2 * SC], F32) for i in range(2)]
                glaT = [fp(f"glaT{i}", [SC, 2 * 128], BF16) for i in range(2)]
                glaS = [fp(f"glaS{i}", [128, 128], F32) for i in range(2)]

                nks = (NCH + CH - 1) // CH
                ln_stats(tag, [qb[:, 0:NCH], vb[:, 0:NCH]],
                         [fb[:, 0:NCH], ob[:, 0:NCH]],
                         [[K("qb", i) for i in range(nks)],
                          [K("vb", i) for i in range(nks)]],
                         [[K("fb", i) for i in range(nks)],
                          [K("ob", i) for i in range(nks)]],
                         qkvP[0], K("qkvP", 0), qkvP[1], K("qkvP", 1),
                         aevt, K("aevt"), eps5)

                b.op("gpsimd", lambda E: E.memset(Sb_[:, :], 0.0),
                     writes=[K("Sb_", h) for h in range(DP)])

                for ch in range(N // CH):
                    t0 = ch * CH
                    b.op("tensor", lambda E, t0=t0: E.matmul(
                        qkvP[0][:, 0:CH], orb[:, :], statA[0:1, t0:t0 + CH],
                        start=True, stop=True),
                        reads=["statA", "orb"], writes=[K("qkvP", 0)])
                    b.op("tensor", lambda E, t0=t0: E.matmul(
                        qkvP[1][:, 0:CH], orb[:, :], statB[0:1, t0:t0 + CH],
                        start=True, stop=True),
                        reads=["statB", "orb"], writes=[K("qkvP", 1)])
                    b.op("vector", lambda E: E.tensor_copy(
                        bcMc[:, :], qkvP[0][:, 0:CH]),
                        reads=[K("qkvP", 0)], writes=[K("bcMc")])
                    b.op("vector", lambda E: E.tensor_copy(
                        bcRc[:, :], qkvP[1][:, 0:CH]),
                        reads=[K("qkvP", 1)], writes=[K("bcRc")])
                    for pt in range(DP):
                        b.op("vector", lambda E, pt=pt, t0=t0:
                             E.tensor_tensor(aevt[:, 0:CH],
                                             xsl(pt, t0, t0 + CH),
                                             bcMc[:, :], ALU.subtract),
                             reads=xks(pt, t0, t0 + CH) + [K("bcMc")], writes=[K("aevt")])
                        b.op("vector", lambda E, pt=pt: E.tensor_tensor(
                            xlnc[:, pt * CH:(pt + 1) * CH], aevt[:, 0:CH],
                            bcRc[:, :], ALU.mult),
                            reads=[K("aevt"), K("bcRc")],
                            writes=[K("xlnc", pt)])
                    for nm, dst, dstk, act in [("wq", qb, "qb", AF.Silu),
                                               ("wk", fb, "fb", AF.Sigmoid),
                                               ("wv", vb, "vb", AF.Copy)]:
                        w = ws[nm]
                        for mo in range(DP):
                            pp = (mo // 2) % 2
                            co = (mo % 2) * CH
                            for kpt in range(DP):
                                b.op("tensor", lambda E, w=w, mo=mo, kpt=kpt,
                                     pp=pp, co=co: E.matmul(
                                         qkvP[pp][:, co:co + CH],
                                         w[:, kpt * D + mo * 128:
                                           kpt * D + (mo + 1) * 128],
                                         xlnc[:, kpt * CH:(kpt + 1) * CH],
                                         start=(kpt == 0),
                                         stop=(kpt == DP - 1)),
                                     reads=[K(nm), K("xlnc", kpt)],
                                     writes=[K("qkvP", pp)])
                            if mo % 2 == 1:
                                b.op("scalar", lambda E, dst=dst, mo=mo,
                                     pp=pp, act=act: E.activation(
                                         dst[:, (mo - 1) * CH:(mo + 1) * CH],
                                         qkvP[pp][:, 0:2 * CH], act),
                                     reads=[K("qkvP", pp)],
                                     writes=[K(dstk, mo - 1), K(dstk, mo)])
                    allf = [K("fb", mo) for mo in range(DP)]
                    alll = [K("lfb", mo) for mo in range(DP)]
                    b.op("scalar", lambda E: E.activation(
                        lfb[:, :], fb[:, :], AF.Sigmoid),
                        reads=allf, writes=alll)
                    b.op("scalar", lambda E: E.activation(
                        lfb[:, :], lfb[:, :], AF.Ln),
                        reads=alll, writes=alll)
                    mid = SC // 2
                    EMH = float(np.exp(-44.36))
                    allc = [K("cb", h) for h in range(DP)]
                    for h in range(DP):
                        for sc in range(NSC):
                            o0 = h * CH + sc * SC
                            b.op("vector", lambda E, o0=o0:
                                 E.tensor_tensor_scan(
                                     cb_[:, o0:o0 + SC], lfb[:, o0:o0 + SC],
                                     zcol[:, 0:SC], 0.0, ALU.add, ALU.add),
                                 reads=[K("lfb", h), "zcol"],
                                 writes=[K("cb", h)])
                    b.op("scalar", lambda E: E.activation(
                        QB[:, :], cb_[:, :], AF.Exp, scale=-1.0,
                        bias=hcolN[:, 0:1]),
                        reads=allc + ["hcolN"], writes=[K("QB")])
                    alll = [K("lfb", mo) for mo in range(DP)]
                    b.op("scalar", lambda E: E.activation(
                        lfb[:, :], cb_[:, :], AF.Exp, bias=hcolP[:, 0:1]),
                        reads=allc + ["hcolP"], writes=alll)
                    b.op("vector", lambda E: E.tensor_tensor(
                        qRb[:, :], qb[:, :], lfb[:, :], ALU.mult),
                        reads=[K("qb", h) for h in range(DP)] + alll,
                        writes=[K("qRb")])
                    b.op("vector", lambda E: E.tensor_tensor(
                        cb_[:, :], fb[:, :], QB[:, :], ALU.mult),
                        reads=[K("fb", h) for h in range(DP)] + [K("QB")],
                        writes=allc)
                    b.op("vector", lambda E: E.tensor_tensor(
                        QB[:, :], QB[:, :], cb_[:, :], ALU.subtract),
                        reads=allc + [K("QB")], writes=[K("QB")])
                    for sc in range(NSC):
                        for h in range(DP):
                            i = h % 2
                            o0 = h * CH + sc * SC
                            rmid = lfb[:, o0 + mid:o0 + mid + 1]
                            rend = lfb[:, o0 + SC - 1:o0 + SC]
                            b.op("vector", lambda E, rmid=rmid, i=i:
                                 E.reciprocal(rcol[i][:, :], rmid),
                                 reads=[K("lfb", h)], writes=[K("rcol", i)])
                            b.op("vector", lambda E, o0=o0, i=i:
                                 E.tensor_scalar(
                                     qt[i][:, :], qRb[:, o0:o0 + SC],
                                     rcol[i][:, 0:1], None, ALU.mult),
                                 reads=[K("qRb"), K("rcol", i)],
                                 writes=[K("qt", i)])
                            b.op("vector", lambda E, o0=o0, i=i, rmid=rmid:
                                 E.tensor_scalar(
                                     kt[i][:, :], QB[:, o0:o0 + SC],
                                     rmid, None, ALU.mult),
                                 reads=[K("QB"), K("lfb", h)],
                                 writes=[K("kt", i)])
                            b.op("vector", lambda E, o0=o0, i=i, rend=rend:
                                 E.tensor_scalar(
                                     kh[i][:, :], QB[:, o0:o0 + SC],
                                     rend, None, ALU.mult),
                                 reads=[K("QB"), K("lfb", h)],
                                 writes=[K("kh", i)])
                            b.op("tensor", lambda E, i=i: E.matmul(
                                glaA[i][:, 0:SC], kt[i][:, :], qt[i][:, :],
                                start=True, stop=True),
                                reads=[K("kt", i), K("qt", i)],
                                writes=[K("glaA", i)])
                            b.op("vector", lambda E, i=i: E.tensor_tensor(
                                atm[i][:, :], glaA[i][0:SC, 0:SC], mkb[:, :],
                                ALU.mult),
                                reads=[K("glaA", i), "mkb"],
                                writes=[K("atm", i)])
                            b.op("tensor", lambda E, o0=o0, i=i: E.transpose(
                                glaT[i][:, 0:128], vb[:, o0:o0 + SC],
                                idb[:, :]),
                                reads=[K("vb", h), "idb"],
                                writes=[K("glaT", i)])
                            b.op("tensor", lambda E, i=i: E.transpose(
                                glaT[i][:, 128:256], kh[i][:, :], idb[:, :]),
                                reads=[K("kh", i), "idb"],
                                writes=[K("glaT", i)])
                            b.op("scalar", lambda E, i=i: E.copy(
                                vk[i][:, :], glaT[i][0:SC, 0:256]),
                                reads=[K("glaT", i)], writes=[K("vk", i)])
                            b.op("tensor", lambda E, h=h, i=i, o0=o0:
                                 E.matmul(
                                     glaA[i][:, SC:2 * SC],
                                     Sb_[:, h * 128:(h + 1) * 128],
                                     qRb[:, o0:o0 + SC],
                                     start=True, stop=False),
                                 reads=[K("Sb_", h), K("qRb")],
                                 writes=[K("glaA", i)])
                            b.op("tensor", lambda E, i=i: E.matmul(
                                glaA[i][:, SC:2 * SC], vk[i][:, 0:128],
                                atm[i][:, :], start=False, stop=True),
                                reads=[K("vk", i), K("atm", i)],
                                writes=[K("glaA", i)])
                            b.op("scalar", lambda E, o0=o0, i=i: E.copy(
                                ob[:, o0:o0 + SC], glaA[i][:, SC:2 * SC]),
                                reads=[K("glaA", i)], writes=[K("ob", h)])
                            b.op("tensor", lambda E, i=i: E.matmul(
                                glaS[i][:, :], vk[i][:, 128:256],
                                vk[i][:, 0:128], start=True, stop=True),
                                reads=[K("vk", i)], writes=[K("glaS", i)])
                            b.op("vector", lambda E, h=h, i=i, rend=rend:
                                 E.tensor_scalar(
                                     Sb_[:, h * 128:(h + 1) * 128],
                                     Sb_[:, h * 128:(h + 1) * 128],
                                     rend, EMH, ALU.mult, ALU.mult),
                                 reads=[K("Sb_", h), K("lfb", h)],
                                 writes=[K("Sb_", h)])
                            b.op("vector", lambda E, h=h, i=i:
                                 E.scalar_tensor_tensor(
                                     Sb_[:, h * 128:(h + 1) * 128],
                                     glaS[i][:, :], EMH,
                                     Sb_[:, h * 128:(h + 1) * 128],
                                     ALU.mult, ALU.add),
                                 reads=[K("glaS", i), K("Sb_", h)],
                                 writes=[K("Sb_", h)])
                    b.op("vector", lambda E: E.tensor_tensor(
                        son[:, :], ob[:, :], ob[:, :], ALU.mult),
                        reads=[K("ob", hh) for hh in range(DP)],
                        writes=[K("son")])
                    for kpt in range(DP):
                        b.op("tensor", lambda E, kpt=kpt: E.matmul(
                            qkvP[0][0:1, 0:CH], ocb[:, :],
                            son[:, kpt * CH:(kpt + 1) * CH],
                            start=(kpt == 0), stop=(kpt == DP - 1)),
                            reads=[K("son"), "ocb"], writes=[K("qkvP", 0)])
                    b.op("scalar", lambda E: E.activation(
                        rrow[0:1, :], qkvP[0][0:1, 0:CH], AF.Ln,
                        scale=1.0 / D, bias=eps6[0:1, 0:1]),
                        reads=[K("qkvP", 0), "eps6"], writes=[K("rrow")])
                    b.op("scalar", lambda E: E.activation(
                        rrow[0:1, :], rrow[0:1, :], AF.Exp, scale=-0.5),
                        reads=[K("rrow")], writes=[K("rrow")])
                    b.op("tensor", lambda E: E.matmul(
                        qkvP[1][:, 0:CH], orf[:, :], rrow[0:1, :],
                        start=True, stop=True),
                        reads=[K("rrow"), "orf"], writes=[K("qkvP", 1)])
                    b.op("vector", lambda E: E.tensor_copy(
                        rbc[:, :], qkvP[1][:, 0:CH]),
                        reads=[K("qkvP", 1)], writes=[K("rbc")])
                    for mo in range(DP):
                        b.op("vector", lambda E, mo=mo: E.scalar_tensor_tensor(
                            son[:, mo * CH:(mo + 1) * CH],
                            ob[:, mo * CH:(mo + 1) * CH],
                            nws[:, mo:mo + 1], rbc[:, :],
                            ALU.mult, ALU.mult),
                            reads=[K("ob", mo), "nws", K("rbc")],
                            writes=[K("son")])
                    for mo in range(DP):
                        pp = (mo // 2) % 2
                        co = (mo % 2) * CH
                        for kpt in range(DP):
                            b.op("tensor", lambda E, mo=mo, kpt=kpt, pp=pp,
                                 co=co: E.matmul(
                                     qkvP[pp][:, co:co + CH],
                                     ws["wo"][:, kpt * D + mo * 128:
                                              kpt * D + (mo + 1) * 128],
                                     son[:, kpt * CH:(kpt + 1) * CH],
                                     start=(kpt == 0), stop=(kpt == DP - 1)),
                                 reads=[K("wo"), K("son")],
                                 writes=[K("qkvP", pp)])
                        if mo % 2 == 1:
                            xpair = xs[:, :].rearrange(
                                "p (k n) -> p k n", k=DP)[
                                :, mo - 1:mo + 1, t0:t0 + CH]
                            b.op("vector", lambda E, xpair=xpair, pp=pp:
                                 E.tensor_tensor(
                                     xpair, qkvP[pp][:, 0:2 * CH],
                                     xpair, ALU.add),
                                 reads=[K("qkvP", pp), xk(mo - 1, t0 // XBLK), xk(mo, t0 // XBLK)],
                                 writes=[xk(mo - 1, t0 // XBLK), xk(mo, t0 // XBLK)])
                b.barrier()

        for bb in range(B):
            b.op("sync", lambda E, bb=bb: E.dma_start(
                out=xs[:, :].rearrange("p (k n) -> p k n", k=DP),
                in_=xd.rearrange("b (k p) n -> b p k n", p=128)[bb]),
                writes=[k for pt in range(DP) for k in xk(pt)], dma=True, dsem="xload")
            ffn_phase(f"F{bb}a_")
            if not DBG["ffn_only"]:
                attn_phase(f"A{bb}_")
                ffn_phase(f"F{bb}b_")
            b.op("sync", lambda E, bb=bb: E.dma_start(
                out=outd.rearrange("b (k p) n -> b p k n", p=128)[bb],
                in_=xs[:, :].rearrange("p (k n) -> p k n", k=DP)),
                reads=[k for pt in range(DP) for k in xk(pt)], dma=True, dsem="xstore")
        b.barrier()
        b.play(top)
    return nc


def _oobs_const(cfg):
    E8 = cfg["E"]
    NG = cfg["CAPR"] // 128
    ST = 2 * E8 * NG
    a = np.arange(ST)
    epar = (a // (2 * NG)) % 2
    gg = epar * (2 * NG) + a % (2 * NG)
    ids = cfg["TILE"] + gg.astype(np.float32)
    pat = np.zeros((ST, 2), dtype=np.float32)
    pat[:, 0] = ids
    return np.tile(pat.reshape(1, ST * 2), (128, 1))


def prep_inputs(L0_attn_x, L0_attn_Wq, L0_attn_Wk, L0_attn_Wv, L0_attn_Wo,
                L0_attn_norm_w, L0_ffn_router_w, L0_ffn_w_fc, L0_ffn_w_proj,
                L0_ffn_shared_fc, L0_ffn_shared_proj, cfg, n_cores=8):
    B, N, D, DFF, E8 = cfg["B"], cfg["N"], cfg["D"], cfg["DFF"], cfg["E"]
    SC = cfg["SC"]
    cc = np.ascontiguousarray
    ascc = cc
    x = np.asarray(L0_attn_x, dtype=np.float32)
    fctT = np.concatenate(
        [np.transpose(np.asarray(L0_ffn_w_fc, np.float32), (0, 2, 1)),
         np.transpose(np.asarray(L0_ffn_shared_fc, np.float32))[None]],
        axis=0)
    pjtT = np.concatenate(
        [np.transpose(np.asarray(L0_ffn_w_proj, np.float32), (0, 2, 1)),
         np.transpose(np.asarray(L0_ffn_shared_proj, np.float32))[None]],
        axis=0)
    common = {
        "wq": cc(np.asarray(L0_attn_Wq, np.float32).T),
        "wk": cc(np.asarray(L0_attn_Wk, np.float32).T),
        "wv": cc(np.asarray(L0_attn_Wv, np.float32).T),
        "wo": cc(np.asarray(L0_attn_Wo, np.float32).T),
        "nw": cc(np.asarray(L0_attn_norm_w, np.float32).reshape(D // 128, 128).T),
        "rt": cc(np.asarray(L0_ffn_router_w, np.float32).T),
        "fct": cc(fctT),
        "pjt": cc(pjtT),
        "identb": np.eye(128, dtype=BFNP),
        "identf": np.eye(128, dtype=np.float32),
        "onescb": np.ones((128, 1), dtype=BFNP),
        "onescf": np.ones((128, 1), dtype=np.float32),
        "onesrf": np.ones((1, 128), dtype=np.float32),
        "halfrf": np.full((1, 128), 0.5, dtype=np.float32),
        "halfrb": np.full((1, 128), 0.5, dtype=BFNP),
        "maskb": np.triu(np.ones((SC, SC), dtype=np.float32)).astype(BFNP),
        "onesrb": np.ones((1, 128), dtype=BFNP),
        "ecols": np.tile(np.arange(E8, dtype=np.float32),
                         (128, cfg["TILE"] // 128)),
        "tvl": (np.arange(128, dtype=np.float32)[:, None] *
                (cfg["TILE"] // 128) +
                np.arange(cfg["TILE"] // 128, dtype=np.float32)[None, :]),
        "oobs": _oobs_const(cfg),
    }
    in_maps = []
    for c in range(n_cores):
        m = dict(common)
        m["x"] = cc(np.transpose(x[c * B:(c + 1) * B], (0, 2, 1)))
        in_maps.append(m)
    return in_maps


def kernel(**inputs):
    cfg = REAL_CFG
    n_cores = 8
    in_maps = prep_inputs(n_cores=n_cores, cfg=cfg, **inputs)
    nc = build_program(cfg)
    res = run_bass_kernel_spmd(nc, in_maps, core_ids=list(range(n_cores)))
    outs = [np.transpose(r["out"], (0, 2, 1)) for r in res.results]
    return np.concatenate(outs, axis=0)


# version 20
# speedup vs baseline: 1.1195x; 1.0061x over previous
import numpy as np
import ml_dtypes
from contextlib import ExitStack

import concourse.bass as bass
import concourse.mybir as mybir
from concourse.bass_utils import run_bass_kernel_spmd
from concourse.bass_core_rust import IndirectOffsetOnAxis

F32 = mybir.dt.float32
BF16 = mybir.dt.bfloat16
I32 = mybir.dt.int32
AF = mybir.ActivationFunctionType
ALU = mybir.AluOpType
AX = mybir.AxisListType
BFNP = ml_dtypes.bfloat16

ENGINES = ("tensor", "vector", "scalar", "gpsimd", "sync")

DBG = {"no_igs": False, "no_gx": False, "no_dsc": False, "no_acc": False,
       "ffn_only": False}

REAL_CFG = dict(B=2, N=2048, D=1024, DFF=4096, E=8,
                TILE=1024, NCH=512, CH=256, SC=128, FCC=256, CAPR=256)


class Builder:

    def __init__(self, nc, ctx):
        self.nc = nc
        self.prog = {e: [] for e in ENGINES}
        self.semh = {}
        self.semc = {}
        self.floors = {e: {} for e in ENGINES}
        self.buf = {}
        self._ctx = ctx
        for name in ["s_tensor", "s_vector", "s_scalar", "s_gpsimd", "s_sync"]:
            self.semh[name] = ctx.enter_context(nc.semaphore(name))
            self.semc[name] = 0

    def op(self, eng, emit, reads=(), writes=(), dma=False, dsem=None):
        if dma:
            assert dsem is not None
            sem = "dma_" + dsem
            if sem not in self.semh:
                self.semh[sem] = self._ctx.enter_context(self.nc.semaphore(sem))
                self.semc[sem] = 0
            inc = 16
        else:
            sem = "s_" + eng
            inc = 1
        needs = {}

        def want(mark):
            s, v = mark
            if eng == "tensor" and s == "s_tensor":
                return
            if v > needs.get(s, 0):
                needs[s] = v

        for k in reads:
            st = self.buf.get(k)
            if st and st["w"]:
                want(st["w"])
        for k in writes:
            st = self.buf.get(k)
            if st:
                if st["w"]:
                    want(st["w"])
                for r in st["r"]:
                    want(r)
        lst = self.prog[eng]
        fl = self.floors[eng]
        for s in sorted(needs):
            v = needs[s]
            if fl.get(s, 0) >= v:
                continue
            fl[s] = v
            h = self.semh[s]
            lst.append(lambda E, h=h, v=v: E.wait_ge(h, v))
        self.semc[sem] += inc
        h = self.semh[sem]
        lst.append(lambda E, h=h, emit=emit, inc=inc: emit(E).then_inc(h, inc))
        mark = (sem, self.semc[sem])
        for k in writes:
            self.buf[k] = {"w": mark, "r": []}
        for k in reads:
            if k in writes:
                continue
            st = self.buf.setdefault(k, {"w": None, "r": []})
            st["r"].append(mark)
        return mark

    def barrier(self):
        targets = [(s, c) for s, c in self.semc.items() if c > 0]
        for e in ENGINES:
            fl = self.floors[e]
            for s, v in targets:
                if fl.get(s, 0) >= v:
                    continue
                fl[s] = v
                h = self.semh[s]
                self.prog[e].append(lambda E, h=h, v=v: E.wait_ge(h, v))
        self.buf.clear()

    def play(self, ctx):
        block = ctx.enter_context(self.nc.Block())
        prog = self.prog

        @block.tensor
        def _(E):
            for f in prog["tensor"]:
                f(E)

        @block.vector
        def _(E):
            for f in prog["vector"]:
                f(E)

        @block.scalar
        def _(E):
            for f in prog["scalar"]:
                f(E)

        @block.gpsimd
        def _(E):
            for f in prog["gpsimd"]:
                f(E)

        @block.sync
        def _(E):
            for f in prog["sync"]:
                f(E)


def build_program(cfg):
    B, N, D, DFF, E8 = cfg["B"], cfg["N"], cfg["D"], cfg["DFF"], cfg["E"]
    TILE, NCH, CH, SC, FCC = (cfg["TILE"], cfg["NCH"], cfg["CH"], cfg["SC"],
                              cfg["FCC"])
    CAPR = cfg["CAPR"]
    CAP2 = 2 * CAPR
    NG = CAPR // 128
    GPE = 2 * NG
    C8 = TILE // 128
    SLOT_TOT = (cfg["E"]) * CAP2
    assert CAPR % 128 == 0 and CAP2 <= NCH
    TQ = 4 if (D // 128) % 4 == 0 else D // 128
    CQ = 4 if C8 % 4 == 0 else min(4, C8)
    DP = D // 128
    DFP = DFF // 128
    NT = N // TILE
    NSC = CH // SC
    assert TILE % NCH == 0 and N % CH == 0 and CH % SC == 0
    assert DFF % FCC == 0 and FCC % 128 == 0 and N % NCH == 0

    nc = bass.Bass("TRN2", target_bir_lowering=False)

    xd = nc.dram_tensor("x", [B, D, N], F32, kind="ExternalInput").ap()
    wqd = nc.dram_tensor("wq", [D, D], F32, kind="ExternalInput").ap()
    wkd = nc.dram_tensor("wk", [D, D], F32, kind="ExternalInput").ap()
    wvd = nc.dram_tensor("wv", [D, D], F32, kind="ExternalInput").ap()
    wod = nc.dram_tensor("wo", [D, D], F32, kind="ExternalInput").ap()
    nwd = nc.dram_tensor("nw", [128, DP], F32, kind="ExternalInput").ap()
    rtd = nc.dram_tensor("rt", [D, E8 + 1], F32, kind="ExternalInput").ap()
    fct = nc.dram_tensor("fct", [E8 + 1, D, DFF], F32, kind="ExternalInput").ap()
    pjt = nc.dram_tensor("pjt", [E8 + 1, DFF, D], F32, kind="ExternalInput").ap()
    identb = nc.dram_tensor("identb", [128, 128], BF16, kind="ExternalInput").ap()
    identf = nc.dram_tensor("identf", [128, 128], F32, kind="ExternalInput").ap()
    onescb = nc.dram_tensor("onescb", [128, 1], BF16, kind="ExternalInput").ap()
    onescf = nc.dram_tensor("onescf", [128, 1], F32, kind="ExternalInput").ap()
    onesrf = nc.dram_tensor("onesrf", [1, 128], F32, kind="ExternalInput").ap()
    halfrf = nc.dram_tensor("halfrf", [1, 128], F32, kind="ExternalInput").ap()
    halfrb = nc.dram_tensor("halfrb", [1, 128], BF16, kind="ExternalInput").ap()
    maskbd = nc.dram_tensor("maskb", [SC, SC], BF16, kind="ExternalInput").ap()
    onesrb = nc.dram_tensor("onesrb", [1, 128], BF16, kind="ExternalInput").ap()
    ecolsd = nc.dram_tensor("ecols", [128, C8 * E8], F32, kind="ExternalInput").ap()
    tvld = nc.dram_tensor("tvl", [128, C8], F32, kind="ExternalInput").ap()
    oobsd = nc.dram_tensor("oobs", [128, SLOT_TOT * 2 // 128], F32,
                           kind="ExternalInput").ap()
    outd = nc.dram_tensor("out", [B, D, N], F32, kind="ExternalOutput").ap()
    NDUMP = 4 * (CAPR // 128)
    xtmd_t = nc.dram_tensor("xtmd", [TILE + NDUMP, D], BF16, kind="Internal")
    igbd_t = nc.dram_tensor("igbd", [SLOT_TOT + 1, 2], F32, kind="Internal")
    dltA_t = nc.dram_tensor("dltA", [TILE + NDUMP, D], BF16, kind="Internal")
    dltB_t = nc.dram_tensor("dltB", [TILE + NDUMP, D], BF16, kind="Internal")
    fctb = nc.dram_tensor("fctb", [E8 + 1, DFF // FCC, 128, (D // 128) * FCC],
                          BF16, kind="Internal").ap()
    pjtb = nc.dram_tensor("pjtb", [E8 + 1, D // 128, 128, (DFF // 128) * 128],
                          BF16, kind="Internal").ap()
    awtb = nc.dram_tensor("awtb", [4, 128, (D // 128) * D], BF16,
                          kind="Internal").ap()

    top = ExitStack()
    with top:
        b = Builder(nc, top)
        sb = lambda nm, sh, dt: top.enter_context(nc.sbuf_tensor(nm, sh, dt))

        xs = sb("xs", [128, DP * N], F32)
        statA = sb("statA", [1, N], BF16)
        statB = sb("statB", [1, N], BF16)
        idb = sb("idb", [128, 128], BF16)
        idf = sb("idf", [128, 128], F32)
        ocb = sb("ocb", [128, 1], BF16)
        orf = sb("orf", [1, 128], F32)
        hrf = sb("hrf", [1, 128], F32)
        hrb = sb("hrb", [1, 128], BF16)
        mkb = sb("mkb", [SC, SC], BF16)
        nws = sb("nws", [128, DP], F32)
        rtsF = sb("rtsF", [128, DP * (E8 + 1)], F32)
        rstdc = sb("rstdc", [128, N // 128], F32)
        ocf = sb("ocf", [128, 1], F32)
        sbc = sb("sbc", [128, E8], F32)
        srow = sb("srow", [1, E8], F32)
        zcol = sb("zcol", [128, max(SC, NCH)], F32)
        orb = sb("orb", [1, 128], BF16)
        ecols = sb("ecols_s", [128, C8 * E8], F32)
        tvl = sb("tvl_s", [128, C8], F32)
        oobs = sb("oobs_s", [128, SLOT_TOT * 2 // 128], F32)
        eps5 = sb("eps5", [128, 1], F32)
        eps6 = sb("eps6", [128, 1], F32)
        hcolP = sb("hcolP", [128, 1], F32)
        hcolN = sb("hcolN", [128, 1], F32)

        XBLK = min(512, N)

        def ecast(E, out_ap, in_ap):
            if hasattr(E, "tensor_copy"):
                return E.tensor_copy(out_ap, in_ap)
            return E.copy(out_ap, in_ap)

        def xk(pt, blk=None):
            if blk is None:
                return [("x", pt, i) for i in range(N // XBLK)]
            return ("x", pt, blk)

        def xks(pt, t0, t1):
            return [("x", pt, i) for i in range(t0 // XBLK,
                                                (t1 + XBLK - 1) // XBLK)]

        def xsl(pt, t0, t1):
            return xs[:, pt * N + t0:pt * N + t1]

        b.op("sync", lambda E: E.dma_start(out=idb[:, :], in_=identb),
             writes=["idb"], dma=True, dsem="idb")
        b.op("sync", lambda E: E.dma_start(out=idf[:, :], in_=identf),
             writes=["idf"], dma=True, dsem="idf")
        b.op("sync", lambda E: E.dma_start(out=ocb[:, :], in_=onescb),
             writes=["ocb"], dma=True, dsem="ocb")
        b.op("sync", lambda E: E.dma_start(out=ocf[:, :], in_=onescf),
             writes=["ocf"], dma=True, dsem="ocf")
        b.op("sync", lambda E: E.dma_start(out=orf[:, :], in_=onesrf),
             writes=["orf"], dma=True, dsem="orf")
        b.op("sync", lambda E: E.dma_start(out=hrf[:, :], in_=halfrf),
             writes=["hrf"], dma=True, dsem="hrf")
        b.op("sync", lambda E: E.dma_start(out=hrb[:, :], in_=halfrb),
             writes=["hrb"], dma=True, dsem="hrb")
        b.op("sync", lambda E: E.dma_start(out=mkb[:, :], in_=maskbd),
             writes=["mkb"], dma=True, dsem="mkb")
        b.op("sync", lambda E: E.dma_start(out=nws[:, :], in_=nwd),
             writes=["nws"], dma=True, dsem="nws")
        b.op("sync", lambda E: E.dma_start(
            out=rtsF[:, :].rearrange("p (k e) -> p k e", k=DP),
            in_=rtd.rearrange("(k p) e -> p k e", p=128)),
            writes=["rtsF"], dma=True, dsem="rts")
        b.op("gpsimd", lambda E: E.memset(zcol[:, :], 0.0), writes=["zcol"])
        b.op("sync", lambda E: E.dma_start(out=orb[:, :], in_=onesrb),
             writes=["orb"], dma=True, dsem="orb")
        b.op("sync", lambda E: E.dma_start(out=ecols[:, :], in_=ecolsd),
             writes=["ecols"], dma=True, dsem="ecols")
        b.op("sync", lambda E: E.dma_start(out=tvl[:, :], in_=tvld),
             writes=["tvl"], dma=True, dsem="tvl")
        b.op("sync", lambda E: E.dma_start(out=oobs[:, :], in_=oobsd),
             writes=["oobs"], dma=True, dsem="oobs")
        b.op("gpsimd", lambda E: E.memset(eps5[:, :], 1e-5), writes=["eps5"])
        b.op("gpsimd", lambda E: E.memset(eps6[:, :], 1e-6), writes=["eps6"])
        b.op("gpsimd", lambda E: E.memset(hcolP[:, :], 44.36), writes=["hcolP"])
        b.op("gpsimd", lambda E: E.memset(hcolN[:, :], -44.36), writes=["hcolN"])
        ZW = max(SC, NCH)
        for gg in range(NDUMP):
            for off in range(0, D, ZW):
                w = min(ZW, D - off)
                b.op("gpsimd", lambda E, gg=gg, off=off, w=w: E.dma_start(
                    out=xtmd_t.ap()[TILE + gg:TILE + gg + 1, off:off + w],
                    in_=zcol[0:1, 0:w]),
                    reads=["zcol"], writes=[("xtmd", "dump", gg, off)],
                    dma=True, dsem="xdump")

        def ln_stats(tag, xbB, sqB, xbK, sqK, psA, psAk, psB, psBk, rowt, rowtK,
                     eps, rowt2=None, rowt2K=None, rstdP=None, rstdPk=None):
            for c in range(N // NCH):
                t0 = c * NCH
                for pt in range(DP):
                    p = pt % 2
                    xb, sq = xbB[p], sqB[p]
                    kxb, ksq = xbK[p], sqK[p]
                    b.op("vector", lambda E, xb=xb, pt=pt, t0=t0:
                         E.tensor_copy(xb[:, 0:NCH], xsl(pt, t0, t0 + NCH)),
                         reads=xks(pt, t0, t0 + NCH), writes=kxb)
                    b.op("vector", lambda E, sq=sq, xb=xb: E.tensor_tensor(
                        sq[:, 0:NCH], xb[:, 0:NCH], xb[:, 0:NCH], ALU.mult),
                        reads=kxb, writes=ksq)
                    st, sp = (pt == 0), (pt == DP - 1)
                    b.op("tensor", lambda E, xb=xb, st=st, sp=sp: E.matmul(
                        psA[0:1, 0:NCH], ocb[:, :], xb[:, 0:NCH],
                        start=st, stop=sp),
                        reads=list(kxb) + ["ocb"], writes=[psAk])
                    b.op("tensor", lambda E, sq=sq, st=st, sp=sp: E.matmul(
                        psB[0:1, 0:NCH], ocb[:, :], sq[:, 0:NCH],
                        start=st, stop=sp),
                        reads=list(ksq) + ["ocb"], writes=[psBk])
                if rowt2 is None:
                    b.op("scalar", lambda E, t0=t0: E.activation(
                        statA[0:1, t0:t0 + NCH], psA[0:1, 0:NCH], AF.Copy,
                        scale=1.0 / D),
                        reads=[psAk], writes=["statA"])
                    b.op("scalar", lambda E, t0=t0: E.activation(
                        statB[0:1, t0:t0 + NCH], psB[0:1, 0:NCH], AF.Copy,
                        scale=1.0 / D),
                        reads=[psBk], writes=["statB"])
                    b.op("vector", lambda E, t0=t0: E.tensor_tensor(
                        rowt[0:1, 0:NCH], statA[0:1, t0:t0 + NCH],
                        statA[0:1, t0:t0 + NCH], ALU.mult),
                        reads=["statA"], writes=[rowtK])
                    b.op("vector", lambda E, t0=t0: E.tensor_tensor(
                        statB[0:1, t0:t0 + NCH], statB[0:1, t0:t0 + NCH],
                        rowt[0:1, 0:NCH], ALU.subtract),
                        reads=[rowtK, "statB"], writes=["statB"])
                    b.op("scalar", lambda E, t0=t0: E.activation(
                        statB[0:1, t0:t0 + NCH], statB[0:1, t0:t0 + NCH],
                        AF.Ln, bias=eps[0:1, 0:1]),
                        reads=["statB", "eps5"], writes=["statB"])
                    b.op("scalar", lambda E, t0=t0: E.activation(
                        statB[0:1, t0:t0 + NCH], statB[0:1, t0:t0 + NCH],
                        AF.Exp, scale=-0.5),
                        reads=["statB"], writes=["statB"])
                else:
                    b.op("scalar", lambda E, t0=t0: E.activation(
                        rowt2[0:1, 0:NCH], psA[0:1, 0:NCH], AF.Copy,
                        scale=1.0 / D),
                        reads=[psAk], writes=[rowt2K])
                    b.op("scalar", lambda E, t0=t0: E.activation(
                        statA[0:1, t0:t0 + NCH], rowt2[0:1, 0:NCH], AF.Copy),
                        reads=[rowt2K], writes=["statA"])
                    b.op("scalar", lambda E, t0=t0: E.activation(
                        rowt[0:1, 0:NCH], psB[0:1, 0:NCH], AF.Copy,
                        scale=1.0 / D),
                        reads=[psBk], writes=[rowtK])
                    b.op("vector", lambda E: E.scalar_tensor_tensor(
                        rowt2[0:1, 0:NCH], rowt2[0:1, 0:NCH], -1.0,
                        rowt2[0:1, 0:NCH], ALU.mult, ALU.mult),
                        reads=[rowt2K], writes=[rowt2K])
                    b.op("vector", lambda E: E.tensor_tensor(
                        rowt[0:1, 0:NCH], rowt[0:1, 0:NCH],
                        rowt2[0:1, 0:NCH], ALU.add),
                        reads=[rowtK, rowt2K], writes=[rowtK])
                    b.op("scalar", lambda E: E.activation(
                        rowt[0:1, 0:NCH], rowt[0:1, 0:NCH], AF.Ln,
                        bias=eps[0:1, 0:1]),
                        reads=[rowtK, "eps5"], writes=[rowtK])
                    b.op("scalar", lambda E: E.activation(
                        rowt[0:1, 0:NCH], rowt[0:1, 0:NCH], AF.Exp,
                        scale=-0.5),
                        reads=[rowtK], writes=[rowtK])
                    b.op("scalar", lambda E, t0=t0: E.activation(
                        statB[0:1, t0:t0 + NCH], rowt[0:1, 0:NCH], AF.Copy),
                        reads=[rowtK], writes=["statB"])
                    for cc_ in range(NCH // 128):
                        col = (t0 // 128) + cc_
                        b.op("tensor", lambda E, cc_=cc_, col=col:
                             E.transpose(
                                 rstdP[0:128, 2 * (col % 4):
                                       2 * (col % 4) + 1],
                                 rowt[0:1, cc_ * 128:(cc_ + 1) * 128],
                                 idf[0:1, 0:1]),
                             reads=[rowtK, "idf"], writes=[rstdPk])
                        b.op("scalar", lambda E, col=col: E.copy(
                            rstdc[:, col:col + 1],
                            rstdP[0:128, 2 * (col % 4):2 * (col % 4) + 1]),
                            reads=[rstdPk], writes=["rstdc"])

        weight_cached = set()

        def ffn_phase(tag):
            fc_ctx = ExitStack()
            with fc_ctx:
                fs = lambda nm, sh, dt: fc_ctx.enter_context(
                    nc.sbuf_tensor(tag + nm, sh, dt))
                fp = lambda nm, sh, dt: fc_ctx.enter_context(
                    nc.psum_tensor(tag + nm, sh, dt))
                K = lambda *a: (tag,) + a
                half = DFP // 2

                xln = fs("xln", [128, DP * TILE], BF16)
                xtm = fs("xtm", [128, C8 * D], BF16)
                hg = fs("hg", [128, DFP * NCH], BF16)
                xgog = [fs(f"xgog{i}", [128, GPE * D], BF16)
                        for i in range(2)]
                xg = fs("xg", [128, DP * CAP2], BF16)
                ogf = xg
                fcb = [fs(f"fcb{i}", [128, DP * FCC], BF16) for i in range(2)]
                pjb = [fs(f"pjb{i}", [128, DFP * 128], BF16) for i in range(2)]
                h1 = [fs(f"h1{i}", [128, NCH], BF16) for i in range(2)]
                evt = fs("evt", [128, NCH], F32)
                evt2 = fs("evt2", [128, NCH], F32)
                SEL = [fs(f"SEL{i}", [E8, TILE], BF16) for i in range(2)]
                pos = fs("pos", [E8, TILE], F32)
                oha = [fs(f"oha{i}", [128, C8 * E8], F32) for i in range(2)]
                posm = fs("posm", [128, C8 * E8], F32)
                ea_ = [fs(f"ea{i}", [128, C8], F32) for i in range(2)]
                pa_ = [fs(f"pa{i}", [128, C8], F32) for i in range(2)]
                si_ = [fs(f"si{i}", [128, C8], I32) for i in range(2)]
                ii_ = [fs(f"ii{i}", [128, C8], I32) for i in range(4)]
                sval = [fs(f"sval{i}", [128, 2 * C8], F32) for i in range(2)]
                idg = fs("idg", [128, GPE * E8 * 2], F32)
                ido = fs("ido", [128, GPE * E8], I32)
                lgb = fs("lgb", [128, E8], F32)
                tmp8 = fs("tmp8", [128, E8], F32)
                m1b = fs("m1b", [128, 1], F32)
                m2b = fs("m2b", [128, 1], F32)
                nm2b = fs("nm2b", [128, 1], F32)

                hP = [fp(f"hP{i}", [128, NCH], F32) for i in range(2)]
                ogP = [fp(f"ogP{i}", [128, CAP2], F32) for i in range(2)]
                trP = [fp(f"trP{i}", [128, 512], BF16) for i in range(2)]
                bcP = [fp(f"bcP{i}", [128, NCH], F32) for i in range(2)]

                ln_stats(tag, [h1[0][:, 0:NCH], h1[1][:, 0:NCH]],
                         [xgog[0][:, 0:NCH], xgog[1][:, 0:NCH]],
                         [[K("h1", 0)], [K("h1", 1)]],
                         [[K("xgog", 0, 0)], [K("xgog", 1, 0)]],
                         bcP[0], K("bcP", 0), bcP[1], K("bcP", 1),
                         evt, K("evt"), eps5,
                         rowt2=evt2, rowt2K=K("evt2"),
                         rstdP=hP[0], rstdPk=K("hP", 0))

                if ("srow",) not in weight_cached:
                    weight_cached.add(("srow",))
                    for kpt in range(DP):
                        b.op("tensor", lambda E, kpt=kpt: E.matmul(
                            bcP[0][0:1, 0:E8], ocf[:, :],
                            rtsF[:, kpt * (E8 + 1):kpt * (E8 + 1) + E8],
                            start=(kpt == 0), stop=(kpt == DP - 1)),
                            reads=["ocf", "rtsF"], writes=[K("bcP", 0)])
                    b.op("scalar", lambda E: E.copy(
                        srow[0:1, :], bcP[0][0:1, 0:E8]),
                        reads=[K("bcP", 0)], writes=["srow"])
                    b.op("tensor", lambda E: E.matmul(
                        bcP[0][:, 0:E8], orf[:, :], srow[0:1, :],
                        start=True, stop=True),
                        reads=["srow", "orf"], writes=[K("bcP", 0)])
                    b.op("scalar", lambda E: E.copy(
                        sbc[:, :], bcP[0][:, 0:E8]),
                        reads=[K("bcP", 0)], writes=["sbc"])

                for par in range(2):
                    b.op("gpsimd", lambda E, par=par: E.memset(
                        xgog[par][:, :], 0.0),
                        writes=[K("xgog", par, g) for g in range(GPE)])
                for r in range(2):
                    b.op("vector", lambda E, r=r: E.tensor_copy(
                        sval[r][:, :].rearrange("p (c v) -> p c v", v=2)
                        [:, :, 0:1],
                        tvl[:, :].rearrange("p (c one) -> p c one", one=1)),
                        reads=["tvl"], writes=[K("sval", r)])

                def stream_fc(e, fcc, buf, bufk):
                    ck = ("wfc", e, fcc)
                    if ck not in weight_cached:
                        weight_cached.add(ck)
                        PW = NCH
                        kpp = max(1, PW // FCC)
                        for pc in range(DP * FCC // PW):
                            k0 = pc * kpp
                            stg = evt if pc % 2 == 0 else evt2
                            stgk = K("evt") if pc % 2 == 0 else K("evt2")
                            din = "sync" if (fcc + pc) % 2 == 0 else "gpsimd"
                            ce = ("vector", "scalar", "gpsimd")[pc % 3]
                            b.op(din, lambda E, e=e, fcc=fcc, k0=k0,
                                 kpp=kpp, stg=stg: E.dma_start(
                                     out=stg[:, :].rearrange(
                                         "p (k c) -> p k c", k=kpp),
                                     in_=fct.rearrange(
                                         "e (k p) c -> e p k c", p=128)
                                     [e, :, k0:k0 + kpp,
                                      fcc * FCC:(fcc + 1) * FCC]),
                                 reads=[stgk], writes=[stgk],
                                 dma=True, dsem="wst" + din + str(pc % 2))
                            b.op(ce, lambda E, buf=buf, pc=pc, PW=PW,
                                 stg=stg: ecast(E,
                                     buf[:, pc * PW:(pc + 1) * PW],
                                     stg[:, :]),
                                 reads=[stgk], writes=[bufk])
                        b.op("sync", lambda E, e=e, fcc=fcc, buf=buf:
                             E.dma_start(out=fctb[e, fcc], in_=buf[:, :]),
                             reads=[bufk], writes=[("fctb", e, fcc)],
                             dma=True, dsem=f"fcw{fcc % 2}")
                    else:
                        b.op("gpsimd", lambda E, e=e, fcc=fcc, buf=buf:
                             E.dma_start(out=buf[:, :], in_=fctb[e, fcc]),
                             reads=[("fctb", e, fcc)], writes=[bufk],
                             dma=True, dsem=f"fcr{fcc % 2}")

                def stream_pj(e, mo, buf, bufk, par):
                    ck = ("wpj", e, mo)
                    if ck not in weight_cached:
                        weight_cached.add(ck)
                        PW = NCH
                        kpj = PW // 128
                        for pc in range(DFP * 128 // PW):
                            k0 = pc * kpj
                            stg = evt if pc % 2 == 0 else evt2
                            stgk = K("evt") if pc % 2 == 0 else K("evt2")
                            din = "sync" if (mo + pc) % 2 == 0 else "gpsimd"
                            ce = ("vector", "scalar", "gpsimd")[pc % 3]
                            b.op(din, lambda E, e=e, mo=mo, k0=k0,
                                 kpj=kpj, stg=stg: E.dma_start(
                                     out=stg[:, :].rearrange(
                                         "p (k c) -> p k c", k=kpj),
                                     in_=pjt.rearrange(
                                         "e (k p) c -> e p k c", p=128)
                                     [e, :, k0:k0 + kpj,
                                      mo * 128:(mo + 1) * 128]),
                                 reads=[stgk], writes=[stgk],
                                 dma=True, dsem="wst" + din + str(pc % 2))
                            b.op(ce, lambda E, buf=buf, pc=pc, PW=PW,
                                 stg=stg: ecast(E,
                                     buf[:, pc * PW:(pc + 1) * PW],
                                     stg[:, :]),
                                 reads=[stgk], writes=[bufk])
                        b.op("sync", lambda E, e=e, mo=mo, buf=buf:
                             E.dma_start(
                                 out=pjtb[e, mo], in_=buf[:, :]),
                             reads=[bufk], writes=[("pjtb", e, mo)],
                             dma=True, dsem=f"pjw{par}")
                    else:
                        b.op("sync", lambda E, e=e, mo=mo, buf=buf:
                             E.dma_start(
                                 out=buf[:, :], in_=pjtb[e, mo]),
                             reads=[("pjtb", e, mo)], writes=[bufk],
                             dma=True, dsem=f"pjr{par}")

                nchn = TILE // NCH
                for tile in range(NT):
                    tt0 = tile * TILE
                    for tc in range(C8):
                        t0c = tt0 + tc * 128
                        for pt in range(DP):
                            b.op("tensor", lambda E, pt=pt, t0c=t0c:
                                 E.matmul(
                                     bcP[0][:, 0:E8 + 1],
                                     xsl(pt, t0c, t0c + 128),
                                     rtsF[:, pt * (E8 + 1):
                                          (pt + 1) * (E8 + 1)],
                                     start=(pt == 0), stop=(pt == DP - 1)),
                                 reads=xks(pt, t0c, t0c + 128) + ["rtsF"],
                                 writes=[K("bcP", 0)])
                        b.op("vector", lambda E: E.tensor_scalar(
                            m1b[:, :], bcP[0][:, E8:E8 + 1], 1.0 / D, None,
                            ALU.mult),
                            reads=[K("bcP", 0)], writes=[K("mcolt")])
                        b.op("vector", lambda E: E.tensor_scalar(
                            tmp8[:, :], sbc[:, :], m1b[:, 0:1], None,
                            ALU.mult),
                            reads=["sbc", K("mcolt")], writes=[K("tmp8")])
                        b.op("vector", lambda E: E.tensor_tensor(
                            lgb[:, :], bcP[0][:, 0:E8], tmp8[:, :],
                            ALU.subtract),
                            reads=[K("bcP", 0), K("tmp8")], writes=[K("lgb")])
                        b.op("vector", lambda E, t0c=t0c: E.tensor_scalar(
                            lgb[:, :], lgb[:, :],
                            rstdc[:, t0c // 128:t0c // 128 + 1], None,
                            ALU.mult),
                            reads=[K("lgb"), "rstdc"], writes=[K("lgb")])
                        b.op("vector", lambda E: E.reduce_max(
                            m1b[:, :], lgb[:, :], AX.X),
                            reads=[K("lgb")], writes=[K("m1b")])
                        o1 = oha[0][:, tc * E8:(tc + 1) * E8]
                        o2 = oha[1][:, tc * E8:(tc + 1) * E8]
                        b.op("vector", lambda E, o1=o1: E.tensor_scalar(
                            o1, lgb[:, :], m1b[:, 0:1], None, ALU.is_ge),
                            reads=[K("lgb"), K("m1b")], writes=[K("oha", 0)])
                        b.op("vector", lambda E, o1=o1: E.scalar_tensor_tensor(
                            tmp8[:, :], o1, -1e30, lgb[:, :],
                            ALU.mult, ALU.add),
                            reads=[K("oha", 0), K("lgb")], writes=[K("tmp8")])
                        b.op("vector", lambda E: E.reduce_max(
                            m2b[:, :], tmp8[:, :], AX.X),
                            reads=[K("tmp8")], writes=[K("m2b")])
                        b.op("vector", lambda E, o2=o2: E.tensor_scalar(
                            o2, tmp8[:, :], m2b[:, 0:1], None, ALU.is_ge),
                            reads=[K("tmp8"), K("m2b")], writes=[K("oha", 1)])
                        b.op("scalar", lambda E: E.activation(
                            nm2b[:, :], m2b[:, 0:1], AF.Copy, scale=-1.0),
                            reads=[K("m2b")], writes=[K("nm2b")])
                        b.op("scalar", lambda E, tc=tc: E.activation(
                            sval[0][:, 2 * tc + 1:2 * tc + 2], m1b[:, 0:1],
                            AF.Sigmoid, bias=nm2b[:, 0:1]),
                            reads=[K("m1b"), K("nm2b")], writes=[K("sval", 0)])
                        b.op("scalar", lambda E, tc=tc: E.activation(
                            sval[1][:, 2 * tc + 1:2 * tc + 2], m1b[:, 0:1],
                            AF.Sigmoid, bias=m2b[:, 0:1], scale=-1.0),
                            reads=[K("m1b"), K("m2b")], writes=[K("sval", 1)])
                        for r in range(2):
                            oo = oha[r][:, tc * E8:(tc + 1) * E8]
                            b.op("tensor", lambda E, oo=oo: E.transpose(
                                bcP[1][0:E8, 0:128], oo, idf[:, :]),
                                reads=[K("oha", r), "idf"],
                                writes=[K("bcP", 1)])
                            b.op("scalar", lambda E, r=r, tc=tc: E.copy(
                                SEL[r][:, tc * 128:(tc + 1) * 128],
                                bcP[1][0:E8, 0:128]),
                                reads=[K("bcP", 1)], writes=[K("SEL", r)])

                    for c in range(nchn):
                        t0 = tt0 + c * NCH
                        b.op("tensor", lambda E, t0=t0: E.matmul(
                            bcP[0][:, :], orb[:, :], statA[0:1, t0:t0 + NCH],
                            start=True, stop=True),
                            reads=["statA", "orb"], writes=[K("bcP", 0)])
                        b.op("tensor", lambda E, t0=t0: E.matmul(
                            bcP[1][:, :], orb[:, :], statB[0:1, t0:t0 + NCH],
                            start=True, stop=True),
                            reads=["statB", "orb"], writes=[K("bcP", 1)])
                        for pt in range(DP):
                            b.op("vector", lambda E, pt=pt, t0=t0:
                                 E.tensor_tensor(
                                     evt[:, :], xsl(pt, t0, t0 + NCH),
                                     bcP[0][:, :], ALU.subtract),
                                 reads=xks(pt, t0, t0 + NCH) + [K("bcP", 0)],
                                 writes=[K("evt")])
                            b.op("vector", lambda E, pt=pt, c=c:
                                 E.tensor_tensor(
                                     xln[:, pt * TILE + c * NCH:
                                         pt * TILE + (c + 1) * NCH],
                                     evt[:, :], bcP[1][:, :], ALU.mult),
                                 reads=[K("evt"), K("bcP", 1)],
                                 writes=[K("xln", pt)])

                    for c in range(C8):
                        for q in range(DP // TQ):
                            par = (c * (DP // TQ) + q) % 2
                            for j in range(TQ):
                                pt = q * TQ + j
                                b.op("tensor", lambda E, c=c, pt=pt, j=j,
                                     par=par: E.transpose(
                                         trP[par][:, j * 128:(j + 1) * 128],
                                         xln[:, pt * TILE + c * 128:
                                             pt * TILE + (c + 1) * 128],
                                         idb[:, :]),
                                     reads=[K("xln", pt), "idb"],
                                     writes=[K("trP", par)])
                            b.op("scalar", lambda E, c=c, q=q, par=par:
                                 E.copy(xtm[:, c * D + q * TQ * 128:
                                            c * D + (q + 1) * TQ * 128],
                                        trP[par][:, 0:TQ * 128]),
                                 reads=[K("trP", par)], writes=[K("xtm")])
                    b.op("sync", lambda E: E.dma_start(
                        out=xtmd_t.ap()[0:TILE, :].rearrange("(p c) d -> p c d", p=128),
                        in_=xtm[:, :].rearrange("p (c d) -> p c d", c=C8)),
                        reads=[K("xtm")], writes=[("xtmd",)], dma=True,
                        dsem="xtmd")

                    for r in range(2):
                        for c in range(TILE // NCH):
                            init = (0.0 if c == 0 else
                                    pos[:, c * NCH - 1:c * NCH])
                            b.op("vector", lambda E, c=c, r=r, init=init:
                                 E.tensor_tensor_scan(
                                     pos[0:E8, c * NCH:(c + 1) * NCH],
                                     SEL[r][:, c * NCH:(c + 1) * NCH],
                                     zcol[0:E8, 0:NCH], init,
                                     ALU.add, ALU.add),
                                 reads=[K("SEL", r), "zcol", K("pos")],
                                 writes=[K("pos")])
                        for tc in range(C8):
                            b.op("tensor", lambda E, tc=tc: E.transpose(
                                bcP[1][0:128, 0:E8],
                                pos[:, tc * 128:(tc + 1) * 128],
                                idf[0:E8, 0:E8]),
                                reads=[K("pos"), "idf"], writes=[K("bcP", 1)])
                            b.op("scalar", lambda E, tc=tc: E.copy(
                                posm[:, tc * E8:(tc + 1) * E8],
                                bcP[1][0:128, 0:E8]),
                                reads=[K("bcP", 1)], writes=[K("posm")])
                        b.op("vector", lambda E, r=r: E.tensor_tensor(
                            posm[:, :], posm[:, :], oha[r][:, :], ALU.mult),
                            reads=[K("posm"), K("oha", r)], writes=[K("posm")])
                        b.op("vector", lambda E, r=r: E.tensor_reduce(
                            pa_[r][:, :].rearrange("p (c one) -> p c one", one=1),
                            posm[:, :].rearrange("p (c e) -> p c e", e=E8),
                            AX.X, ALU.add),
                            reads=[K("posm")], writes=[K("pa", r)])
                        b.op("vector", lambda E, r=r: E.tensor_scalar(
                            pa_[r][:, :], pa_[r][:, :], -1.0, None, ALU.add),
                            reads=[K("pa", r)], writes=[K("pa", r)])
                        b.op("vector", lambda E, r=r: E.tensor_tensor(
                            posm[:, :], ecols[:, :], oha[r][:, :], ALU.mult),
                            reads=["ecols", K("oha", r), K("posm")],
                            writes=[K("posm")])
                        b.op("vector", lambda E, r=r: E.tensor_reduce(
                            ea_[r][:, :].rearrange("p (c one) -> p c one", one=1),
                            posm[:, :].rearrange("p (c e) -> p c e", e=E8),
                            AX.X, ALU.add),
                            reads=[K("posm")], writes=[K("ea", r)])
                        ST = 2 * E8 * NG
                        b.op("vector", lambda E, r=r: E.tensor_copy(
                            ii_[0][:, :], pa_[r][:, :]),
                            reads=[K("pa", r)], writes=[K("ii", 0)])
                        b.op("vector", lambda E: E.tensor_scalar(
                            ii_[1][:, :], ii_[0][:, :], 7, None,
                            ALU.arith_shift_right),
                            reads=[K("ii", 0)], writes=[K("ii", 1)])
                        b.op("vector", lambda E: E.tensor_scalar(
                            ii_[0][:, :], ii_[0][:, :], 127, None,
                            ALU.bitwise_and),
                            reads=[K("ii", 0)], writes=[K("ii", 0)])
                        b.op("vector", lambda E, r=r: E.tensor_copy(
                            ii_[2][:, :], ea_[r][:, :]),
                            reads=[K("ea", r)], writes=[K("ii", 2)])
                        b.op("vector", lambda E, r=r: E.tensor_scalar(
                            ii_[2][:, :], ii_[2][:, :], 2 * NG, r * NG,
                            ALU.mult, ALU.add),
                            reads=[K("ii", 2)], writes=[K("ii", 2)])
                        b.op("vector", lambda E: E.tensor_tensor(
                            ii_[2][:, :], ii_[2][:, :], ii_[1][:, :],
                            ALU.add),
                            reads=[K("ii", 2), K("ii", 1)], writes=[K("ii", 2)])
                        b.op("vector", lambda E, ST=ST: E.tensor_scalar(
                            ii_[0][:, :], ii_[0][:, :], ST, None, ALU.mult),
                            reads=[K("ii", 0)], writes=[K("ii", 0)])
                        b.op("vector", lambda E: E.tensor_tensor(
                            ii_[2][:, :], ii_[2][:, :], ii_[0][:, :],
                            ALU.add),
                            reads=[K("ii", 2), K("ii", 0)], writes=[K("ii", 2)])
                        b.op("vector", lambda E, r=r: E.tensor_scalar(
                            pa_[r][:, :], pa_[r][:, :], float(CAPR), None,
                            ALU.is_ge),
                            reads=[K("pa", r)], writes=[K("pa", r)])
                        b.op("vector", lambda E, r=r: E.tensor_copy(
                            ii_[3][:, :], pa_[r][:, :]),
                            reads=[K("pa", r)], writes=[K("ii", 3)])
                        b.op("vector", lambda E: E.tensor_scalar(
                            ii_[3][:, :], ii_[3][:, :], SLOT_TOT, None,
                            ALU.mult),
                            reads=[K("ii", 3)], writes=[K("ii", 3)])
                        b.op("vector", lambda E: E.tensor_tensor(
                            ii_[2][:, :], ii_[2][:, :], ii_[3][:, :],
                            ALU.add),
                            reads=[K("ii", 2), K("ii", 3)], writes=[K("ii", 2)])
                        b.op("vector", lambda E, r=r: E.tensor_scalar_min(
                            si_[r][:, :], ii_[2][:, :], SLOT_TOT),
                            reads=[K("ii", 2)], writes=[K("si", r)])

                    b.op("sync", lambda E: E.dma_start(
                        out=igbd_t.ap()[0:SLOT_TOT, :].rearrange("(p a) v -> p (a v)", p=128),
                        in_=oobs[:, :]),
                        reads=["oobs"], writes=[("igbd",)], dma=True,
                        dsem="igi")
                    for tc in range(C8):
                        for r in range(2):
                            if DBG["no_igs"]:
                                continue
                            b.op("gpsimd", lambda E, tc=tc, r=r:
                                 E.indirect_dma_start(
                                     igbd_t.ap(), IndirectOffsetOnAxis(
                                         si_[r][:, tc:tc + 1], 0),
                                     sval[r][:, 2 * tc:2 * tc + 2], None),
                                 reads=[("igbd",), K("si", r), K("sval", r)],
                                 dma=True, dsem=f"igs{r}")
                    b.op("sync", lambda E: E.dma_start(
                        out=idg[:, :].rearrange("p (a v) -> p a v", v=2),
                        in_=igbd_t.ap()[0:SLOT_TOT, :].rearrange("(p a) v -> p a v", p=128)),
                        reads=[("igbd",)], writes=[("igbd",), K("idg")],
                        dma=True, dsem="igr")
                    b.op("vector", lambda E: E.tensor_copy(
                        ido[:, :].rearrange("p (a one) -> p a one", one=1),
                        idg[:, :].rearrange("p (a v) -> p a v", v=2)
                        [:, :, 0:1]),
                        reads=[K("idg")], writes=[K("ido")])

                    e = E8
                    nfc = DFF // FCC
                    mpc = FCC // 128
                    for c2 in range(nchn):
                        for fcc in range(nfc):
                            fb_ = fcb[fcc % 2]
                            fbk = K("fcb", fcc % 2)
                            stream_fc(e, fcc, fb_, fbk)
                            for m in range(mpc):
                                mpt = fcc * mpc + m
                                pp = mpt % 2
                                hPk = K("hP", pp)
                                for kpt in range(DP):
                                    b.op("tensor", lambda E, fb_=fb_, m=m,
                                         c2=c2, kpt=kpt, pp=pp: E.matmul(
                                             hP[pp][:, :],
                                             fb_[:, kpt * FCC + m * 128:
                                                 kpt * FCC + (m + 1) * 128],
                                             xln[:, kpt * TILE + c2 * NCH:
                                                 kpt * TILE + (c2 + 1) * NCH],
                                             start=(kpt == 0),
                                             stop=(kpt == DP - 1)),
                                         reads=[fbk, K("xln", kpt)],
                                         writes=[hPk])
                                hb = h1[pp]
                                b.op("scalar", lambda E, hb=hb, pp=pp:
                                     E.activation(hb[:, 0:NCH],
                                                  hP[pp][:, :], AF.Relu),
                                     reads=[hPk], writes=[K("h1", pp)])
                                b.op("vector", lambda E, hb=hb, mpt=mpt:
                                     E.tensor_tensor(
                                         hg[:, mpt * NCH:(mpt + 1) * NCH],
                                         hb[:, 0:NCH], hb[:, 0:NCH],
                                         ALU.mult),
                                     reads=[K("h1", pp)],
                                     writes=[K("hg", mpt)])
                        for mo in range(DP):
                            pp = mo % 2
                            ogPk = K("ogP", pp)
                            pb_ = pjb[pp]
                            pbk = K("pjb", pp)
                            stream_pj(e, mo, pb_, pbk, pp)
                            for kk in range(DFP):
                                b.op("tensor", lambda E, pb_=pb_, kk=kk,
                                     pp=pp: E.matmul(
                                         ogP[pp][:, 0:NCH],
                                         pb_[:, kk * 128:(kk + 1) * 128],
                                         hg[:, kk * NCH:(kk + 1) * NCH],
                                         start=(kk == 0),
                                         stop=(kk == DFP - 1)),
                                     reads=[pbk, K("hg", kk)],
                                     writes=[ogPk])
                            t0 = tt0 + c2 * NCH
                            b.op("vector", lambda E, mo=mo, t0=t0, pp=pp:
                                 E.scalar_tensor_tensor(
                                     xsl(mo, t0, t0 + NCH),
                                     ogP[pp][:, 0:NCH], 0.5,
                                     xsl(mo, t0, t0 + NCH),
                                     ALU.mult, ALU.add),
                                 reads=[ogPk, xk(mo, t0 // XBLK)], writes=[xk(mo, t0 // XBLK)])

                    for e in range(E8):
                        epar = e % 2
                        for g in range(GPE):
                            if DBG["no_gx"]:
                                continue
                            b.op("gpsimd", lambda E, e=e, g=g, epar=epar:
                                 E.indirect_dma_start(
                                     xgog[epar][:, g * D:(g + 1) * D], None,
                                     xtmd_t.ap(), IndirectOffsetOnAxis(
                                         ido[:, e * GPE + g:
                                             e * GPE + g + 1], 0)),
                                 reads=[("xtmd",), K("ido"),
                                        K("xgog", epar, g)] +
                                       [("xtmd", "dump", gg, off)
                                        for gg in range(NDUMP)
                                        for off in range(0, D,
                                                         max(SC, NCH))],
                                 writes=[K("xgog", epar, g)], dma=True,
                                 dsem=f"gx{epar}{g}")
                        for g in range(GPE):
                            for q in range(DP // TQ):
                                par = (g * (DP // TQ) + q) % 2
                                for j in range(TQ):
                                    pt = q * TQ + j
                                    b.op("tensor", lambda E, g=g, pt=pt, j=j,
                                         par=par, epar=epar: E.transpose(
                                             trP[par][:, j * 128:
                                                      (j + 1) * 128],
                                             xgog[epar][:, g * D + pt * 128:
                                                  g * D + (pt + 1) * 128],
                                             idb[:, :]),
                                         reads=[K("xgog", epar, g), "idb"],
                                         writes=[K("trP", par)])
                                b.op("scalar", lambda E, g=g, q=q, par=par:
                                     E.copy(
                                         xg[:, :].rearrange(
                                             "p (k s) -> p k s", k=DP)
                                         [:, q * TQ:(q + 1) * TQ,
                                          g * 128:(g + 1) * 128],
                                         trP[par][:, 0:TQ * 128].rearrange(
                                             "p (k s) -> p k s", k=TQ)),
                                     reads=[K("trP", par)],
                                     writes=[K("xg")])
                        for fcc in range(nfc):
                            fb_ = fcb[fcc % 2]
                            fbk = K("fcb", fcc % 2)
                            stream_fc(e, fcc, fb_, fbk)
                            for m in range(mpc):
                                mpt = fcc * mpc + m
                                pp = mpt % 2
                                hPk = K("hP", pp)
                                for kpt in range(DP):
                                    b.op("tensor", lambda E, fb_=fb_, m=m,
                                         kpt=kpt, pp=pp: E.matmul(
                                             hP[pp][:, 0:CAP2],
                                             fb_[:, kpt * FCC + m * 128:
                                                 kpt * FCC + (m + 1) * 128],
                                             xg[:, kpt * CAP2:
                                                (kpt + 1) * CAP2],
                                             start=(kpt == 0),
                                             stop=(kpt == DP - 1)),
                                         reads=[fbk, K("xg")],
                                         writes=[hPk])
                                hb = h1[pp]
                                b.op("scalar", lambda E, hb=hb, pp=pp:
                                     E.activation(hb[:, 0:CAP2],
                                                  hP[pp][:, 0:CAP2], AF.Relu),
                                     reads=[hPk], writes=[K("h1", pp)])
                                b.op("vector", lambda E, hb=hb, mpt=mpt:
                                     E.tensor_tensor(
                                         hg[:, mpt * NCH:mpt * NCH + CAP2],
                                         hb[:, 0:CAP2], hb[:, 0:CAP2],
                                         ALU.mult),
                                     reads=[K("h1", pp)],
                                     writes=[K("hg", mpt)])
                        for mo in range(DP):
                            pp = mo % 2
                            ogPk = K("ogP", pp)
                            pb_ = pjb[pp]
                            pbk = K("pjb", pp)
                            stream_pj(e, mo, pb_, pbk, pp)
                            for kk in range(DFP):
                                b.op("tensor", lambda E, pb_=pb_, kk=kk,
                                     pp=pp: E.matmul(
                                         ogP[pp][:, 0:CAP2],
                                         pb_[:, kk * 128:(kk + 1) * 128],
                                         hg[:, kk * NCH:kk * NCH + CAP2],
                                         start=(kk == 0),
                                         stop=(kk == DFP - 1)),
                                     reads=[pbk, K("hg", kk)],
                                     writes=[ogPk])
                            b.op("scalar", lambda E, mo=mo, pp=pp: E.copy(
                                ogf[:, mo * CAP2:(mo + 1) * CAP2],
                                ogP[pp][:, 0:CAP2]),
                                reads=[ogPk], writes=[K("xg")])
                        for g in range(GPE):
                            for q in range(DP // TQ):
                                par = (g * (DP // TQ) + q) % 2
                                for j in range(TQ):
                                    pt = q * TQ + j
                                    b.op("tensor", lambda E, g=g, pt=pt, j=j,
                                         par=par: E.transpose(
                                             trP[par][:, j * 128:
                                                      (j + 1) * 128],
                                             ogf[:, pt * CAP2 + g * 128:
                                                 pt * CAP2 + (g + 1) * 128],
                                             idb[:, :]),
                                         reads=[K("xg"), "idb"],
                                         writes=[K("trP", par)])
                                gc = idg[:, (e * GPE + g) * 2 + 1:
                                         (e * GPE + g) * 2 + 2]
                                b.op("vector", lambda E, g=g, q=q, par=par,
                                     gc=gc, epar=epar: E.tensor_scalar(
                                         xgog[epar][:, g * D + q * TQ * 128:
                                              g * D + (q + 1) * TQ * 128],
                                         trP[par][:, 0:TQ * 128],
                                         gc[:, 0:1], None, ALU.mult),
                                     reads=[K("trP", par), K("idg")],
                                     writes=[K("xgog", epar, g)])
                        for g in range(GPE):
                            if DBG["no_dsc"]:
                                continue
                            dst = dltA_t if g < NG else dltB_t
                            dk = ("dltA",) if g < NG else ("dltB",)
                            b.op("gpsimd", lambda E, e=e, g=g, dst=dst,
                                 epar=epar: E.indirect_dma_start(
                                     dst.ap(), IndirectOffsetOnAxis(
                                         ido[:, e * GPE + g:
                                             e * GPE + g + 1], 0),
                                     xgog[epar][:, g * D:(g + 1) * D], None),
                                 reads=[dk, K("ido"), K("xgog", epar, g)],
                                 dma=True, dsem=f"dsc{epar}{g}")

                    for dt_, dkk, dsm in ((dltA_t, ("dltA",), "dra"),
                                          (dltB_t, ("dltB",), "drb")):
                        b.op("sync", lambda E, dt_=dt_: E.dma_start(
                            out=xtm[:, :].rearrange("p (c d) -> p c d", c=C8),
                            in_=dt_.ap()[0:TILE, :].rearrange(
                                "(p c) d -> p c d", p=128)),
                            reads=[dkk], writes=[dkk, K("xtm")],
                            dma=True, dsem=dsm)
                        for pt in range(DP):
                            for cq in range(C8 // CQ):
                                par = (pt * (C8 // CQ) + cq) % 2
                                for j in range(CQ):
                                    c = cq * CQ + j
                                    b.op("tensor", lambda E, c=c, pt=pt, j=j,
                                         par=par: E.transpose(
                                             trP[par][:, j * 128:
                                                      (j + 1) * 128],
                                             xtm[:, c * D + pt * 128:
                                                 c * D + (pt + 1) * 128],
                                             idb[:, :]),
                                         reads=[K("xtm"), "idb"],
                                         writes=[K("trP", par)])
                                t0 = tt0 + cq * CQ * 128
                                b.op("vector", lambda E, pt=pt, t0=t0,
                                     par=par: E.scalar_tensor_tensor(
                                         xsl(pt, t0, t0 + CQ * 128),
                                         trP[par][:, 0:CQ * 128], 0.5,
                                         xsl(pt, t0, t0 + CQ * 128),
                                         ALU.mult, ALU.add),
                                     reads=[K("trP", par),
                                            xk(pt, t0 // XBLK)],
                                     writes=[xk(pt, t0 // XBLK)])
                b.barrier()

        def attn_phase(tag):
            at_ctx = ExitStack()
            with at_ctx:
                fs = lambda nm, sh, dt: at_ctx.enter_context(
                    nc.sbuf_tensor(tag + nm, sh, dt))
                fp = lambda nm, sh, dt: at_ctx.enter_context(
                    nc.psum_tensor(tag + nm, sh, dt))
                K = lambda *a: (tag,) + a

                ws = {}
                for wi, (nm, dram) in enumerate([("wq", wqd), ("wk", wkd),
                                                 ("wv", wvd), ("wo", wod)]):
                    ws[nm] = fs(nm, [128, DP * D], BF16)
                    ck = ("watt", nm)
                    if ck not in weight_cached:
                        weight_cached.add(ck)
                        PW = NCH
                        mp = max(1, D // PW)
                        for pc in range(DP * D // PW):
                            kk_, mo_ = pc // mp, (pc % mp) * PW
                            ce = ("vector", "scalar", "gpsimd")[pc % 3]
                            b.op("sync", lambda E, dram=dram, kk_=kk_,
                                 mo_=mo_, PW=PW: E.dma_start(
                                     out=aevt[:, 0:PW],
                                     in_=dram.rearrange(
                                         "(k p) m -> p k m", p=128)
                                     [:, kk_, mo_:mo_ + PW]),
                                 reads=[K("aevt")], writes=[K("aevt")],
                                 dma=True, dsem="wst")
                            b.op(ce, lambda E, s=ws[nm], pc=pc, PW=PW:
                                 ecast(E,
                                     s[:, pc * PW:(pc + 1) * PW],
                                     aevt[:, 0:PW]),
                                 reads=[K("aevt")], writes=[K(nm)])
                        b.op("sync", lambda E, s=ws[nm], wi=wi: E.dma_start(
                            out=awtb[wi], in_=s[:, :]),
                            reads=[K(nm)], writes=[("awtb", wi)],
                            dma=True, dsem="wsw")
                    else:
                        b.op("sync", lambda E, s=ws[nm], wi=wi: E.dma_start(
                            out=s[:, :], in_=awtb[wi]),
                            reads=[("awtb", wi)], writes=[K(nm)],
                            dma=True, dsem="w_" + nm)

                xlnc = fs("xlnc", [128, DP * CH], BF16)
                bcMc = fs("bcMc", [128, CH], F32)
                bcRc = fs("bcRc", [128, CH], F32)
                qb = fs("qb", [128, DP * CH], BF16)
                fb = fs("fb", [128, DP * CH], BF16)
                vb = fs("vb", [128, DP * CH], BF16)
                ob = fs("ob", [128, DP * CH], BF16)
                lfb = fs("lfb", [128, DP * CH], F32)
                cb_ = fs("cb_", [128, DP * CH], F32)
                QB = fs("QB", [128, DP * CH], BF16)
                qRb = fs("qRb", [128, DP * CH], BF16)
                son = fs("son", [128, DP * CH], BF16)
                rbc = fs("rbc", [128, CH], F32)
                rrow = fs("rrow", [1, CH], F32)
                Sb_ = fs("Sb_", [128, DP * 128], BF16)
                qt = [fs(f"qt{i}", [128, SC], BF16) for i in range(2)]
                kt = [fs(f"kt{i}", [128, SC], BF16) for i in range(2)]
                kh = [fs(f"kh{i}", [128, SC], BF16) for i in range(2)]
                atm = [fs(f"atm{i}", [SC, SC], BF16) for i in range(2)]
                vk = [fs(f"vk{i}", [SC, 256], BF16) for i in range(2)]
                rcol = [fs(f"rcol{i}", [128, 1], F32) for i in range(2)]
                aevt = fs("aevt", [128, NCH], F32)

                qkvP = [fp(f"qkvP{i}", [128, max(CH, NCH)], F32)
                        for i in range(2)]
                glaA = [fp(f"glaA{i}", [128, 2 * SC], F32) for i in range(2)]
                glaT = [fp(f"glaT{i}", [SC, 2 * 128], BF16) for i in range(2)]
                glaS = [fp(f"glaS{i}", [128, 128], F32) for i in range(2)]

                nks = (NCH + CH - 1) // CH
                ln_stats(tag, [qb[:, 0:NCH], vb[:, 0:NCH]],
                         [fb[:, 0:NCH], ob[:, 0:NCH]],
                         [[K("qb", i) for i in range(nks)],
                          [K("vb", i) for i in range(nks)]],
                         [[K("fb", i) for i in range(nks)],
                          [K("ob", i) for i in range(nks)]],
                         qkvP[0], K("qkvP", 0), qkvP[1], K("qkvP", 1),
                         aevt, K("aevt"), eps5)

                b.op("gpsimd", lambda E: E.memset(Sb_[:, :], 0.0),
                     writes=[K("Sb_", h) for h in range(DP)])

                for ch in range(N // CH):
                    t0 = ch * CH
                    b.op("tensor", lambda E, t0=t0: E.matmul(
                        qkvP[0][:, 0:CH], orb[:, :], statA[0:1, t0:t0 + CH],
                        start=True, stop=True),
                        reads=["statA", "orb"], writes=[K("qkvP", 0)])
                    b.op("tensor", lambda E, t0=t0: E.matmul(
                        qkvP[1][:, 0:CH], orb[:, :], statB[0:1, t0:t0 + CH],
                        start=True, stop=True),
                        reads=["statB", "orb"], writes=[K("qkvP", 1)])
                    b.op("vector", lambda E: E.tensor_copy(
                        bcMc[:, :], qkvP[0][:, 0:CH]),
                        reads=[K("qkvP", 0)], writes=[K("bcMc")])
                    b.op("vector", lambda E: E.tensor_copy(
                        bcRc[:, :], qkvP[1][:, 0:CH]),
                        reads=[K("qkvP", 1)], writes=[K("bcRc")])
                    for pt in range(DP):
                        b.op("vector", lambda E, pt=pt, t0=t0:
                             E.tensor_tensor(aevt[:, 0:CH],
                                             xsl(pt, t0, t0 + CH),
                                             bcMc[:, :], ALU.subtract),
                             reads=xks(pt, t0, t0 + CH) + [K("bcMc")], writes=[K("aevt")])
                        b.op("vector", lambda E, pt=pt: E.tensor_tensor(
                            xlnc[:, pt * CH:(pt + 1) * CH], aevt[:, 0:CH],
                            bcRc[:, :], ALU.mult),
                            reads=[K("aevt"), K("bcRc")],
                            writes=[K("xlnc", pt)])
                    for nm, dst, dstk, act in [("wq", qb, "qb", AF.Silu),
                                               ("wk", fb, "fb", AF.Sigmoid),
                                               ("wv", vb, "vb", AF.Copy)]:
                        w = ws[nm]
                        for mo in range(DP):
                            pp = (mo // 2) % 2
                            co = (mo % 2) * CH
                            for kpt in range(DP):
                                b.op("tensor", lambda E, w=w, mo=mo, kpt=kpt,
                                     pp=pp, co=co: E.matmul(
                                         qkvP[pp][:, co:co + CH],
                                         w[:, kpt * D + mo * 128:
                                           kpt * D + (mo + 1) * 128],
                                         xlnc[:, kpt * CH:(kpt + 1) * CH],
                                         start=(kpt == 0),
                                         stop=(kpt == DP - 1)),
                                     reads=[K(nm), K("xlnc", kpt)],
                                     writes=[K("qkvP", pp)])
                            if mo % 2 == 1:
                                b.op("scalar", lambda E, dst=dst, mo=mo,
                                     pp=pp, act=act: E.activation(
                                         dst[:, (mo - 1) * CH:(mo + 1) * CH],
                                         qkvP[pp][:, 0:2 * CH], act),
                                     reads=[K("qkvP", pp)],
                                     writes=[K(dstk, mo - 1), K(dstk, mo)])
                    allf = [K("fb", mo) for mo in range(DP)]
                    alll = [K("lfb", mo) for mo in range(DP)]
                    b.op("scalar", lambda E: E.activation(
                        lfb[:, :], fb[:, :], AF.Sigmoid),
                        reads=allf, writes=alll)
                    b.op("scalar", lambda E: E.activation(
                        lfb[:, :], lfb[:, :], AF.Ln),
                        reads=alll, writes=alll)
                    mid = SC // 2
                    EMH = float(np.exp(-44.36))
                    allc = [K("cb", h) for h in range(DP)]
                    for h in range(DP):
                        for sc in range(NSC):
                            o0 = h * CH + sc * SC
                            b.op("vector", lambda E, o0=o0:
                                 E.tensor_tensor_scan(
                                     cb_[:, o0:o0 + SC], lfb[:, o0:o0 + SC],
                                     zcol[:, 0:SC], 0.0, ALU.add, ALU.add),
                                 reads=[K("lfb", h), "zcol"],
                                 writes=[K("cb", h)])
                    b.op("scalar", lambda E: E.activation(
                        QB[:, :], cb_[:, :], AF.Exp, scale=-1.0,
                        bias=hcolN[:, 0:1]),
                        reads=allc + ["hcolN"], writes=[K("QB")])
                    alll = [K("lfb", mo) for mo in range(DP)]
                    b.op("scalar", lambda E: E.activation(
                        lfb[:, :], cb_[:, :], AF.Exp, bias=hcolP[:, 0:1]),
                        reads=allc + ["hcolP"], writes=alll)
                    b.op("vector", lambda E: E.tensor_tensor(
                        qRb[:, :], qb[:, :], lfb[:, :], ALU.mult),
                        reads=[K("qb", h) for h in range(DP)] + alll,
                        writes=[K("qRb")])
                    b.op("vector", lambda E: E.tensor_tensor(
                        cb_[:, :], fb[:, :], QB[:, :], ALU.mult),
                        reads=[K("fb", h) for h in range(DP)] + [K("QB")],
                        writes=allc)
                    b.op("vector", lambda E: E.tensor_tensor(
                        QB[:, :], QB[:, :], cb_[:, :], ALU.subtract),
                        reads=allc + [K("QB")], writes=[K("QB")])
                    for sc in range(NSC):
                        for h in range(DP):
                            i = h % 2
                            o0 = h * CH + sc * SC
                            rmid = lfb[:, o0 + mid:o0 + mid + 1]
                            rend = lfb[:, o0 + SC - 1:o0 + SC]
                            b.op("vector", lambda E, rmid=rmid, i=i:
                                 E.reciprocal(rcol[i][:, :], rmid),
                                 reads=[K("lfb", h)], writes=[K("rcol", i)])
                            b.op("vector", lambda E, o0=o0, i=i:
                                 E.tensor_scalar(
                                     qt[i][:, :], qRb[:, o0:o0 + SC],
                                     rcol[i][:, 0:1], None, ALU.mult),
                                 reads=[K("qRb"), K("rcol", i)],
                                 writes=[K("qt", i)])
                            b.op("vector", lambda E, o0=o0, i=i, rmid=rmid:
                                 E.tensor_scalar(
                                     kt[i][:, :], QB[:, o0:o0 + SC],
                                     rmid, None, ALU.mult),
                                 reads=[K("QB"), K("lfb", h)],
                                 writes=[K("kt", i)])
                            b.op("vector", lambda E, o0=o0, i=i, rend=rend:
                                 E.tensor_scalar(
                                     kh[i][:, :], QB[:, o0:o0 + SC],
                                     rend, None, ALU.mult),
                                 reads=[K("QB"), K("lfb", h)],
                                 writes=[K("kh", i)])
                            b.op("tensor", lambda E, i=i: E.matmul(
                                glaA[i][:, 0:SC], kt[i][:, :], qt[i][:, :],
                                start=True, stop=True),
                                reads=[K("kt", i), K("qt", i)],
                                writes=[K("glaA", i)])
                            b.op("vector", lambda E, i=i: E.tensor_tensor(
                                atm[i][:, :], glaA[i][0:SC, 0:SC], mkb[:, :],
                                ALU.mult),
                                reads=[K("glaA", i), "mkb"],
                                writes=[K("atm", i)])
                            b.op("tensor", lambda E, o0=o0, i=i: E.transpose(
                                glaT[i][:, 0:128], vb[:, o0:o0 + SC],
                                idb[:, :]),
                                reads=[K("vb", h), "idb"],
                                writes=[K("glaT", i)])
                            b.op("tensor", lambda E, i=i: E.transpose(
                                glaT[i][:, 128:256], kh[i][:, :], idb[:, :]),
                                reads=[K("kh", i), "idb"],
                                writes=[K("glaT", i)])
                            b.op("scalar", lambda E, i=i: E.copy(
                                vk[i][:, :], glaT[i][0:SC, 0:256]),
                                reads=[K("glaT", i)], writes=[K("vk", i)])
                            b.op("tensor", lambda E, h=h, i=i, o0=o0:
                                 E.matmul(
                                     glaA[i][:, SC:2 * SC],
                                     Sb_[:, h * 128:(h + 1) * 128],
                                     qRb[:, o0:o0 + SC],
                                     start=True, stop=False),
                                 reads=[K("Sb_", h), K("qRb")],
                                 writes=[K("glaA", i)])
                            b.op("tensor", lambda E, i=i: E.matmul(
                                glaA[i][:, SC:2 * SC], vk[i][:, 0:128],
                                atm[i][:, :], start=False, stop=True),
                                reads=[K("vk", i), K("atm", i)],
                                writes=[K("glaA", i)])
                            b.op("scalar", lambda E, o0=o0, i=i: E.copy(
                                ob[:, o0:o0 + SC], glaA[i][:, SC:2 * SC]),
                                reads=[K("glaA", i)], writes=[K("ob", h)])
                            b.op("tensor", lambda E, i=i: E.matmul(
                                glaS[i][:, :], vk[i][:, 128:256],
                                vk[i][:, 0:128], start=True, stop=True),
                                reads=[K("vk", i)], writes=[K("glaS", i)])
                            b.op("vector", lambda E, h=h, i=i, rend=rend:
                                 E.tensor_scalar(
                                     Sb_[:, h * 128:(h + 1) * 128],
                                     Sb_[:, h * 128:(h + 1) * 128],
                                     rend, EMH, ALU.mult, ALU.mult),
                                 reads=[K("Sb_", h), K("lfb", h)],
                                 writes=[K("Sb_", h)])
                            b.op("vector", lambda E, h=h, i=i:
                                 E.scalar_tensor_tensor(
                                     Sb_[:, h * 128:(h + 1) * 128],
                                     glaS[i][:, :], EMH,
                                     Sb_[:, h * 128:(h + 1) * 128],
                                     ALU.mult, ALU.add),
                                 reads=[K("glaS", i), K("Sb_", h)],
                                 writes=[K("Sb_", h)])
                    b.op("vector", lambda E: E.tensor_tensor(
                        son[:, :], ob[:, :], ob[:, :], ALU.mult),
                        reads=[K("ob", hh) for hh in range(DP)],
                        writes=[K("son")])
                    for kpt in range(DP):
                        b.op("tensor", lambda E, kpt=kpt: E.matmul(
                            qkvP[0][0:1, 0:CH], ocb[:, :],
                            son[:, kpt * CH:(kpt + 1) * CH],
                            start=(kpt == 0), stop=(kpt == DP - 1)),
                            reads=[K("son"), "ocb"], writes=[K("qkvP", 0)])
                    b.op("scalar", lambda E: E.activation(
                        rrow[0:1, :], qkvP[0][0:1, 0:CH], AF.Ln,
                        scale=1.0 / D, bias=eps6[0:1, 0:1]),
                        reads=[K("qkvP", 0), "eps6"], writes=[K("rrow")])
                    b.op("scalar", lambda E: E.activation(
                        rrow[0:1, :], rrow[0:1, :], AF.Exp, scale=-0.5),
                        reads=[K("rrow")], writes=[K("rrow")])
                    b.op("tensor", lambda E: E.matmul(
                        qkvP[1][:, 0:CH], orf[:, :], rrow[0:1, :],
                        start=True, stop=True),
                        reads=[K("rrow"), "orf"], writes=[K("qkvP", 1)])
                    b.op("vector", lambda E: E.tensor_copy(
                        rbc[:, :], qkvP[1][:, 0:CH]),
                        reads=[K("qkvP", 1)], writes=[K("rbc")])
                    for mo in range(DP):
                        b.op("vector", lambda E, mo=mo: E.scalar_tensor_tensor(
                            son[:, mo * CH:(mo + 1) * CH],
                            ob[:, mo * CH:(mo + 1) * CH],
                            nws[:, mo:mo + 1], rbc[:, :],
                            ALU.mult, ALU.mult),
                            reads=[K("ob", mo), "nws", K("rbc")],
                            writes=[K("son")])
                    for mo in range(DP):
                        pp = (mo // 2) % 2
                        co = (mo % 2) * CH
                        for kpt in range(DP):
                            b.op("tensor", lambda E, mo=mo, kpt=kpt, pp=pp,
                                 co=co: E.matmul(
                                     qkvP[pp][:, co:co + CH],
                                     ws["wo"][:, kpt * D + mo * 128:
                                              kpt * D + (mo + 1) * 128],
                                     son[:, kpt * CH:(kpt + 1) * CH],
                                     start=(kpt == 0), stop=(kpt == DP - 1)),
                                 reads=[K("wo"), K("son")],
                                 writes=[K("qkvP", pp)])
                        if mo % 2 == 1:
                            xpair = xs[:, :].rearrange(
                                "p (k n) -> p k n", k=DP)[
                                :, mo - 1:mo + 1, t0:t0 + CH]
                            b.op("vector", lambda E, xpair=xpair, pp=pp:
                                 E.tensor_tensor(
                                     xpair, qkvP[pp][:, 0:2 * CH],
                                     xpair, ALU.add),
                                 reads=[K("qkvP", pp), xk(mo - 1, t0 // XBLK), xk(mo, t0 // XBLK)],
                                 writes=[xk(mo - 1, t0 // XBLK), xk(mo, t0 // XBLK)])
                b.barrier()

        for bb in range(B):
            b.op("sync", lambda E, bb=bb: E.dma_start(
                out=xs[:, :].rearrange("p (k n) -> p k n", k=DP),
                in_=xd.rearrange("b (k p) n -> b p k n", p=128)[bb]),
                writes=[k for pt in range(DP) for k in xk(pt)], dma=True, dsem="xload")
            ffn_phase(f"F{bb}a_")
            if not DBG["ffn_only"]:
                attn_phase(f"A{bb}_")
                ffn_phase(f"F{bb}b_")
            b.op("sync", lambda E, bb=bb: E.dma_start(
                out=outd.rearrange("b (k p) n -> b p k n", p=128)[bb],
                in_=xs[:, :].rearrange("p (k n) -> p k n", k=DP)),
                reads=[k for pt in range(DP) for k in xk(pt)], dma=True, dsem="xstore")
        b.barrier()
        b.play(top)
    return nc


def _oobs_const(cfg):
    E8 = cfg["E"]
    NG = cfg["CAPR"] // 128
    ST = 2 * E8 * NG
    a = np.arange(ST)
    epar = (a // (2 * NG)) % 2
    gg = epar * (2 * NG) + a % (2 * NG)
    ids = cfg["TILE"] + gg.astype(np.float32)
    pat = np.zeros((ST, 2), dtype=np.float32)
    pat[:, 0] = ids
    return np.tile(pat.reshape(1, ST * 2), (128, 1))


def prep_inputs(L0_attn_x, L0_attn_Wq, L0_attn_Wk, L0_attn_Wv, L0_attn_Wo,
                L0_attn_norm_w, L0_ffn_router_w, L0_ffn_w_fc, L0_ffn_w_proj,
                L0_ffn_shared_fc, L0_ffn_shared_proj, cfg, n_cores=8):
    B, N, D, DFF, E8 = cfg["B"], cfg["N"], cfg["D"], cfg["DFF"], cfg["E"]
    SC = cfg["SC"]
    cc = np.ascontiguousarray
    ascc = cc
    x = np.asarray(L0_attn_x, dtype=np.float32)
    fctT = np.concatenate(
        [np.transpose(np.asarray(L0_ffn_w_fc, np.float32), (0, 2, 1)),
         np.transpose(np.asarray(L0_ffn_shared_fc, np.float32))[None]],
        axis=0)
    pjtT = np.concatenate(
        [np.transpose(np.asarray(L0_ffn_w_proj, np.float32), (0, 2, 1)),
         np.transpose(np.asarray(L0_ffn_shared_proj, np.float32))[None]],
        axis=0)
    common = {
        "wq": cc(np.asarray(L0_attn_Wq, np.float32).T),
        "wk": cc(np.asarray(L0_attn_Wk, np.float32).T),
        "wv": cc(np.asarray(L0_attn_Wv, np.float32).T),
        "wo": cc(np.asarray(L0_attn_Wo, np.float32).T),
        "nw": cc(np.asarray(L0_attn_norm_w, np.float32).reshape(D // 128, 128).T),
        "rt": cc(np.concatenate(
            [np.asarray(L0_ffn_router_w, np.float32).T,
             np.ones((D, 1), np.float32)], axis=1)),
        "fct": cc(fctT),
        "pjt": cc(pjtT),
        "identb": np.eye(128, dtype=BFNP),
        "identf": np.eye(128, dtype=np.float32),
        "onescb": np.ones((128, 1), dtype=BFNP),
        "onescf": np.ones((128, 1), dtype=np.float32),
        "onesrf": np.ones((1, 128), dtype=np.float32),
        "halfrf": np.full((1, 128), 0.5, dtype=np.float32),
        "halfrb": np.full((1, 128), 0.5, dtype=BFNP),
        "maskb": np.triu(np.ones((SC, SC), dtype=np.float32)).astype(BFNP),
        "onesrb": np.ones((1, 128), dtype=BFNP),
        "ecols": np.tile(np.arange(E8, dtype=np.float32),
                         (128, cfg["TILE"] // 128)),
        "tvl": (np.arange(128, dtype=np.float32)[:, None] *
                (cfg["TILE"] // 128) +
                np.arange(cfg["TILE"] // 128, dtype=np.float32)[None, :]),
        "oobs": _oobs_const(cfg),
    }
    in_maps = []
    for c in range(n_cores):
        m = dict(common)
        m["x"] = cc(np.transpose(x[c * B:(c + 1) * B], (0, 2, 1)))
        in_maps.append(m)
    return in_maps


def kernel(**inputs):
    cfg = REAL_CFG
    n_cores = 8
    in_maps = prep_inputs(n_cores=n_cores, cfg=cfg, **inputs)
    nc = build_program(cfg)
    res = run_bass_kernel_spmd(nc, in_maps, core_ids=list(range(n_cores)))
    outs = [np.transpose(r["out"], (0, 2, 1)) for r in res.results]
    return np.concatenate(outs, axis=0)


# version 21
# speedup vs baseline: 1.1962x; 1.0685x over previous
import numpy as np
import ml_dtypes
from contextlib import ExitStack

import concourse.bass as bass
import concourse.mybir as mybir
from concourse.bass_utils import run_bass_kernel_spmd
from concourse.bass_core_rust import IndirectOffsetOnAxis

F32 = mybir.dt.float32
BF16 = mybir.dt.bfloat16
I32 = mybir.dt.int32
AF = mybir.ActivationFunctionType
ALU = mybir.AluOpType
AX = mybir.AxisListType
BFNP = ml_dtypes.bfloat16

ENGINES = ("tensor", "vector", "scalar", "gpsimd", "sync")

DBG = {"no_igs": False, "no_gx": False, "no_dsc": False, "no_acc": False,
       "ffn_only": False}

REAL_CFG = dict(B=2, N=2048, D=1024, DFF=4096, E=8,
                TILE=1024, NCH=512, CH=256, SC=128, FCC=256, CAPR=256)


class Builder:

    def __init__(self, nc, ctx):
        self.nc = nc
        self.prog = {e: [] for e in ENGINES}
        self.semh = {}
        self.semc = {}
        self.floors = {e: {} for e in ENGINES}
        self.buf = {}
        self._ctx = ctx
        for name in ["s_tensor", "s_vector", "s_scalar", "s_gpsimd", "s_sync"]:
            self.semh[name] = ctx.enter_context(nc.semaphore(name))
            self.semc[name] = 0

    def op(self, eng, emit, reads=(), writes=(), dma=False, dsem=None):
        if dma:
            assert dsem is not None
            sem = "dma_" + dsem
            if sem not in self.semh:
                self.semh[sem] = self._ctx.enter_context(self.nc.semaphore(sem))
                self.semc[sem] = 0
            inc = 16
        else:
            sem = "s_" + eng
            inc = 1
        needs = {}

        def want(mark):
            s, v = mark
            if eng == "tensor" and s == "s_tensor":
                return
            if v > needs.get(s, 0):
                needs[s] = v

        for k in reads:
            st = self.buf.get(k)
            if st and st["w"]:
                want(st["w"])
        for k in writes:
            st = self.buf.get(k)
            if st:
                if st["w"]:
                    want(st["w"])
                for r in st["r"]:
                    want(r)
        lst = self.prog[eng]
        fl = self.floors[eng]
        for s in sorted(needs):
            v = needs[s]
            if fl.get(s, 0) >= v:
                continue
            fl[s] = v
            h = self.semh[s]
            lst.append(lambda E, h=h, v=v: E.wait_ge(h, v))
        self.semc[sem] += inc
        h = self.semh[sem]
        lst.append(lambda E, h=h, emit=emit, inc=inc: emit(E).then_inc(h, inc))
        mark = (sem, self.semc[sem])
        for k in writes:
            self.buf[k] = {"w": mark, "r": []}
        for k in reads:
            if k in writes:
                continue
            st = self.buf.setdefault(k, {"w": None, "r": []})
            st["r"].append(mark)
        return mark

    def barrier(self):
        targets = [(s, c) for s, c in self.semc.items() if c > 0]
        for e in ENGINES:
            fl = self.floors[e]
            for s, v in targets:
                if fl.get(s, 0) >= v:
                    continue
                fl[s] = v
                h = self.semh[s]
                self.prog[e].append(lambda E, h=h, v=v: E.wait_ge(h, v))
        self.buf.clear()

    def play(self, ctx):
        block = ctx.enter_context(self.nc.Block())
        prog = self.prog

        @block.tensor
        def _(E):
            for f in prog["tensor"]:
                f(E)

        @block.vector
        def _(E):
            for f in prog["vector"]:
                f(E)

        @block.scalar
        def _(E):
            for f in prog["scalar"]:
                f(E)

        @block.gpsimd
        def _(E):
            for f in prog["gpsimd"]:
                f(E)

        @block.sync
        def _(E):
            for f in prog["sync"]:
                f(E)


def build_program(cfg):
    B, N, D, DFF, E8 = cfg["B"], cfg["N"], cfg["D"], cfg["DFF"], cfg["E"]
    TILE, NCH, CH, SC, FCC = (cfg["TILE"], cfg["NCH"], cfg["CH"], cfg["SC"],
                              cfg["FCC"])
    CAPR = cfg["CAPR"]
    CAP2 = 2 * CAPR
    NG = CAPR // 128
    GPE = 2 * NG
    C8 = TILE // 128
    SLOT_TOT = (cfg["E"]) * CAP2
    assert CAPR % 128 == 0 and CAP2 <= NCH
    TQ = 4 if (D // 128) % 4 == 0 else D // 128
    CQ = 4 if C8 % 4 == 0 else min(4, C8)
    DP = D // 128
    DFP = DFF // 128
    NT = N // TILE
    NSC = CH // SC
    assert TILE % NCH == 0 and N % CH == 0 and CH % SC == 0
    assert DFF % FCC == 0 and FCC % 128 == 0 and N % NCH == 0

    nc = bass.Bass("TRN2", target_bir_lowering=False)

    xd = nc.dram_tensor("x", [B, D, N], F32, kind="ExternalInput").ap()
    wqd = nc.dram_tensor("wq", [D, D], F32, kind="ExternalInput").ap()
    wkd = nc.dram_tensor("wk", [D, D], F32, kind="ExternalInput").ap()
    wvd = nc.dram_tensor("wv", [D, D], F32, kind="ExternalInput").ap()
    wod = nc.dram_tensor("wo", [D, D], F32, kind="ExternalInput").ap()
    nwd = nc.dram_tensor("nw", [128, DP], F32, kind="ExternalInput").ap()
    rtd = nc.dram_tensor("rt", [D, E8 + 1], F32, kind="ExternalInput").ap()
    fct = nc.dram_tensor("fct", [E8 + 1, D, DFF], F32, kind="ExternalInput").ap()
    pjt = nc.dram_tensor("pjt", [E8 + 1, DFF, D], F32, kind="ExternalInput").ap()
    identb = nc.dram_tensor("identb", [128, 128], BF16, kind="ExternalInput").ap()
    identf = nc.dram_tensor("identf", [128, 128], F32, kind="ExternalInput").ap()
    onescb = nc.dram_tensor("onescb", [128, 1], BF16, kind="ExternalInput").ap()
    onescf = nc.dram_tensor("onescf", [128, 1], F32, kind="ExternalInput").ap()
    onesrf = nc.dram_tensor("onesrf", [1, 128], F32, kind="ExternalInput").ap()
    halfrf = nc.dram_tensor("halfrf", [1, 128], F32, kind="ExternalInput").ap()
    halfrb = nc.dram_tensor("halfrb", [1, 128], BF16, kind="ExternalInput").ap()
    maskbd = nc.dram_tensor("maskb", [SC, SC], BF16, kind="ExternalInput").ap()
    onesrb = nc.dram_tensor("onesrb", [1, 128], BF16, kind="ExternalInput").ap()
    ecolsd = nc.dram_tensor("ecols", [128, C8 * E8], F32, kind="ExternalInput").ap()
    tvld = nc.dram_tensor("tvl", [128, C8], F32, kind="ExternalInput").ap()
    oobsd = nc.dram_tensor("oobs", [128, SLOT_TOT * 2 // 128], F32,
                           kind="ExternalInput").ap()
    outd = nc.dram_tensor("out", [B, D, N], F32, kind="ExternalOutput").ap()
    NDUMP = 4 * (CAPR // 128)
    xtmd_t = nc.dram_tensor("xtmd", [TILE + NDUMP, D], BF16, kind="Internal")
    igbd_t = nc.dram_tensor("igbd", [SLOT_TOT + 1, 2], F32, kind="Internal")
    dltA_t = nc.dram_tensor("dltA", [TILE + NDUMP, D], BF16, kind="Internal")
    dltB_t = nc.dram_tensor("dltB", [TILE + NDUMP, D], BF16, kind="Internal")
    fctb = nc.dram_tensor("fctb", [E8 + 1, DFF // FCC, 128, (D // 128) * FCC],
                          BF16, kind="Internal").ap()
    pjtb = nc.dram_tensor("pjtb", [E8 + 1, D // 128, 128, (DFF // 128) * 128],
                          BF16, kind="Internal").ap()
    awtb = nc.dram_tensor("awtb", [4, 128, (D // 128) * D], BF16,
                          kind="Internal").ap()

    top = ExitStack()
    with top:
        b = Builder(nc, top)
        sb = lambda nm, sh, dt: top.enter_context(nc.sbuf_tensor(nm, sh, dt))

        xs = sb("xs", [128, DP * N], F32)
        statA = sb("statA", [1, N], BF16)
        statB = sb("statB", [1, N], BF16)
        idb = sb("idb", [128, 128], BF16)
        idf = sb("idf", [128, 128], F32)
        ocb = sb("ocb", [128, 1], BF16)
        orf = sb("orf", [1, 128], F32)
        hrf = sb("hrf", [1, 128], F32)
        hrb = sb("hrb", [1, 128], BF16)
        mkb = sb("mkb", [SC, SC], BF16)
        nws = sb("nws", [128, DP], F32)
        rtsF = sb("rtsF", [128, DP * (E8 + 1)], F32)
        rstdc = sb("rstdc", [128, N // 128], F32)
        ocf = sb("ocf", [128, 1], F32)
        sbc = sb("sbc", [128, E8], F32)
        srow = sb("srow", [1, E8], F32)
        zcol = sb("zcol", [128, max(SC, NCH)], F32)
        orb = sb("orb", [1, 128], BF16)
        ecols = sb("ecols_s", [128, C8 * E8], F32)
        tvl = sb("tvl_s", [128, C8], F32)
        oobs = sb("oobs_s", [128, SLOT_TOT * 2 // 128], F32)
        eps5 = sb("eps5", [128, 1], F32)
        eps6 = sb("eps6", [128, 1], F32)
        hcolP = sb("hcolP", [128, 1], F32)
        hcolN = sb("hcolN", [128, 1], F32)

        XBLK = min(512, N)

        def ecast(E, out_ap, in_ap):
            if hasattr(E, "tensor_copy"):
                return E.tensor_copy(out_ap, in_ap)
            return E.copy(out_ap, in_ap)

        def xk(pt, blk=None):
            if blk is None:
                return [("x", pt, i) for i in range(N // XBLK)]
            return ("x", pt, blk)

        def xks(pt, t0, t1):
            return [("x", pt, i) for i in range(t0 // XBLK,
                                                (t1 + XBLK - 1) // XBLK)]

        def xsl(pt, t0, t1):
            return xs[:, pt * N + t0:pt * N + t1]

        b.op("sync", lambda E: E.dma_start(out=idb[:, :], in_=identb),
             writes=["idb"], dma=True, dsem="idb")
        b.op("sync", lambda E: E.dma_start(out=idf[:, :], in_=identf),
             writes=["idf"], dma=True, dsem="idf")
        b.op("sync", lambda E: E.dma_start(out=ocb[:, :], in_=onescb),
             writes=["ocb"], dma=True, dsem="ocb")
        b.op("sync", lambda E: E.dma_start(out=ocf[:, :], in_=onescf),
             writes=["ocf"], dma=True, dsem="ocf")
        b.op("sync", lambda E: E.dma_start(out=orf[:, :], in_=onesrf),
             writes=["orf"], dma=True, dsem="orf")
        b.op("sync", lambda E: E.dma_start(out=hrf[:, :], in_=halfrf),
             writes=["hrf"], dma=True, dsem="hrf")
        b.op("sync", lambda E: E.dma_start(out=hrb[:, :], in_=halfrb),
             writes=["hrb"], dma=True, dsem="hrb")
        b.op("sync", lambda E: E.dma_start(out=mkb[:, :], in_=maskbd),
             writes=["mkb"], dma=True, dsem="mkb")
        b.op("sync", lambda E: E.dma_start(out=nws[:, :], in_=nwd),
             writes=["nws"], dma=True, dsem="nws")
        b.op("sync", lambda E: E.dma_start(
            out=rtsF[:, :].rearrange("p (k e) -> p k e", k=DP),
            in_=rtd.rearrange("(k p) e -> p k e", p=128)),
            writes=["rtsF"], dma=True, dsem="rts")
        b.op("gpsimd", lambda E: E.memset(zcol[:, :], 0.0), writes=["zcol"])
        b.op("sync", lambda E: E.dma_start(out=orb[:, :], in_=onesrb),
             writes=["orb"], dma=True, dsem="orb")
        b.op("sync", lambda E: E.dma_start(out=ecols[:, :], in_=ecolsd),
             writes=["ecols"], dma=True, dsem="ecols")
        b.op("sync", lambda E: E.dma_start(out=tvl[:, :], in_=tvld),
             writes=["tvl"], dma=True, dsem="tvl")
        b.op("sync", lambda E: E.dma_start(out=oobs[:, :], in_=oobsd),
             writes=["oobs"], dma=True, dsem="oobs")
        b.op("gpsimd", lambda E: E.memset(eps5[:, :], 1e-5), writes=["eps5"])
        b.op("gpsimd", lambda E: E.memset(eps6[:, :], 1e-6), writes=["eps6"])
        b.op("gpsimd", lambda E: E.memset(hcolP[:, :], 44.36), writes=["hcolP"])
        b.op("gpsimd", lambda E: E.memset(hcolN[:, :], -44.36), writes=["hcolN"])
        ZW = max(SC, NCH)
        for gg in range(NDUMP):
            for off in range(0, D, ZW):
                w = min(ZW, D - off)
                b.op("gpsimd", lambda E, gg=gg, off=off, w=w: E.dma_start(
                    out=xtmd_t.ap()[TILE + gg:TILE + gg + 1, off:off + w],
                    in_=zcol[0:1, 0:w]),
                    reads=["zcol"], writes=[("xtmd", "dump", gg, off)],
                    dma=True, dsem="xdump")

        def ln_stats(tag, xbB, sqB, xbK, sqK, psA, psAk, psB, psBk, rowt, rowtK,
                     eps, rowt2=None, rowt2K=None, rstdP=None, rstdPk=None):
            for c in range(N // NCH):
                t0 = c * NCH
                for pt in range(DP):
                    p = pt % 2
                    xb, sq = xbB[p], sqB[p]
                    kxb, ksq = xbK[p], sqK[p]
                    b.op("vector", lambda E, xb=xb, pt=pt, t0=t0:
                         E.tensor_copy(xb[:, 0:NCH], xsl(pt, t0, t0 + NCH)),
                         reads=xks(pt, t0, t0 + NCH), writes=kxb)
                    b.op("vector", lambda E, sq=sq, xb=xb: E.tensor_tensor(
                        sq[:, 0:NCH], xb[:, 0:NCH], xb[:, 0:NCH], ALU.mult),
                        reads=kxb, writes=ksq)
                    st, sp = (pt == 0), (pt == DP - 1)
                    b.op("tensor", lambda E, xb=xb, st=st, sp=sp: E.matmul(
                        psA[0:1, 0:NCH], ocb[:, :], xb[:, 0:NCH],
                        start=st, stop=sp),
                        reads=list(kxb) + ["ocb"], writes=[psAk])
                    b.op("tensor", lambda E, sq=sq, st=st, sp=sp: E.matmul(
                        psB[0:1, 0:NCH], ocb[:, :], sq[:, 0:NCH],
                        start=st, stop=sp),
                        reads=list(ksq) + ["ocb"], writes=[psBk])
                if rowt2 is None:
                    b.op("scalar", lambda E, t0=t0: E.activation(
                        statA[0:1, t0:t0 + NCH], psA[0:1, 0:NCH], AF.Copy,
                        scale=1.0 / D),
                        reads=[psAk], writes=["statA"])
                    b.op("scalar", lambda E, t0=t0: E.activation(
                        statB[0:1, t0:t0 + NCH], psB[0:1, 0:NCH], AF.Copy,
                        scale=1.0 / D),
                        reads=[psBk], writes=["statB"])
                    b.op("vector", lambda E, t0=t0: E.tensor_tensor(
                        rowt[0:1, 0:NCH], statA[0:1, t0:t0 + NCH],
                        statA[0:1, t0:t0 + NCH], ALU.mult),
                        reads=["statA"], writes=[rowtK])
                    b.op("vector", lambda E, t0=t0: E.tensor_tensor(
                        statB[0:1, t0:t0 + NCH], statB[0:1, t0:t0 + NCH],
                        rowt[0:1, 0:NCH], ALU.subtract),
                        reads=[rowtK, "statB"], writes=["statB"])
                    b.op("scalar", lambda E, t0=t0: E.activation(
                        statB[0:1, t0:t0 + NCH], statB[0:1, t0:t0 + NCH],
                        AF.Ln, bias=eps[0:1, 0:1]),
                        reads=["statB", "eps5"], writes=["statB"])
                    b.op("scalar", lambda E, t0=t0: E.activation(
                        statB[0:1, t0:t0 + NCH], statB[0:1, t0:t0 + NCH],
                        AF.Exp, scale=-0.5),
                        reads=["statB"], writes=["statB"])
                else:
                    b.op("scalar", lambda E, t0=t0: E.activation(
                        rowt2[0:1, 0:NCH], psA[0:1, 0:NCH], AF.Copy,
                        scale=1.0 / D),
                        reads=[psAk], writes=[rowt2K])
                    b.op("scalar", lambda E, t0=t0: E.activation(
                        statA[0:1, t0:t0 + NCH], rowt2[0:1, 0:NCH], AF.Copy),
                        reads=[rowt2K], writes=["statA"])
                    b.op("scalar", lambda E, t0=t0: E.activation(
                        rowt[0:1, 0:NCH], psB[0:1, 0:NCH], AF.Copy,
                        scale=1.0 / D),
                        reads=[psBk], writes=[rowtK])
                    b.op("vector", lambda E: E.scalar_tensor_tensor(
                        rowt2[0:1, 0:NCH], rowt2[0:1, 0:NCH], -1.0,
                        rowt2[0:1, 0:NCH], ALU.mult, ALU.mult),
                        reads=[rowt2K], writes=[rowt2K])
                    b.op("vector", lambda E: E.tensor_tensor(
                        rowt[0:1, 0:NCH], rowt[0:1, 0:NCH],
                        rowt2[0:1, 0:NCH], ALU.add),
                        reads=[rowtK, rowt2K], writes=[rowtK])
                    b.op("scalar", lambda E: E.activation(
                        rowt[0:1, 0:NCH], rowt[0:1, 0:NCH], AF.Ln,
                        bias=eps[0:1, 0:1]),
                        reads=[rowtK, "eps5"], writes=[rowtK])
                    b.op("scalar", lambda E: E.activation(
                        rowt[0:1, 0:NCH], rowt[0:1, 0:NCH], AF.Exp,
                        scale=-0.5),
                        reads=[rowtK], writes=[rowtK])
                    b.op("scalar", lambda E, t0=t0: E.activation(
                        statB[0:1, t0:t0 + NCH], rowt[0:1, 0:NCH], AF.Copy),
                        reads=[rowtK], writes=["statB"])
                    for cc_ in range(NCH // 128):
                        col = (t0 // 128) + cc_
                        b.op("tensor", lambda E, cc_=cc_, col=col:
                             E.transpose(
                                 rstdP[0:128, 2 * (col % 4):
                                       2 * (col % 4) + 1],
                                 rowt[0:1, cc_ * 128:(cc_ + 1) * 128],
                                 idf[0:1, 0:1]),
                             reads=[rowtK, "idf"], writes=[rstdPk])
                        b.op("scalar", lambda E, col=col: E.copy(
                            rstdc[:, col:col + 1],
                            rstdP[0:128, 2 * (col % 4):2 * (col % 4) + 1]),
                            reads=[rstdPk], writes=["rstdc"])

        weight_cached = set()

        def ffn_phase(tag):
            fc_ctx = ExitStack()
            with fc_ctx:
                fs = lambda nm, sh, dt: fc_ctx.enter_context(
                    nc.sbuf_tensor(tag + nm, sh, dt))
                fp = lambda nm, sh, dt: fc_ctx.enter_context(
                    nc.psum_tensor(tag + nm, sh, dt))
                K = lambda *a: (tag,) + a
                half = DFP // 2

                xln = fs("xln", [128, DP * TILE], BF16)
                xtm = fs("xtm", [128, C8 * D], BF16)
                hg = fs("hg", [128, DFP * NCH], BF16)
                xgog = [fs(f"xgog{i}", [128, GPE * D], BF16)
                        for i in range(2)]
                xg = fs("xg", [128, DP * CAP2], BF16)
                ogf = xg
                fcb = [fs(f"fcb{i}", [128, DP * FCC], BF16) for i in range(2)]
                pjb = [fs(f"pjb{i}", [128, DFP * 128], BF16) for i in range(2)]
                h1 = [fs(f"h1{i}", [128, NCH], BF16) for i in range(2)]
                evt = fs("evt", [128, 2 * NCH], F32)
                evt2 = fs("evt2", [128, 2 * NCH], F32)
                SEL = [xg[0:E8, i * TILE:(i + 1) * TILE] for i in range(2)]
                pos = fs("pos", [E8, TILE], F32)
                oha = [fs(f"oha{i}", [128, C8 * E8], F32) for i in range(2)]
                posm = fs("posm", [128, C8 * E8], F32)
                ea_ = [fs(f"ea{i}", [128, C8], F32) for i in range(2)]
                pa_ = [fs(f"pa{i}", [128, C8], F32) for i in range(2)]
                si_ = [fs(f"si{i}", [128, C8], I32) for i in range(2)]
                ii_ = [fs(f"ii{i}", [128, C8], I32) for i in range(4)]
                sval = [fs(f"sval{i}", [128, 2 * C8], F32) for i in range(2)]
                idg = fs("idg", [128, GPE * E8 * 2], F32)
                ido = fs("ido", [128, GPE * E8], I32)
                lgb = fs("lgb", [128, E8], F32)
                tmp8 = fs("tmp8", [128, E8], F32)
                m1b = fs("m1b", [128, 1], F32)
                m2b = fs("m2b", [128, 1], F32)
                nm2b = fs("nm2b", [128, 1], F32)

                hP = [fp(f"hP{i}", [128, NCH], F32) for i in range(2)]
                ogP = [fp(f"ogP{i}", [128, CAP2], F32) for i in range(2)]
                trP = [fp(f"trP{i}", [128, 512], BF16) for i in range(2)]
                bcP = [fp(f"bcP{i}", [128, NCH], F32) for i in range(2)]

                ln_stats(tag, [h1[0][:, 0:NCH], h1[1][:, 0:NCH]],
                         [xgog[0][:, 0:NCH], xgog[1][:, 0:NCH]],
                         [[K("h1", 0)], [K("h1", 1)]],
                         [[K("xgog", 0, 0)], [K("xgog", 1, 0)]],
                         bcP[0], K("bcP", 0), bcP[1], K("bcP", 1),
                         evt, K("evt"), eps5,
                         rowt2=evt2, rowt2K=K("evt2"),
                         rstdP=hP[0], rstdPk=K("hP", 0))

                if ("srow",) not in weight_cached:
                    weight_cached.add(("srow",))
                    for kpt in range(DP):
                        b.op("tensor", lambda E, kpt=kpt: E.matmul(
                            bcP[0][0:1, 0:E8], ocf[:, :],
                            rtsF[:, kpt * (E8 + 1):kpt * (E8 + 1) + E8],
                            start=(kpt == 0), stop=(kpt == DP - 1)),
                            reads=["ocf", "rtsF"], writes=[K("bcP", 0)])
                    b.op("scalar", lambda E: E.copy(
                        srow[0:1, :], bcP[0][0:1, 0:E8]),
                        reads=[K("bcP", 0)], writes=["srow"])
                    b.op("tensor", lambda E: E.matmul(
                        bcP[0][:, 0:E8], orf[:, :], srow[0:1, :],
                        start=True, stop=True),
                        reads=["srow", "orf"], writes=[K("bcP", 0)])
                    b.op("scalar", lambda E: E.copy(
                        sbc[:, :], bcP[0][:, 0:E8]),
                        reads=[K("bcP", 0)], writes=["sbc"])

                for par in range(2):
                    b.op("gpsimd", lambda E, par=par: E.memset(
                        xgog[par][:, :], 0.0),
                        writes=[K("xgog", par, g) for g in range(GPE)])
                for r in range(2):
                    b.op("vector", lambda E, r=r: E.tensor_copy(
                        sval[r][:, :].rearrange("p (c v) -> p c v", v=2)
                        [:, :, 0:1],
                        tvl[:, :].rearrange("p (c one) -> p c one", one=1)),
                        reads=["tvl"], writes=[K("sval", r)])

                def stream_fc(e, fcc, buf, bufk):
                    ck = ("wfc", e, fcc)
                    if ck not in weight_cached:
                        weight_cached.add(ck)
                        PW = min(2 * NCH, DP * FCC)
                        kpp = max(1, PW // FCC)
                        for pc in range(DP * FCC // PW):
                            k0 = pc * kpp
                            stg = evt if pc % 2 == 0 else evt2
                            stgk = K("evt") if pc % 2 == 0 else K("evt2")
                            din = "sync" if (fcc + pc) % 2 == 0 else "gpsimd"
                            ce = ("vector", "scalar", "gpsimd")[pc % 3]
                            b.op(din, lambda E, e=e, fcc=fcc, k0=k0,
                                 kpp=kpp, stg=stg: E.dma_start(
                                     out=stg[:, 0:PW].rearrange(
                                         "p (k c) -> p k c", k=kpp),
                                     in_=fct.rearrange(
                                         "e (k p) c -> e p k c", p=128)
                                     [e, :, k0:k0 + kpp,
                                      fcc * FCC:(fcc + 1) * FCC]),
                                 reads=[stgk], writes=[stgk],
                                 dma=True, dsem="wst" + din + str(pc % 2))
                            b.op(ce, lambda E, buf=buf, pc=pc, PW=PW,
                                 stg=stg: ecast(E,
                                     buf[:, pc * PW:(pc + 1) * PW],
                                     stg[:, 0:PW]),
                                 reads=[stgk], writes=[bufk])
                        b.op("sync", lambda E, e=e, fcc=fcc, buf=buf:
                             E.dma_start(out=fctb[e, fcc], in_=buf[:, :]),
                             reads=[bufk], writes=[("fctb", e, fcc)],
                             dma=True, dsem=f"fcw{fcc % 2}")
                    else:
                        b.op("gpsimd", lambda E, e=e, fcc=fcc, buf=buf:
                             E.dma_start(out=buf[:, :], in_=fctb[e, fcc]),
                             reads=[("fctb", e, fcc)], writes=[bufk],
                             dma=True, dsem=f"fcr{fcc % 2}")

                def stream_pj(e, mo, buf, bufk, par):
                    ck = ("wpj", e, mo)
                    if ck not in weight_cached:
                        weight_cached.add(ck)
                        PW = min(2 * NCH, DFP * 128)
                        kpj = PW // 128
                        for pc in range(DFP * 128 // PW):
                            k0 = pc * kpj
                            stg = evt if pc % 2 == 0 else evt2
                            stgk = K("evt") if pc % 2 == 0 else K("evt2")
                            din = "sync" if (mo + pc) % 2 == 0 else "gpsimd"
                            ce = ("vector", "scalar", "gpsimd")[pc % 3]
                            b.op(din, lambda E, e=e, mo=mo, k0=k0,
                                 kpj=kpj, stg=stg: E.dma_start(
                                     out=stg[:, 0:PW].rearrange(
                                         "p (k c) -> p k c", k=kpj),
                                     in_=pjt.rearrange(
                                         "e (k p) c -> e p k c", p=128)
                                     [e, :, k0:k0 + kpj,
                                      mo * 128:(mo + 1) * 128]),
                                 reads=[stgk], writes=[stgk],
                                 dma=True, dsem="wst" + din + str(pc % 2))
                            b.op(ce, lambda E, buf=buf, pc=pc, PW=PW,
                                 stg=stg: ecast(E,
                                     buf[:, pc * PW:(pc + 1) * PW],
                                     stg[:, 0:PW]),
                                 reads=[stgk], writes=[bufk])
                        b.op("sync", lambda E, e=e, mo=mo, buf=buf:
                             E.dma_start(
                                 out=pjtb[e, mo], in_=buf[:, :]),
                             reads=[bufk], writes=[("pjtb", e, mo)],
                             dma=True, dsem=f"pjw{par}")
                    else:
                        b.op("sync", lambda E, e=e, mo=mo, buf=buf:
                             E.dma_start(
                                 out=buf[:, :], in_=pjtb[e, mo]),
                             reads=[("pjtb", e, mo)], writes=[bufk],
                             dma=True, dsem=f"pjr{par}")

                nchn = TILE // NCH
                for tile in range(NT):
                    tt0 = tile * TILE
                    for tc in range(C8):
                        t0c = tt0 + tc * 128
                        for pt in range(DP):
                            b.op("tensor", lambda E, pt=pt, t0c=t0c:
                                 E.matmul(
                                     bcP[0][:, 0:E8 + 1],
                                     xsl(pt, t0c, t0c + 128),
                                     rtsF[:, pt * (E8 + 1):
                                          (pt + 1) * (E8 + 1)],
                                     start=(pt == 0), stop=(pt == DP - 1)),
                                 reads=xks(pt, t0c, t0c + 128) + ["rtsF"],
                                 writes=[K("bcP", 0)])
                        b.op("vector", lambda E: E.tensor_scalar(
                            m1b[:, :], bcP[0][:, E8:E8 + 1], 1.0 / D, None,
                            ALU.mult),
                            reads=[K("bcP", 0)], writes=[K("mcolt")])
                        b.op("vector", lambda E: E.tensor_scalar(
                            tmp8[:, :], sbc[:, :], m1b[:, 0:1], None,
                            ALU.mult),
                            reads=["sbc", K("mcolt")], writes=[K("tmp8")])
                        b.op("vector", lambda E: E.tensor_tensor(
                            lgb[:, :], bcP[0][:, 0:E8], tmp8[:, :],
                            ALU.subtract),
                            reads=[K("bcP", 0), K("tmp8")], writes=[K("lgb")])
                        b.op("vector", lambda E, t0c=t0c: E.tensor_scalar(
                            lgb[:, :], lgb[:, :],
                            rstdc[:, t0c // 128:t0c // 128 + 1], None,
                            ALU.mult),
                            reads=[K("lgb"), "rstdc"], writes=[K("lgb")])
                        b.op("vector", lambda E: E.reduce_max(
                            m1b[:, :], lgb[:, :], AX.X),
                            reads=[K("lgb")], writes=[K("m1b")])
                        o1 = oha[0][:, tc * E8:(tc + 1) * E8]
                        o2 = oha[1][:, tc * E8:(tc + 1) * E8]
                        b.op("vector", lambda E, o1=o1: E.tensor_scalar(
                            o1, lgb[:, :], m1b[:, 0:1], None, ALU.is_ge),
                            reads=[K("lgb"), K("m1b")], writes=[K("oha", 0)])
                        b.op("vector", lambda E, o1=o1: E.scalar_tensor_tensor(
                            tmp8[:, :], o1, -1e30, lgb[:, :],
                            ALU.mult, ALU.add),
                            reads=[K("oha", 0), K("lgb")], writes=[K("tmp8")])
                        b.op("vector", lambda E: E.reduce_max(
                            m2b[:, :], tmp8[:, :], AX.X),
                            reads=[K("tmp8")], writes=[K("m2b")])
                        b.op("vector", lambda E, o2=o2: E.tensor_scalar(
                            o2, tmp8[:, :], m2b[:, 0:1], None, ALU.is_ge),
                            reads=[K("tmp8"), K("m2b")], writes=[K("oha", 1)])
                        b.op("scalar", lambda E: E.activation(
                            nm2b[:, :], m2b[:, 0:1], AF.Copy, scale=-1.0),
                            reads=[K("m2b")], writes=[K("nm2b")])
                        b.op("scalar", lambda E, tc=tc: E.activation(
                            sval[0][:, 2 * tc + 1:2 * tc + 2], m1b[:, 0:1],
                            AF.Sigmoid, bias=nm2b[:, 0:1]),
                            reads=[K("m1b"), K("nm2b")], writes=[K("sval", 0)])
                        b.op("scalar", lambda E, tc=tc: E.activation(
                            sval[1][:, 2 * tc + 1:2 * tc + 2], m1b[:, 0:1],
                            AF.Sigmoid, bias=m2b[:, 0:1], scale=-1.0),
                            reads=[K("m1b"), K("m2b")], writes=[K("sval", 1)])
                        for r in range(2):
                            oo = oha[r][:, tc * E8:(tc + 1) * E8]
                            b.op("tensor", lambda E, oo=oo: E.transpose(
                                bcP[1][0:E8, 0:128], oo, idf[:, :]),
                                reads=[K("oha", r), "idf"],
                                writes=[K("bcP", 1)])
                            b.op("scalar", lambda E, r=r, tc=tc: E.copy(
                                SEL[r][:, tc * 128:(tc + 1) * 128],
                                bcP[1][0:E8, 0:128]),
                                reads=[K("bcP", 1)], writes=[K("xg")])

                    for c in range(nchn):
                        t0 = tt0 + c * NCH
                        b.op("tensor", lambda E, t0=t0: E.matmul(
                            bcP[0][:, :], orb[:, :], statA[0:1, t0:t0 + NCH],
                            start=True, stop=True),
                            reads=["statA", "orb"], writes=[K("bcP", 0)])
                        b.op("tensor", lambda E, t0=t0: E.matmul(
                            bcP[1][:, :], orb[:, :], statB[0:1, t0:t0 + NCH],
                            start=True, stop=True),
                            reads=["statB", "orb"], writes=[K("bcP", 1)])
                        for pt in range(DP):
                            b.op("vector", lambda E, pt=pt, t0=t0:
                                 E.tensor_tensor(
                                     evt[:, 0:NCH], xsl(pt, t0, t0 + NCH),
                                     bcP[0][:, :], ALU.subtract),
                                 reads=xks(pt, t0, t0 + NCH) + [K("bcP", 0)],
                                 writes=[K("evt")])
                            b.op("vector", lambda E, pt=pt, c=c:
                                 E.tensor_tensor(
                                     xln[:, pt * TILE + c * NCH:
                                         pt * TILE + (c + 1) * NCH],
                                     evt[:, 0:NCH], bcP[1][:, :], ALU.mult),
                                 reads=[K("evt"), K("bcP", 1)],
                                 writes=[K("xln", pt)])

                    for c in range(C8):
                        for q in range(DP // TQ):
                            par = (c * (DP // TQ) + q) % 2
                            for j in range(TQ):
                                pt = q * TQ + j
                                b.op("tensor", lambda E, c=c, pt=pt, j=j,
                                     par=par: E.transpose(
                                         trP[par][:, j * 128:(j + 1) * 128],
                                         xln[:, pt * TILE + c * 128:
                                             pt * TILE + (c + 1) * 128],
                                         idb[:, :]),
                                     reads=[K("xln", pt), "idb"],
                                     writes=[K("trP", par)])
                            b.op("scalar", lambda E, c=c, q=q, par=par:
                                 E.copy(xtm[:, c * D + q * TQ * 128:
                                            c * D + (q + 1) * TQ * 128],
                                        trP[par][:, 0:TQ * 128]),
                                 reads=[K("trP", par)], writes=[K("xtm")])
                    b.op("sync", lambda E: E.dma_start(
                        out=xtmd_t.ap()[0:TILE, :].rearrange("(p c) d -> p c d", p=128),
                        in_=xtm[:, :].rearrange("p (c d) -> p c d", c=C8)),
                        reads=[K("xtm")], writes=[("xtmd",)], dma=True,
                        dsem="xtmd")

                    for r in range(2):
                        for c in range(TILE // NCH):
                            init = (0.0 if c == 0 else
                                    pos[:, c * NCH - 1:c * NCH])
                            b.op("vector", lambda E, c=c, r=r, init=init:
                                 E.tensor_tensor_scan(
                                     pos[0:E8, c * NCH:(c + 1) * NCH],
                                     SEL[r][:, c * NCH:(c + 1) * NCH],
                                     zcol[0:E8, 0:NCH], init,
                                     ALU.add, ALU.add),
                                 reads=[K("xg"), "zcol", K("pos")],
                                 writes=[K("pos")])
                        for tc in range(C8):
                            b.op("tensor", lambda E, tc=tc: E.transpose(
                                bcP[1][0:128, 0:E8],
                                pos[:, tc * 128:(tc + 1) * 128],
                                idf[0:E8, 0:E8]),
                                reads=[K("pos"), "idf"], writes=[K("bcP", 1)])
                            b.op("scalar", lambda E, tc=tc: E.copy(
                                posm[:, tc * E8:(tc + 1) * E8],
                                bcP[1][0:128, 0:E8]),
                                reads=[K("bcP", 1)], writes=[K("posm")])
                        b.op("vector", lambda E, r=r: E.tensor_tensor(
                            posm[:, :], posm[:, :], oha[r][:, :], ALU.mult),
                            reads=[K("posm"), K("oha", r)], writes=[K("posm")])
                        b.op("vector", lambda E, r=r: E.tensor_reduce(
                            pa_[r][:, :].rearrange("p (c one) -> p c one", one=1),
                            posm[:, :].rearrange("p (c e) -> p c e", e=E8),
                            AX.X, ALU.add),
                            reads=[K("posm")], writes=[K("pa", r)])
                        b.op("vector", lambda E, r=r: E.tensor_scalar(
                            pa_[r][:, :], pa_[r][:, :], -1.0, None, ALU.add),
                            reads=[K("pa", r)], writes=[K("pa", r)])
                        b.op("vector", lambda E, r=r: E.tensor_tensor(
                            posm[:, :], ecols[:, :], oha[r][:, :], ALU.mult),
                            reads=["ecols", K("oha", r), K("posm")],
                            writes=[K("posm")])
                        b.op("vector", lambda E, r=r: E.tensor_reduce(
                            ea_[r][:, :].rearrange("p (c one) -> p c one", one=1),
                            posm[:, :].rearrange("p (c e) -> p c e", e=E8),
                            AX.X, ALU.add),
                            reads=[K("posm")], writes=[K("ea", r)])
                        ST = 2 * E8 * NG
                        b.op("vector", lambda E, r=r: E.tensor_copy(
                            ii_[0][:, :], pa_[r][:, :]),
                            reads=[K("pa", r)], writes=[K("ii", 0)])
                        b.op("vector", lambda E: E.tensor_scalar(
                            ii_[1][:, :], ii_[0][:, :], 7, None,
                            ALU.arith_shift_right),
                            reads=[K("ii", 0)], writes=[K("ii", 1)])
                        b.op("vector", lambda E: E.tensor_scalar(
                            ii_[0][:, :], ii_[0][:, :], 127, None,
                            ALU.bitwise_and),
                            reads=[K("ii", 0)], writes=[K("ii", 0)])
                        b.op("vector", lambda E, r=r: E.tensor_copy(
                            ii_[2][:, :], ea_[r][:, :]),
                            reads=[K("ea", r)], writes=[K("ii", 2)])
                        b.op("vector", lambda E, r=r: E.tensor_scalar(
                            ii_[2][:, :], ii_[2][:, :], 2 * NG, r * NG,
                            ALU.mult, ALU.add),
                            reads=[K("ii", 2)], writes=[K("ii", 2)])
                        b.op("vector", lambda E: E.tensor_tensor(
                            ii_[2][:, :], ii_[2][:, :], ii_[1][:, :],
                            ALU.add),
                            reads=[K("ii", 2), K("ii", 1)], writes=[K("ii", 2)])
                        b.op("vector", lambda E, ST=ST: E.tensor_scalar(
                            ii_[0][:, :], ii_[0][:, :], ST, None, ALU.mult),
                            reads=[K("ii", 0)], writes=[K("ii", 0)])
                        b.op("vector", lambda E: E.tensor_tensor(
                            ii_[2][:, :], ii_[2][:, :], ii_[0][:, :],
                            ALU.add),
                            reads=[K("ii", 2), K("ii", 0)], writes=[K("ii", 2)])
                        b.op("vector", lambda E, r=r: E.tensor_scalar(
                            pa_[r][:, :], pa_[r][:, :], float(CAPR), None,
                            ALU.is_ge),
                            reads=[K("pa", r)], writes=[K("pa", r)])
                        b.op("vector", lambda E, r=r: E.tensor_copy(
                            ii_[3][:, :], pa_[r][:, :]),
                            reads=[K("pa", r)], writes=[K("ii", 3)])
                        b.op("vector", lambda E: E.tensor_scalar(
                            ii_[3][:, :], ii_[3][:, :], SLOT_TOT, None,
                            ALU.mult),
                            reads=[K("ii", 3)], writes=[K("ii", 3)])
                        b.op("vector", lambda E: E.tensor_tensor(
                            ii_[2][:, :], ii_[2][:, :], ii_[3][:, :],
                            ALU.add),
                            reads=[K("ii", 2), K("ii", 3)], writes=[K("ii", 2)])
                        b.op("vector", lambda E, r=r: E.tensor_scalar_min(
                            si_[r][:, :], ii_[2][:, :], SLOT_TOT),
                            reads=[K("ii", 2)], writes=[K("si", r)])

                    b.op("sync", lambda E: E.dma_start(
                        out=igbd_t.ap()[0:SLOT_TOT, :].rearrange("(p a) v -> p (a v)", p=128),
                        in_=oobs[:, :]),
                        reads=["oobs"], writes=[("igbd",)], dma=True,
                        dsem="igi")
                    for tc in range(C8):
                        for r in range(2):
                            if DBG["no_igs"]:
                                continue
                            b.op("gpsimd", lambda E, tc=tc, r=r:
                                 E.indirect_dma_start(
                                     igbd_t.ap(), IndirectOffsetOnAxis(
                                         si_[r][:, tc:tc + 1], 0),
                                     sval[r][:, 2 * tc:2 * tc + 2], None),
                                 reads=[("igbd",), K("si", r), K("sval", r)],
                                 dma=True, dsem=f"igs{r}")
                    b.op("sync", lambda E: E.dma_start(
                        out=idg[:, :].rearrange("p (a v) -> p a v", v=2),
                        in_=igbd_t.ap()[0:SLOT_TOT, :].rearrange("(p a) v -> p a v", p=128)),
                        reads=[("igbd",)], writes=[("igbd",), K("idg")],
                        dma=True, dsem="igr")
                    b.op("vector", lambda E: E.tensor_copy(
                        ido[:, :].rearrange("p (a one) -> p a one", one=1),
                        idg[:, :].rearrange("p (a v) -> p a v", v=2)
                        [:, :, 0:1]),
                        reads=[K("idg")], writes=[K("ido")])

                    e = E8
                    nfc = DFF // FCC
                    mpc = FCC // 128
                    for c2 in range(nchn):
                        for fcc in range(nfc):
                            fb_ = fcb[fcc % 2]
                            fbk = K("fcb", fcc % 2)
                            stream_fc(e, fcc, fb_, fbk)
                            for m in range(mpc):
                                mpt = fcc * mpc + m
                                pp = mpt % 2
                                hPk = K("hP", pp)
                                for kpt in range(DP):
                                    b.op("tensor", lambda E, fb_=fb_, m=m,
                                         c2=c2, kpt=kpt, pp=pp: E.matmul(
                                             hP[pp][:, :],
                                             fb_[:, kpt * FCC + m * 128:
                                                 kpt * FCC + (m + 1) * 128],
                                             xln[:, kpt * TILE + c2 * NCH:
                                                 kpt * TILE + (c2 + 1) * NCH],
                                             start=(kpt == 0),
                                             stop=(kpt == DP - 1)),
                                         reads=[fbk, K("xln", kpt)],
                                         writes=[hPk])
                                hb = h1[pp]
                                b.op("scalar", lambda E, hb=hb, pp=pp:
                                     E.activation(hb[:, 0:NCH],
                                                  hP[pp][:, :], AF.Relu),
                                     reads=[hPk], writes=[K("h1", pp)])
                                b.op("vector", lambda E, hb=hb, mpt=mpt:
                                     E.tensor_tensor(
                                         hg[:, mpt * NCH:(mpt + 1) * NCH],
                                         hb[:, 0:NCH], hb[:, 0:NCH],
                                         ALU.mult),
                                     reads=[K("h1", pp)],
                                     writes=[K("hg", mpt)])
                        for mo in range(DP):
                            pp = mo % 2
                            ogPk = K("ogP", pp)
                            pb_ = pjb[pp]
                            pbk = K("pjb", pp)
                            stream_pj(e, mo, pb_, pbk, pp)
                            for kk in range(DFP):
                                b.op("tensor", lambda E, pb_=pb_, kk=kk,
                                     pp=pp: E.matmul(
                                         ogP[pp][:, 0:NCH],
                                         pb_[:, kk * 128:(kk + 1) * 128],
                                         hg[:, kk * NCH:(kk + 1) * NCH],
                                         start=(kk == 0),
                                         stop=(kk == DFP - 1)),
                                     reads=[pbk, K("hg", kk)],
                                     writes=[ogPk])
                            t0 = tt0 + c2 * NCH
                            b.op("vector", lambda E, mo=mo, t0=t0, pp=pp:
                                 E.scalar_tensor_tensor(
                                     xsl(mo, t0, t0 + NCH),
                                     ogP[pp][:, 0:NCH], 0.5,
                                     xsl(mo, t0, t0 + NCH),
                                     ALU.mult, ALU.add),
                                 reads=[ogPk, xk(mo, t0 // XBLK)], writes=[xk(mo, t0 // XBLK)])

                    for e in range(E8):
                        epar = e % 2
                        for g in range(GPE):
                            if DBG["no_gx"]:
                                continue
                            b.op("gpsimd", lambda E, e=e, g=g, epar=epar:
                                 E.indirect_dma_start(
                                     xgog[epar][:, g * D:(g + 1) * D], None,
                                     xtmd_t.ap(), IndirectOffsetOnAxis(
                                         ido[:, e * GPE + g:
                                             e * GPE + g + 1], 0)),
                                 reads=[("xtmd",), K("ido"),
                                        K("xgog", epar, g)] +
                                       [("xtmd", "dump", gg, off)
                                        for gg in range(NDUMP)
                                        for off in range(0, D,
                                                         max(SC, NCH))],
                                 writes=[K("xgog", epar, g)], dma=True,
                                 dsem=f"gx{epar}{g}")
                        for g in range(GPE):
                            for q in range(DP // TQ):
                                par = (g * (DP // TQ) + q) % 2
                                for j in range(TQ):
                                    pt = q * TQ + j
                                    b.op("tensor", lambda E, g=g, pt=pt, j=j,
                                         par=par, epar=epar: E.transpose(
                                             trP[par][:, j * 128:
                                                      (j + 1) * 128],
                                             xgog[epar][:, g * D + pt * 128:
                                                  g * D + (pt + 1) * 128],
                                             idb[:, :]),
                                         reads=[K("xgog", epar, g), "idb"],
                                         writes=[K("trP", par)])
                                b.op("scalar", lambda E, g=g, q=q, par=par:
                                     E.copy(
                                         xg[:, :].rearrange(
                                             "p (k s) -> p k s", k=DP)
                                         [:, q * TQ:(q + 1) * TQ,
                                          g * 128:(g + 1) * 128],
                                         trP[par][:, 0:TQ * 128].rearrange(
                                             "p (k s) -> p k s", k=TQ)),
                                     reads=[K("trP", par)],
                                     writes=[K("xg")])
                        for fcc in range(nfc):
                            fb_ = fcb[fcc % 2]
                            fbk = K("fcb", fcc % 2)
                            stream_fc(e, fcc, fb_, fbk)
                            for m in range(mpc):
                                mpt = fcc * mpc + m
                                pp = mpt % 2
                                hPk = K("hP", pp)
                                for kpt in range(DP):
                                    b.op("tensor", lambda E, fb_=fb_, m=m,
                                         kpt=kpt, pp=pp: E.matmul(
                                             hP[pp][:, 0:CAP2],
                                             fb_[:, kpt * FCC + m * 128:
                                                 kpt * FCC + (m + 1) * 128],
                                             xg[:, kpt * CAP2:
                                                (kpt + 1) * CAP2],
                                             start=(kpt == 0),
                                             stop=(kpt == DP - 1)),
                                         reads=[fbk, K("xg")],
                                         writes=[hPk])
                                hb = h1[pp]
                                b.op("scalar", lambda E, hb=hb, pp=pp:
                                     E.activation(hb[:, 0:CAP2],
                                                  hP[pp][:, 0:CAP2], AF.Relu),
                                     reads=[hPk], writes=[K("h1", pp)])
                                b.op("vector", lambda E, hb=hb, mpt=mpt:
                                     E.tensor_tensor(
                                         hg[:, mpt * NCH:mpt * NCH + CAP2],
                                         hb[:, 0:CAP2], hb[:, 0:CAP2],
                                         ALU.mult),
                                     reads=[K("h1", pp)],
                                     writes=[K("hg", mpt)])
                        for mo in range(DP):
                            pp = mo % 2
                            ogPk = K("ogP", pp)
                            pb_ = pjb[pp]
                            pbk = K("pjb", pp)
                            stream_pj(e, mo, pb_, pbk, pp)
                            for kk in range(DFP):
                                b.op("tensor", lambda E, pb_=pb_, kk=kk,
                                     pp=pp: E.matmul(
                                         ogP[pp][:, 0:CAP2],
                                         pb_[:, kk * 128:(kk + 1) * 128],
                                         hg[:, kk * NCH:kk * NCH + CAP2],
                                         start=(kk == 0),
                                         stop=(kk == DFP - 1)),
                                     reads=[pbk, K("hg", kk)],
                                     writes=[ogPk])
                            b.op("scalar", lambda E, mo=mo, pp=pp: E.copy(
                                ogf[:, mo * CAP2:(mo + 1) * CAP2],
                                ogP[pp][:, 0:CAP2]),
                                reads=[ogPk], writes=[K("xg")])
                        for g in range(GPE):
                            for q in range(DP // TQ):
                                par = (g * (DP // TQ) + q) % 2
                                for j in range(TQ):
                                    pt = q * TQ + j
                                    b.op("tensor", lambda E, g=g, pt=pt, j=j,
                                         par=par: E.transpose(
                                             trP[par][:, j * 128:
                                                      (j + 1) * 128],
                                             ogf[:, pt * CAP2 + g * 128:
                                                 pt * CAP2 + (g + 1) * 128],
                                             idb[:, :]),
                                         reads=[K("xg"), "idb"],
                                         writes=[K("trP", par)])
                                gc = idg[:, (e * GPE + g) * 2 + 1:
                                         (e * GPE + g) * 2 + 2]
                                b.op("vector", lambda E, g=g, q=q, par=par,
                                     gc=gc, epar=epar: E.tensor_scalar(
                                         xgog[epar][:, g * D + q * TQ * 128:
                                              g * D + (q + 1) * TQ * 128],
                                         trP[par][:, 0:TQ * 128],
                                         gc[:, 0:1], None, ALU.mult),
                                     reads=[K("trP", par), K("idg")],
                                     writes=[K("xgog", epar, g)])
                        for g in range(GPE):
                            if DBG["no_dsc"]:
                                continue
                            dst = dltA_t if g < NG else dltB_t
                            dk = ("dltA",) if g < NG else ("dltB",)
                            b.op("gpsimd", lambda E, e=e, g=g, dst=dst,
                                 epar=epar: E.indirect_dma_start(
                                     dst.ap(), IndirectOffsetOnAxis(
                                         ido[:, e * GPE + g:
                                             e * GPE + g + 1], 0),
                                     xgog[epar][:, g * D:(g + 1) * D], None),
                                 reads=[dk, K("ido"), K("xgog", epar, g)],
                                 dma=True, dsem=f"dsc{epar}{g}")

                    for dt_, dkk, dsm in ((dltA_t, ("dltA",), "dra"),
                                          (dltB_t, ("dltB",), "drb")):
                        b.op("sync", lambda E, dt_=dt_: E.dma_start(
                            out=xtm[:, :].rearrange("p (c d) -> p c d", c=C8),
                            in_=dt_.ap()[0:TILE, :].rearrange(
                                "(p c) d -> p c d", p=128)),
                            reads=[dkk], writes=[dkk, K("xtm")],
                            dma=True, dsem=dsm)
                        for pt in range(DP):
                            for cq in range(C8 // CQ):
                                par = (pt * (C8 // CQ) + cq) % 2
                                for j in range(CQ):
                                    c = cq * CQ + j
                                    b.op("tensor", lambda E, c=c, pt=pt, j=j,
                                         par=par: E.transpose(
                                             trP[par][:, j * 128:
                                                      (j + 1) * 128],
                                             xtm[:, c * D + pt * 128:
                                                 c * D + (pt + 1) * 128],
                                             idb[:, :]),
                                         reads=[K("xtm"), "idb"],
                                         writes=[K("trP", par)])
                                t0 = tt0 + cq * CQ * 128
                                b.op("vector", lambda E, pt=pt, t0=t0,
                                     par=par: E.scalar_tensor_tensor(
                                         xsl(pt, t0, t0 + CQ * 128),
                                         trP[par][:, 0:CQ * 128], 0.5,
                                         xsl(pt, t0, t0 + CQ * 128),
                                         ALU.mult, ALU.add),
                                     reads=[K("trP", par),
                                            xk(pt, t0 // XBLK)],
                                     writes=[xk(pt, t0 // XBLK)])
                b.barrier()

        def attn_phase(tag):
            at_ctx = ExitStack()
            with at_ctx:
                fs = lambda nm, sh, dt: at_ctx.enter_context(
                    nc.sbuf_tensor(tag + nm, sh, dt))
                fp = lambda nm, sh, dt: at_ctx.enter_context(
                    nc.psum_tensor(tag + nm, sh, dt))
                K = lambda *a: (tag,) + a

                ws = {}
                for wi, (nm, dram) in enumerate([("wq", wqd), ("wk", wkd),
                                                 ("wv", wvd), ("wo", wod)]):
                    ws[nm] = fs(nm, [128, DP * D], BF16)
                    ck = ("watt", nm)
                    if ck not in weight_cached:
                        weight_cached.add(ck)
                        PW = NCH
                        mp = max(1, D // PW)
                        for pc in range(DP * D // PW):
                            kk_, mo_ = pc // mp, (pc % mp) * PW
                            ce = ("vector", "scalar", "gpsimd")[pc % 3]
                            b.op("sync", lambda E, dram=dram, kk_=kk_,
                                 mo_=mo_, PW=PW: E.dma_start(
                                     out=aevt[:, 0:PW],
                                     in_=dram.rearrange(
                                         "(k p) m -> p k m", p=128)
                                     [:, kk_, mo_:mo_ + PW]),
                                 reads=[K("aevt")], writes=[K("aevt")],
                                 dma=True, dsem="wst")
                            b.op(ce, lambda E, s=ws[nm], pc=pc, PW=PW:
                                 ecast(E,
                                     s[:, pc * PW:(pc + 1) * PW],
                                     aevt[:, 0:PW]),
                                 reads=[K("aevt")], writes=[K(nm)])
                        b.op("sync", lambda E, s=ws[nm], wi=wi: E.dma_start(
                            out=awtb[wi], in_=s[:, :]),
                            reads=[K(nm)], writes=[("awtb", wi)],
                            dma=True, dsem="wsw")
                    else:
                        b.op("sync", lambda E, s=ws[nm], wi=wi: E.dma_start(
                            out=s[:, :], in_=awtb[wi]),
                            reads=[("awtb", wi)], writes=[K(nm)],
                            dma=True, dsem="w_" + nm)

                xlnc = fs("xlnc", [128, DP * CH], BF16)
                bcMc = fs("bcMc", [128, CH], F32)
                bcRc = fs("bcRc", [128, CH], F32)
                qb = fs("qb", [128, DP * CH], BF16)
                fb = fs("fb", [128, DP * CH], BF16)
                vb = fs("vb", [128, DP * CH], BF16)
                ob = fs("ob", [128, DP * CH], BF16)
                lfb = fs("lfb", [128, DP * CH], F32)
                cb_ = fs("cb_", [128, DP * CH], F32)
                QB = fs("QB", [128, DP * CH], BF16)
                qRb = fs("qRb", [128, DP * CH], BF16)
                son = fs("son", [128, DP * CH], BF16)
                rbc = fs("rbc", [128, CH], F32)
                rrow = fs("rrow", [1, CH], F32)
                Sb_ = fs("Sb_", [128, DP * 128], BF16)
                qt = [fs(f"qt{i}", [128, SC], BF16) for i in range(2)]
                kt = [fs(f"kt{i}", [128, SC], BF16) for i in range(2)]
                kh = [fs(f"kh{i}", [128, SC], BF16) for i in range(2)]
                atm = [fs(f"atm{i}", [SC, SC], BF16) for i in range(2)]
                vk = [fs(f"vk{i}", [SC, 256], BF16) for i in range(2)]
                rcol = [fs(f"rcol{i}", [128, 1], F32) for i in range(2)]
                aevt = fs("aevt", [128, NCH], F32)

                qkvP = [fp(f"qkvP{i}", [128, max(CH, NCH)], F32)
                        for i in range(2)]
                glaA = [fp(f"glaA{i}", [128, 2 * SC], F32) for i in range(2)]
                glaT = [fp(f"glaT{i}", [SC, 2 * 128], BF16) for i in range(2)]
                glaS = [fp(f"glaS{i}", [128, 128], F32) for i in range(2)]

                nks = (NCH + CH - 1) // CH
                ln_stats(tag, [qb[:, 0:NCH], vb[:, 0:NCH]],
                         [fb[:, 0:NCH], ob[:, 0:NCH]],
                         [[K("qb", i) for i in range(nks)],
                          [K("vb", i) for i in range(nks)]],
                         [[K("fb", i) for i in range(nks)],
                          [K("ob", i) for i in range(nks)]],
                         qkvP[0], K("qkvP", 0), qkvP[1], K("qkvP", 1),
                         aevt, K("aevt"), eps5)

                b.op("gpsimd", lambda E: E.memset(Sb_[:, :], 0.0),
                     writes=[K("Sb_", h) for h in range(DP)])

                for ch in range(N // CH):
                    t0 = ch * CH
                    b.op("tensor", lambda E, t0=t0: E.matmul(
                        qkvP[0][:, 0:CH], orb[:, :], statA[0:1, t0:t0 + CH],
                        start=True, stop=True),
                        reads=["statA", "orb"], writes=[K("qkvP", 0)])
                    b.op("tensor", lambda E, t0=t0: E.matmul(
                        qkvP[1][:, 0:CH], orb[:, :], statB[0:1, t0:t0 + CH],
                        start=True, stop=True),
                        reads=["statB", "orb"], writes=[K("qkvP", 1)])
                    b.op("vector", lambda E: E.tensor_copy(
                        bcMc[:, :], qkvP[0][:, 0:CH]),
                        reads=[K("qkvP", 0)], writes=[K("bcMc")])
                    b.op("vector", lambda E: E.tensor_copy(
                        bcRc[:, :], qkvP[1][:, 0:CH]),
                        reads=[K("qkvP", 1)], writes=[K("bcRc")])
                    for pt in range(DP):
                        b.op("vector", lambda E, pt=pt, t0=t0:
                             E.tensor_tensor(aevt[:, 0:CH],
                                             xsl(pt, t0, t0 + CH),
                                             bcMc[:, :], ALU.subtract),
                             reads=xks(pt, t0, t0 + CH) + [K("bcMc")], writes=[K("aevt")])
                        b.op("vector", lambda E, pt=pt: E.tensor_tensor(
                            xlnc[:, pt * CH:(pt + 1) * CH], aevt[:, 0:CH],
                            bcRc[:, :], ALU.mult),
                            reads=[K("aevt"), K("bcRc")],
                            writes=[K("xlnc", pt)])
                    for nm, dst, dstk, act in [("wq", qb, "qb", AF.Silu),
                                               ("wk", fb, "fb", AF.Sigmoid),
                                               ("wv", vb, "vb", AF.Copy)]:
                        w = ws[nm]
                        for mo in range(DP):
                            pp = (mo // 2) % 2
                            co = (mo % 2) * CH
                            for kpt in range(DP):
                                b.op("tensor", lambda E, w=w, mo=mo, kpt=kpt,
                                     pp=pp, co=co: E.matmul(
                                         qkvP[pp][:, co:co + CH],
                                         w[:, kpt * D + mo * 128:
                                           kpt * D + (mo + 1) * 128],
                                         xlnc[:, kpt * CH:(kpt + 1) * CH],
                                         start=(kpt == 0),
                                         stop=(kpt == DP - 1)),
                                     reads=[K(nm), K("xlnc", kpt)],
                                     writes=[K("qkvP", pp)])
                            if mo % 2 == 1:
                                b.op("scalar", lambda E, dst=dst, mo=mo,
                                     pp=pp, act=act: E.activation(
                                         dst[:, (mo - 1) * CH:(mo + 1) * CH],
                                         qkvP[pp][:, 0:2 * CH], act),
                                     reads=[K("qkvP", pp)],
                                     writes=[K(dstk, mo - 1), K(dstk, mo)])
                    allf = [K("fb", mo) for mo in range(DP)]
                    alll = [K("lfb", mo) for mo in range(DP)]
                    b.op("scalar", lambda E: E.activation(
                        lfb[:, :], fb[:, :], AF.Sigmoid),
                        reads=allf, writes=alll)
                    b.op("scalar", lambda E: E.activation(
                        lfb[:, :], lfb[:, :], AF.Ln),
                        reads=alll, writes=alll)
                    mid = SC // 2
                    EMH = float(np.exp(-44.36))
                    allc = [K("cb", h) for h in range(DP)]
                    for h in range(DP):
                        for sc in range(NSC):
                            o0 = h * CH + sc * SC
                            b.op("vector", lambda E, o0=o0:
                                 E.tensor_tensor_scan(
                                     cb_[:, o0:o0 + SC], lfb[:, o0:o0 + SC],
                                     zcol[:, 0:SC], 0.0, ALU.add, ALU.add),
                                 reads=[K("lfb", h), "zcol"],
                                 writes=[K("cb", h)])
                    b.op("scalar", lambda E: E.activation(
                        QB[:, :], cb_[:, :], AF.Exp, scale=-1.0,
                        bias=hcolN[:, 0:1]),
                        reads=allc + ["hcolN"], writes=[K("QB")])
                    alll = [K("lfb", mo) for mo in range(DP)]
                    b.op("scalar", lambda E: E.activation(
                        lfb[:, :], cb_[:, :], AF.Exp, bias=hcolP[:, 0:1]),
                        reads=allc + ["hcolP"], writes=alll)
                    b.op("vector", lambda E: E.tensor_tensor(
                        qRb[:, :], qb[:, :], lfb[:, :], ALU.mult),
                        reads=[K("qb", h) for h in range(DP)] + alll,
                        writes=[K("qRb")])
                    b.op("vector", lambda E: E.tensor_tensor(
                        cb_[:, :], fb[:, :], QB[:, :], ALU.mult),
                        reads=[K("fb", h) for h in range(DP)] + [K("QB")],
                        writes=allc)
                    b.op("vector", lambda E: E.tensor_tensor(
                        QB[:, :], QB[:, :], cb_[:, :], ALU.subtract),
                        reads=allc + [K("QB")], writes=[K("QB")])
                    for sc in range(NSC):
                        for h in range(DP):
                            i = h % 2
                            o0 = h * CH + sc * SC
                            rmid = lfb[:, o0 + mid:o0 + mid + 1]
                            rend = lfb[:, o0 + SC - 1:o0 + SC]
                            b.op("vector", lambda E, rmid=rmid, i=i:
                                 E.reciprocal(rcol[i][:, :], rmid),
                                 reads=[K("lfb", h)], writes=[K("rcol", i)])
                            b.op("vector", lambda E, o0=o0, i=i:
                                 E.tensor_scalar(
                                     qt[i][:, :], qRb[:, o0:o0 + SC],
                                     rcol[i][:, 0:1], None, ALU.mult),
                                 reads=[K("qRb"), K("rcol", i)],
                                 writes=[K("qt", i)])
                            b.op("vector", lambda E, o0=o0, i=i, rmid=rmid:
                                 E.tensor_scalar(
                                     kt[i][:, :], QB[:, o0:o0 + SC],
                                     rmid, None, ALU.mult),
                                 reads=[K("QB"), K("lfb", h)],
                                 writes=[K("kt", i)])
                            b.op("vector", lambda E, o0=o0, i=i, rend=rend:
                                 E.tensor_scalar(
                                     kh[i][:, :], QB[:, o0:o0 + SC],
                                     rend, None, ALU.mult),
                                 reads=[K("QB"), K("lfb", h)],
                                 writes=[K("kh", i)])
                            b.op("tensor", lambda E, i=i: E.matmul(
                                glaA[i][:, 0:SC], kt[i][:, :], qt[i][:, :],
                                start=True, stop=True),
                                reads=[K("kt", i), K("qt", i)],
                                writes=[K("glaA", i)])
                            b.op("vector", lambda E, i=i: E.tensor_tensor(
                                atm[i][:, :], glaA[i][0:SC, 0:SC], mkb[:, :],
                                ALU.mult),
                                reads=[K("glaA", i), "mkb"],
                                writes=[K("atm", i)])
                            b.op("tensor", lambda E, o0=o0, i=i: E.transpose(
                                glaT[i][:, 0:128], vb[:, o0:o0 + SC],
                                idb[:, :]),
                                reads=[K("vb", h), "idb"],
                                writes=[K("glaT", i)])
                            b.op("tensor", lambda E, i=i: E.transpose(
                                glaT[i][:, 128:256], kh[i][:, :], idb[:, :]),
                                reads=[K("kh", i), "idb"],
                                writes=[K("glaT", i)])
                            b.op("scalar", lambda E, i=i: E.copy(
                                vk[i][:, :], glaT[i][0:SC, 0:256]),
                                reads=[K("glaT", i)], writes=[K("vk", i)])
                            b.op("tensor", lambda E, h=h, i=i, o0=o0:
                                 E.matmul(
                                     glaA[i][:, SC:2 * SC],
                                     Sb_[:, h * 128:(h + 1) * 128],
                                     qRb[:, o0:o0 + SC],
                                     start=True, stop=False),
                                 reads=[K("Sb_", h), K("qRb")],
                                 writes=[K("glaA", i)])
                            b.op("tensor", lambda E, i=i: E.matmul(
                                glaA[i][:, SC:2 * SC], vk[i][:, 0:128],
                                atm[i][:, :], start=False, stop=True),
                                reads=[K("vk", i), K("atm", i)],
                                writes=[K("glaA", i)])
                            b.op("scalar", lambda E, o0=o0, i=i: E.copy(
                                ob[:, o0:o0 + SC], glaA[i][:, SC:2 * SC]),
                                reads=[K("glaA", i)], writes=[K("ob", h)])
                            b.op("tensor", lambda E, i=i: E.matmul(
                                glaS[i][:, :], vk[i][:, 128:256],
                                vk[i][:, 0:128], start=True, stop=True),
                                reads=[K("vk", i)], writes=[K("glaS", i)])
                            b.op("vector", lambda E, h=h, i=i, rend=rend:
                                 E.tensor_scalar(
                                     Sb_[:, h * 128:(h + 1) * 128],
                                     Sb_[:, h * 128:(h + 1) * 128],
                                     rend, EMH, ALU.mult, ALU.mult),
                                 reads=[K("Sb_", h), K("lfb", h)],
                                 writes=[K("Sb_", h)])
                            b.op("vector", lambda E, h=h, i=i:
                                 E.scalar_tensor_tensor(
                                     Sb_[:, h * 128:(h + 1) * 128],
                                     glaS[i][:, :], EMH,
                                     Sb_[:, h * 128:(h + 1) * 128],
                                     ALU.mult, ALU.add),
                                 reads=[K("glaS", i), K("Sb_", h)],
                                 writes=[K("Sb_", h)])
                    b.op("vector", lambda E: E.tensor_tensor(
                        son[:, :], ob[:, :], ob[:, :], ALU.mult),
                        reads=[K("ob", hh) for hh in range(DP)],
                        writes=[K("son")])
                    for kpt in range(DP):
                        b.op("tensor", lambda E, kpt=kpt: E.matmul(
                            qkvP[0][0:1, 0:CH], ocb[:, :],
                            son[:, kpt * CH:(kpt + 1) * CH],
                            start=(kpt == 0), stop=(kpt == DP - 1)),
                            reads=[K("son"), "ocb"], writes=[K("qkvP", 0)])
                    b.op("scalar", lambda E: E.activation(
                        rrow[0:1, :], qkvP[0][0:1, 0:CH], AF.Ln,
                        scale=1.0 / D, bias=eps6[0:1, 0:1]),
                        reads=[K("qkvP", 0), "eps6"], writes=[K("rrow")])
                    b.op("scalar", lambda E: E.activation(
                        rrow[0:1, :], rrow[0:1, :], AF.Exp, scale=-0.5),
                        reads=[K("rrow")], writes=[K("rrow")])
                    b.op("tensor", lambda E: E.matmul(
                        qkvP[1][:, 0:CH], orf[:, :], rrow[0:1, :],
                        start=True, stop=True),
                        reads=[K("rrow"), "orf"], writes=[K("qkvP", 1)])
                    b.op("vector", lambda E: E.tensor_copy(
                        rbc[:, :], qkvP[1][:, 0:CH]),
                        reads=[K("qkvP", 1)], writes=[K("rbc")])
                    for mo in range(DP):
                        b.op("vector", lambda E, mo=mo: E.scalar_tensor_tensor(
                            son[:, mo * CH:(mo + 1) * CH],
                            ob[:, mo * CH:(mo + 1) * CH],
                            nws[:, mo:mo + 1], rbc[:, :],
                            ALU.mult, ALU.mult),
                            reads=[K("ob", mo), "nws", K("rbc")],
                            writes=[K("son")])
                    for mo in range(DP):
                        pp = (mo // 2) % 2
                        co = (mo % 2) * CH
                        for kpt in range(DP):
                            b.op("tensor", lambda E, mo=mo, kpt=kpt, pp=pp,
                                 co=co: E.matmul(
                                     qkvP[pp][:, co:co + CH],
                                     ws["wo"][:, kpt * D + mo * 128:
                                              kpt * D + (mo + 1) * 128],
                                     son[:, kpt * CH:(kpt + 1) * CH],
                                     start=(kpt == 0), stop=(kpt == DP - 1)),
                                 reads=[K("wo"), K("son")],
                                 writes=[K("qkvP", pp)])
                        if mo % 2 == 1:
                            xpair = xs[:, :].rearrange(
                                "p (k n) -> p k n", k=DP)[
                                :, mo - 1:mo + 1, t0:t0 + CH]
                            b.op("vector", lambda E, xpair=xpair, pp=pp:
                                 E.tensor_tensor(
                                     xpair, qkvP[pp][:, 0:2 * CH],
                                     xpair, ALU.add),
                                 reads=[K("qkvP", pp), xk(mo - 1, t0 // XBLK), xk(mo, t0 // XBLK)],
                                 writes=[xk(mo - 1, t0 // XBLK), xk(mo, t0 // XBLK)])
                b.barrier()

        for bb in range(B):
            b.op("sync", lambda E, bb=bb: E.dma_start(
                out=xs[:, :].rearrange("p (k n) -> p k n", k=DP),
                in_=xd.rearrange("b (k p) n -> b p k n", p=128)[bb]),
                writes=[k for pt in range(DP) for k in xk(pt)], dma=True, dsem="xload")
            ffn_phase(f"F{bb}a_")
            if not DBG["ffn_only"]:
                attn_phase(f"A{bb}_")
                ffn_phase(f"F{bb}b_")
            b.op("sync", lambda E, bb=bb: E.dma_start(
                out=outd.rearrange("b (k p) n -> b p k n", p=128)[bb],
                in_=xs[:, :].rearrange("p (k n) -> p k n", k=DP)),
                reads=[k for pt in range(DP) for k in xk(pt)], dma=True, dsem="xstore")
        b.barrier()
        b.play(top)
    return nc


def _oobs_const(cfg):
    E8 = cfg["E"]
    NG = cfg["CAPR"] // 128
    ST = 2 * E8 * NG
    a = np.arange(ST)
    epar = (a // (2 * NG)) % 2
    gg = epar * (2 * NG) + a % (2 * NG)
    ids = cfg["TILE"] + gg.astype(np.float32)
    pat = np.zeros((ST, 2), dtype=np.float32)
    pat[:, 0] = ids
    return np.tile(pat.reshape(1, ST * 2), (128, 1))


def prep_inputs(L0_attn_x, L0_attn_Wq, L0_attn_Wk, L0_attn_Wv, L0_attn_Wo,
                L0_attn_norm_w, L0_ffn_router_w, L0_ffn_w_fc, L0_ffn_w_proj,
                L0_ffn_shared_fc, L0_ffn_shared_proj, cfg, n_cores=8):
    B, N, D, DFF, E8 = cfg["B"], cfg["N"], cfg["D"], cfg["DFF"], cfg["E"]
    SC = cfg["SC"]
    cc = np.ascontiguousarray
    ascc = cc
    x = np.asarray(L0_attn_x, dtype=np.float32)
    fctT = np.concatenate(
        [np.transpose(np.asarray(L0_ffn_w_fc, np.float32), (0, 2, 1)),
         np.transpose(np.asarray(L0_ffn_shared_fc, np.float32))[None]],
        axis=0)
    pjtT = np.concatenate(
        [np.transpose(np.asarray(L0_ffn_w_proj, np.float32), (0, 2, 1)),
         np.transpose(np.asarray(L0_ffn_shared_proj, np.float32))[None]],
        axis=0)
    common = {
        "wq": cc(np.asarray(L0_attn_Wq, np.float32).T),
        "wk": cc(np.asarray(L0_attn_Wk, np.float32).T),
        "wv": cc(np.asarray(L0_attn_Wv, np.float32).T),
        "wo": cc(np.asarray(L0_attn_Wo, np.float32).T),
        "nw": cc(np.asarray(L0_attn_norm_w, np.float32).reshape(D // 128, 128).T),
        "rt": cc(np.concatenate(
            [np.asarray(L0_ffn_router_w, np.float32).T,
             np.ones((D, 1), np.float32)], axis=1)),
        "fct": cc(fctT),
        "pjt": cc(pjtT),
        "identb": np.eye(128, dtype=BFNP),
        "identf": np.eye(128, dtype=np.float32),
        "onescb": np.ones((128, 1), dtype=BFNP),
        "onescf": np.ones((128, 1), dtype=np.float32),
        "onesrf": np.ones((1, 128), dtype=np.float32),
        "halfrf": np.full((1, 128), 0.5, dtype=np.float32),
        "halfrb": np.full((1, 128), 0.5, dtype=BFNP),
        "maskb": np.triu(np.ones((SC, SC), dtype=np.float32)).astype(BFNP),
        "onesrb": np.ones((1, 128), dtype=BFNP),
        "ecols": np.tile(np.arange(E8, dtype=np.float32),
                         (128, cfg["TILE"] // 128)),
        "tvl": (np.arange(128, dtype=np.float32)[:, None] *
                (cfg["TILE"] // 128) +
                np.arange(cfg["TILE"] // 128, dtype=np.float32)[None, :]),
        "oobs": _oobs_const(cfg),
    }
    in_maps = []
    for c in range(n_cores):
        m = dict(common)
        m["x"] = cc(np.transpose(x[c * B:(c + 1) * B], (0, 2, 1)))
        in_maps.append(m)
    return in_maps


def kernel(**inputs):
    cfg = REAL_CFG
    n_cores = 8
    in_maps = prep_inputs(n_cores=n_cores, cfg=cfg, **inputs)
    nc = build_program(cfg)
    res = run_bass_kernel_spmd(nc, in_maps, core_ids=list(range(n_cores)))
    outs = [np.transpose(r["out"], (0, 2, 1)) for r in res.results]
    return np.concatenate(outs, axis=0)


# version 22
# speedup vs baseline: 1.3182x; 1.1020x over previous
import numpy as np
import ml_dtypes
from contextlib import ExitStack

import concourse.bass as bass
import concourse.mybir as mybir
from concourse.bass_utils import run_bass_kernel_spmd
from concourse.bass_core_rust import IndirectOffsetOnAxis

F32 = mybir.dt.float32
BF16 = mybir.dt.bfloat16
I32 = mybir.dt.int32
AF = mybir.ActivationFunctionType
ALU = mybir.AluOpType
AX = mybir.AxisListType
BFNP = ml_dtypes.bfloat16

ENGINES = ("tensor", "vector", "scalar", "gpsimd", "sync")

DBG = {"no_igs": False, "no_gx": False, "no_dsc": False, "no_acc": False,
       "ffn_only": False}

REAL_CFG = dict(B=2, N=2048, D=1024, DFF=4096, E=8,
                TILE=1024, NCH=512, CH=256, SC=128, FCC=256, CAPR=256)


class Builder:

    def __init__(self, nc, ctx):
        self.nc = nc
        self.prog = {e: [] for e in ENGINES}
        self.semh = {}
        self.semc = {}
        self.floors = {e: {} for e in ENGINES}
        self.buf = {}
        self._ctx = ctx
        for name in ["s_tensor", "s_vector", "s_scalar", "s_gpsimd", "s_sync"]:
            self.semh[name] = ctx.enter_context(nc.semaphore(name))
            self.semc[name] = 0

    def op(self, eng, emit, reads=(), writes=(), dma=False, dsem=None):
        if dma:
            assert dsem is not None
            sem = "dma_" + dsem
            if sem not in self.semh:
                self.semh[sem] = self._ctx.enter_context(self.nc.semaphore(sem))
                self.semc[sem] = 0
            inc = 16
        else:
            sem = "s_" + eng
            inc = 1
        needs = {}

        def want(mark):
            s, v = mark
            if eng == "tensor" and s == "s_tensor":
                return
            if v > needs.get(s, 0):
                needs[s] = v

        for k in reads:
            st = self.buf.get(k)
            if st and st["w"]:
                want(st["w"])
        for k in writes:
            st = self.buf.get(k)
            if st:
                if st["w"]:
                    want(st["w"])
                for r in st["r"]:
                    want(r)
        lst = self.prog[eng]
        fl = self.floors[eng]
        for s in sorted(needs):
            v = needs[s]
            if fl.get(s, 0) >= v:
                continue
            fl[s] = v
            h = self.semh[s]
            lst.append(lambda E, h=h, v=v: E.wait_ge(h, v))
        self.semc[sem] += inc
        h = self.semh[sem]
        lst.append(lambda E, h=h, emit=emit, inc=inc: emit(E).then_inc(h, inc))
        mark = (sem, self.semc[sem])
        for k in writes:
            self.buf[k] = {"w": mark, "r": []}
        for k in reads:
            if k in writes:
                continue
            st = self.buf.setdefault(k, {"w": None, "r": []})
            st["r"].append(mark)
        return mark

    def barrier(self):
        targets = [(s, c) for s, c in self.semc.items() if c > 0]
        for e in ENGINES:
            fl = self.floors[e]
            for s, v in targets:
                if fl.get(s, 0) >= v:
                    continue
                fl[s] = v
                h = self.semh[s]
                self.prog[e].append(lambda E, h=h, v=v: E.wait_ge(h, v))
        self.buf.clear()

    def play(self, ctx):
        block = ctx.enter_context(self.nc.Block())
        prog = self.prog

        @block.tensor
        def _(E):
            for f in prog["tensor"]:
                f(E)

        @block.vector
        def _(E):
            for f in prog["vector"]:
                f(E)

        @block.scalar
        def _(E):
            for f in prog["scalar"]:
                f(E)

        @block.gpsimd
        def _(E):
            for f in prog["gpsimd"]:
                f(E)

        @block.sync
        def _(E):
            for f in prog["sync"]:
                f(E)


def build_program(cfg):
    B, N, D, DFF, E8 = cfg["B"], cfg["N"], cfg["D"], cfg["DFF"], cfg["E"]
    TILE, NCH, CH, SC, FCC = (cfg["TILE"], cfg["NCH"], cfg["CH"], cfg["SC"],
                              cfg["FCC"])
    CAPR = cfg["CAPR"]
    CAP2 = 2 * CAPR
    NG = CAPR // 128
    GPE = 2 * NG
    C8 = TILE // 128
    SLOT_TOT = (cfg["E"]) * CAP2
    assert CAPR % 128 == 0 and CAP2 <= NCH
    TQ = 4 if (D // 128) % 4 == 0 else D // 128
    CQ = 4 if C8 % 4 == 0 else min(4, C8)
    DP = D // 128
    DFP = DFF // 128
    NT = N // TILE
    NSC = CH // SC
    assert TILE % NCH == 0 and N % CH == 0 and CH % SC == 0
    assert DFF % FCC == 0 and FCC % 128 == 0 and N % NCH == 0

    nc = bass.Bass("TRN2", target_bir_lowering=False)

    xd = nc.dram_tensor("x", [B, D, N], F32, kind="ExternalInput").ap()
    wqd = nc.dram_tensor("wq", [D, D], F32, kind="ExternalInput").ap()
    wkd = nc.dram_tensor("wk", [D, D], F32, kind="ExternalInput").ap()
    wvd = nc.dram_tensor("wv", [D, D], F32, kind="ExternalInput").ap()
    wod = nc.dram_tensor("wo", [D, D], F32, kind="ExternalInput").ap()
    nwd = nc.dram_tensor("nw", [128, DP], F32, kind="ExternalInput").ap()
    rtd = nc.dram_tensor("rt", [D, E8 + 1], F32, kind="ExternalInput").ap()
    fct = nc.dram_tensor("fct", [E8 + 1, D, DFF], F32, kind="ExternalInput").ap()
    pjt = nc.dram_tensor("pjt", [E8 + 1, DFF, D], F32, kind="ExternalInput").ap()
    identb = nc.dram_tensor("identb", [128, 128], BF16, kind="ExternalInput").ap()
    identf = nc.dram_tensor("identf", [128, 128], F32, kind="ExternalInput").ap()
    onescb = nc.dram_tensor("onescb", [128, 1], BF16, kind="ExternalInput").ap()
    onescf = nc.dram_tensor("onescf", [128, 1], F32, kind="ExternalInput").ap()
    onesrf = nc.dram_tensor("onesrf", [1, 128], F32, kind="ExternalInput").ap()
    halfrf = nc.dram_tensor("halfrf", [1, 128], F32, kind="ExternalInput").ap()
    halfrb = nc.dram_tensor("halfrb", [1, 128], BF16, kind="ExternalInput").ap()
    maskbd = nc.dram_tensor("maskb", [SC, SC], BF16, kind="ExternalInput").ap()
    onesrb = nc.dram_tensor("onesrb", [1, 128], BF16, kind="ExternalInput").ap()
    ecolsd = nc.dram_tensor("ecols", [128, C8 * E8], F32, kind="ExternalInput").ap()
    tvld = nc.dram_tensor("tvl", [128, C8], F32, kind="ExternalInput").ap()
    oobsd = nc.dram_tensor("oobs", [128, SLOT_TOT * 2 // 128], F32,
                           kind="ExternalInput").ap()
    outd = nc.dram_tensor("out", [B, D, N], F32, kind="ExternalOutput").ap()
    NDUMP = 4 * (CAPR // 128)
    xtmd_t = nc.dram_tensor("xtmd", [TILE + NDUMP, D], BF16, kind="Internal")
    igbd_t = nc.dram_tensor("igbd", [SLOT_TOT + 1, 2], F32, kind="Internal")
    dltA_t = nc.dram_tensor("dltA", [TILE + NDUMP, D], BF16, kind="Internal")
    dltB_t = nc.dram_tensor("dltB", [TILE + NDUMP, D], BF16, kind="Internal")
    fctb = nc.dram_tensor("fctb", [E8 + 1, DFF // FCC, 128, (D // 128) * FCC],
                          BF16, kind="Internal").ap()
    pjtb = nc.dram_tensor("pjtb", [E8 + 1, D // 128, 128, (DFF // 128) * 128],
                          BF16, kind="Internal").ap()
    awtb = nc.dram_tensor("awtb", [4, 128, (D // 128) * D], BF16,
                          kind="Internal").ap()

    top = ExitStack()
    with top:
        b = Builder(nc, top)
        sb = lambda nm, sh, dt: top.enter_context(nc.sbuf_tensor(nm, sh, dt))

        xs = sb("xs", [128, DP * N], F32)
        statA = sb("statA", [1, N], BF16)
        statB = sb("statB", [1, N], BF16)
        idb = sb("idb", [128, 128], BF16)
        idf = sb("idf", [128, 128], F32)
        ocb = sb("ocb", [128, 1], BF16)
        orf = sb("orf", [1, 128], F32)
        hrf = sb("hrf", [1, 128], F32)
        hrb = sb("hrb", [1, 128], BF16)
        mkb = sb("mkb", [SC, SC], BF16)
        nws = sb("nws", [128, DP], F32)
        rtsF = sb("rtsF", [128, DP * (E8 + 1)], F32)
        rstdc = sb("rstdc", [128, N // 128], F32)
        ocf = sb("ocf", [128, 1], F32)
        sbc = sb("sbc", [128, E8], F32)
        srow = sb("srow", [1, E8], F32)
        zcol = sb("zcol", [128, max(SC, NCH)], F32)
        orb = sb("orb", [1, 128], BF16)
        ecols = sb("ecols_s", [128, C8 * E8], F32)
        tvl = sb("tvl_s", [128, C8], F32)
        oobs = sb("oobs_s", [128, SLOT_TOT * 2 // 128], F32)
        eps5 = sb("eps5", [128, 1], F32)
        eps6 = sb("eps6", [128, 1], F32)
        hcolP = sb("hcolP", [128, 1], F32)
        hcolN = sb("hcolN", [128, 1], F32)

        XBLK = min(512, N)

        def ecast(E, out_ap, in_ap):
            if hasattr(E, "tensor_copy"):
                return E.tensor_copy(out_ap, in_ap)
            return E.copy(out_ap, in_ap)

        def xk(pt, blk=None):
            if blk is None:
                return [("x", pt, i) for i in range(N // XBLK)]
            return ("x", pt, blk)

        def xks(pt, t0, t1):
            return [("x", pt, i) for i in range(t0 // XBLK,
                                                (t1 + XBLK - 1) // XBLK)]

        def xsl(pt, t0, t1):
            return xs[:, pt * N + t0:pt * N + t1]

        b.op("sync", lambda E: E.dma_start(out=idb[:, :], in_=identb),
             writes=["idb"], dma=True, dsem="idb")
        b.op("sync", lambda E: E.dma_start(out=idf[:, :], in_=identf),
             writes=["idf"], dma=True, dsem="idf")
        b.op("sync", lambda E: E.dma_start(out=ocb[:, :], in_=onescb),
             writes=["ocb"], dma=True, dsem="ocb")
        b.op("sync", lambda E: E.dma_start(out=ocf[:, :], in_=onescf),
             writes=["ocf"], dma=True, dsem="ocf")
        b.op("sync", lambda E: E.dma_start(out=orf[:, :], in_=onesrf),
             writes=["orf"], dma=True, dsem="orf")
        b.op("sync", lambda E: E.dma_start(out=hrf[:, :], in_=halfrf),
             writes=["hrf"], dma=True, dsem="hrf")
        b.op("sync", lambda E: E.dma_start(out=hrb[:, :], in_=halfrb),
             writes=["hrb"], dma=True, dsem="hrb")
        b.op("sync", lambda E: E.dma_start(out=mkb[:, :], in_=maskbd),
             writes=["mkb"], dma=True, dsem="mkb")
        b.op("sync", lambda E: E.dma_start(out=nws[:, :], in_=nwd),
             writes=["nws"], dma=True, dsem="nws")
        b.op("sync", lambda E: E.dma_start(
            out=rtsF[:, :].rearrange("p (k e) -> p k e", k=DP),
            in_=rtd.rearrange("(k p) e -> p k e", p=128)),
            writes=["rtsF"], dma=True, dsem="rts")
        b.op("gpsimd", lambda E: E.memset(zcol[:, :], 0.0), writes=["zcol"])
        b.op("sync", lambda E: E.dma_start(out=orb[:, :], in_=onesrb),
             writes=["orb"], dma=True, dsem="orb")
        b.op("sync", lambda E: E.dma_start(out=ecols[:, :], in_=ecolsd),
             writes=["ecols"], dma=True, dsem="ecols")
        b.op("sync", lambda E: E.dma_start(out=tvl[:, :], in_=tvld),
             writes=["tvl"], dma=True, dsem="tvl")
        b.op("sync", lambda E: E.dma_start(out=oobs[:, :], in_=oobsd),
             writes=["oobs"], dma=True, dsem="oobs")
        b.op("gpsimd", lambda E: E.memset(eps5[:, :], 1e-5), writes=["eps5"])
        b.op("gpsimd", lambda E: E.memset(eps6[:, :], 1e-6), writes=["eps6"])
        b.op("gpsimd", lambda E: E.memset(hcolP[:, :], 44.36), writes=["hcolP"])
        b.op("gpsimd", lambda E: E.memset(hcolN[:, :], -44.36), writes=["hcolN"])
        ZW = max(SC, NCH)
        for gg in range(NDUMP):
            for off in range(0, D, ZW):
                w = min(ZW, D - off)
                b.op("gpsimd", lambda E, gg=gg, off=off, w=w: E.dma_start(
                    out=xtmd_t.ap()[TILE + gg:TILE + gg + 1, off:off + w],
                    in_=zcol[0:1, 0:w]),
                    reads=["zcol"], writes=[("xtmd", "dump", gg, off)],
                    dma=True, dsem="xdump")

        def ln_stats(tag, xbB, sqB, xbK, sqK, psA, psAk, psB, psBk, rowt, rowtK,
                     eps, rowt2=None, rowt2K=None, rstdP=None, rstdPk=None):
            for c in range(N // NCH):
                t0 = c * NCH
                for pt in range(DP):
                    p = pt % 2
                    xb, sq = xbB[p], sqB[p]
                    kxb, ksq = xbK[p], sqK[p]
                    b.op("vector", lambda E, xb=xb, pt=pt, t0=t0:
                         E.tensor_copy(xb[:, 0:NCH], xsl(pt, t0, t0 + NCH)),
                         reads=xks(pt, t0, t0 + NCH), writes=kxb)
                    b.op("vector", lambda E, sq=sq, xb=xb: E.tensor_tensor(
                        sq[:, 0:NCH], xb[:, 0:NCH], xb[:, 0:NCH], ALU.mult),
                        reads=kxb, writes=ksq)
                    st, sp = (pt == 0), (pt == DP - 1)
                    b.op("tensor", lambda E, xb=xb, st=st, sp=sp: E.matmul(
                        psA[0:1, 0:NCH], ocb[:, :], xb[:, 0:NCH],
                        start=st, stop=sp),
                        reads=list(kxb) + ["ocb"], writes=[psAk])
                    b.op("tensor", lambda E, sq=sq, st=st, sp=sp: E.matmul(
                        psB[0:1, 0:NCH], ocb[:, :], sq[:, 0:NCH],
                        start=st, stop=sp),
                        reads=list(ksq) + ["ocb"], writes=[psBk])
                if rowt2 is None:
                    b.op("scalar", lambda E, t0=t0: E.activation(
                        statA[0:1, t0:t0 + NCH], psA[0:1, 0:NCH], AF.Copy,
                        scale=1.0 / D),
                        reads=[psAk], writes=["statA"])
                    b.op("scalar", lambda E, t0=t0: E.activation(
                        statB[0:1, t0:t0 + NCH], psB[0:1, 0:NCH], AF.Copy,
                        scale=1.0 / D),
                        reads=[psBk], writes=["statB"])
                    b.op("vector", lambda E, t0=t0: E.tensor_tensor(
                        rowt[0:1, 0:NCH], statA[0:1, t0:t0 + NCH],
                        statA[0:1, t0:t0 + NCH], ALU.mult),
                        reads=["statA"], writes=[rowtK])
                    b.op("vector", lambda E, t0=t0: E.tensor_tensor(
                        statB[0:1, t0:t0 + NCH], statB[0:1, t0:t0 + NCH],
                        rowt[0:1, 0:NCH], ALU.subtract),
                        reads=[rowtK, "statB"], writes=["statB"])
                    b.op("scalar", lambda E, t0=t0: E.activation(
                        statB[0:1, t0:t0 + NCH], statB[0:1, t0:t0 + NCH],
                        AF.Ln, bias=eps[0:1, 0:1]),
                        reads=["statB", "eps5"], writes=["statB"])
                    b.op("scalar", lambda E, t0=t0: E.activation(
                        statB[0:1, t0:t0 + NCH], statB[0:1, t0:t0 + NCH],
                        AF.Exp, scale=-0.5),
                        reads=["statB"], writes=["statB"])
                else:
                    b.op("scalar", lambda E, t0=t0: E.activation(
                        rowt2[0:1, 0:NCH], psA[0:1, 0:NCH], AF.Copy,
                        scale=1.0 / D),
                        reads=[psAk], writes=[rowt2K])
                    b.op("scalar", lambda E, t0=t0: E.activation(
                        statA[0:1, t0:t0 + NCH], rowt2[0:1, 0:NCH], AF.Copy),
                        reads=[rowt2K], writes=["statA"])
                    b.op("scalar", lambda E, t0=t0: E.activation(
                        rowt[0:1, 0:NCH], psB[0:1, 0:NCH], AF.Copy,
                        scale=1.0 / D),
                        reads=[psBk], writes=[rowtK])
                    b.op("vector", lambda E: E.scalar_tensor_tensor(
                        rowt2[0:1, 0:NCH], rowt2[0:1, 0:NCH], -1.0,
                        rowt2[0:1, 0:NCH], ALU.mult, ALU.mult),
                        reads=[rowt2K], writes=[rowt2K])
                    b.op("vector", lambda E: E.tensor_tensor(
                        rowt[0:1, 0:NCH], rowt[0:1, 0:NCH],
                        rowt2[0:1, 0:NCH], ALU.add),
                        reads=[rowtK, rowt2K], writes=[rowtK])
                    b.op("scalar", lambda E: E.activation(
                        rowt[0:1, 0:NCH], rowt[0:1, 0:NCH], AF.Ln,
                        bias=eps[0:1, 0:1]),
                        reads=[rowtK, "eps5"], writes=[rowtK])
                    b.op("scalar", lambda E: E.activation(
                        rowt[0:1, 0:NCH], rowt[0:1, 0:NCH], AF.Exp,
                        scale=-0.5),
                        reads=[rowtK], writes=[rowtK])
                    b.op("scalar", lambda E, t0=t0: E.activation(
                        statB[0:1, t0:t0 + NCH], rowt[0:1, 0:NCH], AF.Copy),
                        reads=[rowtK], writes=["statB"])
                    for cc_ in range(NCH // 128):
                        col = (t0 // 128) + cc_
                        b.op("tensor", lambda E, cc_=cc_, col=col:
                             E.transpose(
                                 rstdP[0:128, 2 * (col % 4):
                                       2 * (col % 4) + 1],
                                 rowt[0:1, cc_ * 128:(cc_ + 1) * 128],
                                 idf[0:1, 0:1]),
                             reads=[rowtK, "idf"], writes=[rstdPk])
                        b.op("scalar", lambda E, col=col: E.copy(
                            rstdc[:, col:col + 1],
                            rstdP[0:128, 2 * (col % 4):2 * (col % 4) + 1]),
                            reads=[rstdPk], writes=["rstdc"])

        weight_cached = set()

        def ffn_phase(tag):
            fc_ctx = ExitStack()
            with fc_ctx:
                fs = lambda nm, sh, dt: fc_ctx.enter_context(
                    nc.sbuf_tensor(tag + nm, sh, dt))
                fp = lambda nm, sh, dt: fc_ctx.enter_context(
                    nc.psum_tensor(tag + nm, sh, dt))
                K = lambda *a: (tag,) + a
                half = DFP // 2

                xln = fs("xln", [128, DP * TILE], BF16)
                xtm = fs("xtm", [128, C8 * D], BF16)
                hg = fs("hg", [128, DFP * NCH], BF16)
                xgog = [fs(f"xgog{i}", [128, GPE * D], BF16)
                        for i in range(2)]
                xg = fs("xg", [128, DP * CAP2], BF16)
                ogf = xg
                fcb = [fs(f"fcb{i}", [128, DP * FCC], BF16) for i in range(2)]
                pjb = [fs(f"pjb{i}", [128, DFP * 128], BF16) for i in range(2)]
                h1 = [fs(f"h1{i}", [128, NCH], BF16) for i in range(2)]
                evt = fs("evt", [128, NCH], F32)
                evt2 = fs("evt2", [128, NCH], F32)
                SEL = [xg[0:E8, i * TILE:(i + 1) * TILE] for i in range(2)]
                pos = fs("pos", [E8, TILE], F32)
                oha = [fs(f"oha{i}", [128, C8 * E8], F32) for i in range(2)]
                posm = fs("posm", [128, C8 * E8], F32)
                ea_ = [fs(f"ea{i}", [128, C8], F32) for i in range(2)]
                pa_ = [fs(f"pa{i}", [128, C8], F32) for i in range(2)]
                si_ = [fs(f"si{i}", [128, C8], I32) for i in range(2)]
                ii_ = [fs(f"ii{i}", [128, C8], I32) for i in range(4)]
                sval = [fs(f"sval{i}", [128, 2 * C8], F32) for i in range(2)]
                idg = fs("idg", [128, GPE * E8 * 2], F32)
                ido = fs("ido", [128, GPE * E8], I32)
                lgb = fs("lgb", [128, E8], F32)
                tmp8 = fs("tmp8", [128, E8], F32)
                m1b = fs("m1b", [128, 1], F32)
                m2b = fs("m2b", [128, 1], F32)
                nm2b = fs("nm2b", [128, 1], F32)

                hP = [fp(f"hP{i}", [128, NCH], F32) for i in range(2)]
                ogP = [fp(f"ogP{i}", [128, CAP2], F32) for i in range(2)]
                trP = [fp(f"trP{i}", [128, 512], BF16) for i in range(2)]
                bcP = [fp(f"bcP{i}", [128, NCH], F32) for i in range(2)]

                ln_stats(tag, [h1[0][:, 0:NCH], h1[1][:, 0:NCH]],
                         [xgog[0][:, 0:NCH], xgog[1][:, 0:NCH]],
                         [[K("h1", 0)], [K("h1", 1)]],
                         [[K("xgog", 0, 0)], [K("xgog", 1, 0)]],
                         bcP[0], K("bcP", 0), bcP[1], K("bcP", 1),
                         evt, K("evt"), eps5,
                         rowt2=evt2, rowt2K=K("evt2"),
                         rstdP=hP[0], rstdPk=K("hP", 0))

                if ("srow",) not in weight_cached:
                    weight_cached.add(("srow",))
                    for kpt in range(DP):
                        b.op("tensor", lambda E, kpt=kpt: E.matmul(
                            bcP[0][0:1, 0:E8], ocf[:, :],
                            rtsF[:, kpt * (E8 + 1):kpt * (E8 + 1) + E8],
                            start=(kpt == 0), stop=(kpt == DP - 1)),
                            reads=["ocf", "rtsF"], writes=[K("bcP", 0)])
                    b.op("scalar", lambda E: E.copy(
                        srow[0:1, :], bcP[0][0:1, 0:E8]),
                        reads=[K("bcP", 0)], writes=["srow"])
                    b.op("tensor", lambda E: E.matmul(
                        bcP[0][:, 0:E8], orf[:, :], srow[0:1, :],
                        start=True, stop=True),
                        reads=["srow", "orf"], writes=[K("bcP", 0)])
                    b.op("scalar", lambda E: E.copy(
                        sbc[:, :], bcP[0][:, 0:E8]),
                        reads=[K("bcP", 0)], writes=["sbc"])

                for par in range(2):
                    b.op("gpsimd", lambda E, par=par: E.memset(
                        xgog[par][:, :], 0.0),
                        writes=[K("xgog", par, g) for g in range(GPE)])
                for r in range(2):
                    b.op("vector", lambda E, r=r: E.tensor_copy(
                        sval[r][:, :].rearrange("p (c v) -> p c v", v=2)
                        [:, :, 0:1],
                        tvl[:, :].rearrange("p (c one) -> p c one", one=1)),
                        reads=["tvl"], writes=[K("sval", r)])

                def stream_fc(e, fcc, buf, bufk):
                    ck = ("wfc", e, fcc)
                    if ck not in weight_cached:
                        weight_cached.add(ck)
                        b.op("gpsimd", lambda E, e=e, fcc=fcc, buf=buf:
                             E.dma_start(
                                 out=buf[:, :].rearrange(
                                     "p (k c) -> p k c", k=DP),
                                 in_=fct.rearrange(
                                     "e (k p) c -> e p k c", p=128)
                                 [e, :, :, fcc * FCC:(fcc + 1) * FCC]),
                             writes=[bufk],
                             dma=True, dsem=f"fcr{fcc % 2}")
                        b.op("sync", lambda E, e=e, fcc=fcc, buf=buf:
                             E.dma_start(out=fctb[e, fcc], in_=buf[:, :]),
                             reads=[bufk], writes=[("fctb", e, fcc)],
                             dma=True, dsem=f"fcw{fcc % 2}")
                    else:
                        b.op("gpsimd", lambda E, e=e, fcc=fcc, buf=buf:
                             E.dma_start(out=buf[:, :], in_=fctb[e, fcc]),
                             reads=[("fctb", e, fcc)], writes=[bufk],
                             dma=True, dsem=f"fcr{fcc % 2}")

                def stream_pj(e, mo, buf, bufk, par):
                    ck = ("wpj", e, mo)
                    if ck not in weight_cached:
                        weight_cached.add(ck)
                        b.op("gpsimd", lambda E, e=e, mo=mo, buf=buf:
                             E.dma_start(
                                 out=buf[:, :].rearrange(
                                     "p (k c) -> p k c", k=DFP),
                                 in_=pjt.rearrange(
                                     "e (k p) c -> e p k c", p=128)
                                 [e, :, :, mo * 128:(mo + 1) * 128]),
                             writes=[bufk],
                             dma=True, dsem=f"pjc{par}")
                        b.op("sync", lambda E, e=e, mo=mo, buf=buf:
                             E.dma_start(
                                 out=pjtb[e, mo], in_=buf[:, :]),
                             reads=[bufk], writes=[("pjtb", e, mo)],
                             dma=True, dsem=f"pjw{par}")
                    else:
                        b.op("sync", lambda E, e=e, mo=mo, buf=buf:
                             E.dma_start(
                                 out=buf[:, :], in_=pjtb[e, mo]),
                             reads=[("pjtb", e, mo)], writes=[bufk],
                             dma=True, dsem=f"pjr{par}")

                nchn = TILE // NCH
                for tile in range(NT):
                    tt0 = tile * TILE
                    for tc in range(C8):
                        t0c = tt0 + tc * 128
                        for pt in range(DP):
                            b.op("tensor", lambda E, pt=pt, t0c=t0c:
                                 E.matmul(
                                     bcP[0][:, 0:E8 + 1],
                                     xsl(pt, t0c, t0c + 128),
                                     rtsF[:, pt * (E8 + 1):
                                          (pt + 1) * (E8 + 1)],
                                     start=(pt == 0), stop=(pt == DP - 1)),
                                 reads=xks(pt, t0c, t0c + 128) + ["rtsF"],
                                 writes=[K("bcP", 0)])
                        b.op("vector", lambda E: E.tensor_scalar(
                            m1b[:, :], bcP[0][:, E8:E8 + 1], 1.0 / D, None,
                            ALU.mult),
                            reads=[K("bcP", 0)], writes=[K("mcolt")])
                        b.op("vector", lambda E: E.tensor_scalar(
                            tmp8[:, :], sbc[:, :], m1b[:, 0:1], None,
                            ALU.mult),
                            reads=["sbc", K("mcolt")], writes=[K("tmp8")])
                        b.op("vector", lambda E: E.tensor_tensor(
                            lgb[:, :], bcP[0][:, 0:E8], tmp8[:, :],
                            ALU.subtract),
                            reads=[K("bcP", 0), K("tmp8")], writes=[K("lgb")])
                        b.op("vector", lambda E, t0c=t0c: E.tensor_scalar(
                            lgb[:, :], lgb[:, :],
                            rstdc[:, t0c // 128:t0c // 128 + 1], None,
                            ALU.mult),
                            reads=[K("lgb"), "rstdc"], writes=[K("lgb")])
                        b.op("vector", lambda E: E.reduce_max(
                            m1b[:, :], lgb[:, :], AX.X),
                            reads=[K("lgb")], writes=[K("m1b")])
                        o1 = oha[0][:, tc * E8:(tc + 1) * E8]
                        o2 = oha[1][:, tc * E8:(tc + 1) * E8]
                        b.op("vector", lambda E, o1=o1: E.tensor_scalar(
                            o1, lgb[:, :], m1b[:, 0:1], None, ALU.is_ge),
                            reads=[K("lgb"), K("m1b")], writes=[K("oha", 0)])
                        b.op("vector", lambda E, o1=o1: E.scalar_tensor_tensor(
                            tmp8[:, :], o1, -1e30, lgb[:, :],
                            ALU.mult, ALU.add),
                            reads=[K("oha", 0), K("lgb")], writes=[K("tmp8")])
                        b.op("vector", lambda E: E.reduce_max(
                            m2b[:, :], tmp8[:, :], AX.X),
                            reads=[K("tmp8")], writes=[K("m2b")])
                        b.op("vector", lambda E, o2=o2: E.tensor_scalar(
                            o2, tmp8[:, :], m2b[:, 0:1], None, ALU.is_ge),
                            reads=[K("tmp8"), K("m2b")], writes=[K("oha", 1)])
                        b.op("scalar", lambda E: E.activation(
                            nm2b[:, :], m2b[:, 0:1], AF.Copy, scale=-1.0),
                            reads=[K("m2b")], writes=[K("nm2b")])
                        b.op("scalar", lambda E, tc=tc: E.activation(
                            sval[0][:, 2 * tc + 1:2 * tc + 2], m1b[:, 0:1],
                            AF.Sigmoid, bias=nm2b[:, 0:1]),
                            reads=[K("m1b"), K("nm2b")], writes=[K("sval", 0)])
                        b.op("scalar", lambda E, tc=tc: E.activation(
                            sval[1][:, 2 * tc + 1:2 * tc + 2], m1b[:, 0:1],
                            AF.Sigmoid, bias=m2b[:, 0:1], scale=-1.0),
                            reads=[K("m1b"), K("m2b")], writes=[K("sval", 1)])
                        for r in range(2):
                            oo = oha[r][:, tc * E8:(tc + 1) * E8]
                            b.op("tensor", lambda E, oo=oo: E.transpose(
                                bcP[1][0:E8, 0:128], oo, idf[:, :]),
                                reads=[K("oha", r), "idf"],
                                writes=[K("bcP", 1)])
                            b.op("scalar", lambda E, r=r, tc=tc: E.copy(
                                SEL[r][:, tc * 128:(tc + 1) * 128],
                                bcP[1][0:E8, 0:128]),
                                reads=[K("bcP", 1)], writes=[K("xg")])

                    for c in range(nchn):
                        t0 = tt0 + c * NCH
                        b.op("tensor", lambda E, t0=t0: E.matmul(
                            bcP[0][:, :], orb[:, :], statA[0:1, t0:t0 + NCH],
                            start=True, stop=True),
                            reads=["statA", "orb"], writes=[K("bcP", 0)])
                        b.op("tensor", lambda E, t0=t0: E.matmul(
                            bcP[1][:, :], orb[:, :], statB[0:1, t0:t0 + NCH],
                            start=True, stop=True),
                            reads=["statB", "orb"], writes=[K("bcP", 1)])
                        for pt in range(DP):
                            b.op("vector", lambda E, pt=pt, t0=t0:
                                 E.tensor_tensor(
                                     evt[:, 0:NCH], xsl(pt, t0, t0 + NCH),
                                     bcP[0][:, :], ALU.subtract),
                                 reads=xks(pt, t0, t0 + NCH) + [K("bcP", 0)],
                                 writes=[K("evt")])
                            b.op("vector", lambda E, pt=pt, c=c:
                                 E.tensor_tensor(
                                     xln[:, pt * TILE + c * NCH:
                                         pt * TILE + (c + 1) * NCH],
                                     evt[:, 0:NCH], bcP[1][:, :], ALU.mult),
                                 reads=[K("evt"), K("bcP", 1)],
                                 writes=[K("xln", pt)])

                    for c in range(C8):
                        for q in range(DP // TQ):
                            par = (c * (DP // TQ) + q) % 2
                            for j in range(TQ):
                                pt = q * TQ + j
                                b.op("tensor", lambda E, c=c, pt=pt, j=j,
                                     par=par: E.transpose(
                                         trP[par][:, j * 128:(j + 1) * 128],
                                         xln[:, pt * TILE + c * 128:
                                             pt * TILE + (c + 1) * 128],
                                         idb[:, :]),
                                     reads=[K("xln", pt), "idb"],
                                     writes=[K("trP", par)])
                            b.op("scalar", lambda E, c=c, q=q, par=par:
                                 E.copy(xtm[:, c * D + q * TQ * 128:
                                            c * D + (q + 1) * TQ * 128],
                                        trP[par][:, 0:TQ * 128]),
                                 reads=[K("trP", par)], writes=[K("xtm")])
                    b.op("sync", lambda E: E.dma_start(
                        out=xtmd_t.ap()[0:TILE, :].rearrange("(p c) d -> p c d", p=128),
                        in_=xtm[:, :].rearrange("p (c d) -> p c d", c=C8)),
                        reads=[K("xtm")], writes=[("xtmd",)], dma=True,
                        dsem="xtmd")

                    for r in range(2):
                        for c in range(TILE // NCH):
                            init = (0.0 if c == 0 else
                                    pos[:, c * NCH - 1:c * NCH])
                            b.op("vector", lambda E, c=c, r=r, init=init:
                                 E.tensor_tensor_scan(
                                     pos[0:E8, c * NCH:(c + 1) * NCH],
                                     SEL[r][:, c * NCH:(c + 1) * NCH],
                                     zcol[0:E8, 0:NCH], init,
                                     ALU.add, ALU.add),
                                 reads=[K("xg"), "zcol", K("pos")],
                                 writes=[K("pos")])
                        for tc in range(C8):
                            b.op("tensor", lambda E, tc=tc: E.transpose(
                                bcP[1][0:128, 0:E8],
                                pos[:, tc * 128:(tc + 1) * 128],
                                idf[0:E8, 0:E8]),
                                reads=[K("pos"), "idf"], writes=[K("bcP", 1)])
                            b.op("scalar", lambda E, tc=tc: E.copy(
                                posm[:, tc * E8:(tc + 1) * E8],
                                bcP[1][0:128, 0:E8]),
                                reads=[K("bcP", 1)], writes=[K("posm")])
                        b.op("vector", lambda E, r=r: E.tensor_tensor(
                            posm[:, :], posm[:, :], oha[r][:, :], ALU.mult),
                            reads=[K("posm"), K("oha", r)], writes=[K("posm")])
                        b.op("vector", lambda E, r=r: E.tensor_reduce(
                            pa_[r][:, :].rearrange("p (c one) -> p c one", one=1),
                            posm[:, :].rearrange("p (c e) -> p c e", e=E8),
                            AX.X, ALU.add),
                            reads=[K("posm")], writes=[K("pa", r)])
                        b.op("vector", lambda E, r=r: E.tensor_scalar(
                            pa_[r][:, :], pa_[r][:, :], -1.0, None, ALU.add),
                            reads=[K("pa", r)], writes=[K("pa", r)])
                        b.op("vector", lambda E, r=r: E.tensor_tensor(
                            posm[:, :], ecols[:, :], oha[r][:, :], ALU.mult),
                            reads=["ecols", K("oha", r), K("posm")],
                            writes=[K("posm")])
                        b.op("vector", lambda E, r=r: E.tensor_reduce(
                            ea_[r][:, :].rearrange("p (c one) -> p c one", one=1),
                            posm[:, :].rearrange("p (c e) -> p c e", e=E8),
                            AX.X, ALU.add),
                            reads=[K("posm")], writes=[K("ea", r)])
                        ST = 2 * E8 * NG
                        b.op("vector", lambda E, r=r: E.tensor_copy(
                            ii_[0][:, :], pa_[r][:, :]),
                            reads=[K("pa", r)], writes=[K("ii", 0)])
                        b.op("vector", lambda E: E.tensor_scalar(
                            ii_[1][:, :], ii_[0][:, :], 7, None,
                            ALU.arith_shift_right),
                            reads=[K("ii", 0)], writes=[K("ii", 1)])
                        b.op("vector", lambda E: E.tensor_scalar(
                            ii_[0][:, :], ii_[0][:, :], 127, None,
                            ALU.bitwise_and),
                            reads=[K("ii", 0)], writes=[K("ii", 0)])
                        b.op("vector", lambda E, r=r: E.tensor_copy(
                            ii_[2][:, :], ea_[r][:, :]),
                            reads=[K("ea", r)], writes=[K("ii", 2)])
                        b.op("vector", lambda E, r=r: E.tensor_scalar(
                            ii_[2][:, :], ii_[2][:, :], 2 * NG, r * NG,
                            ALU.mult, ALU.add),
                            reads=[K("ii", 2)], writes=[K("ii", 2)])
                        b.op("vector", lambda E: E.tensor_tensor(
                            ii_[2][:, :], ii_[2][:, :], ii_[1][:, :],
                            ALU.add),
                            reads=[K("ii", 2), K("ii", 1)], writes=[K("ii", 2)])
                        b.op("vector", lambda E, ST=ST: E.tensor_scalar(
                            ii_[0][:, :], ii_[0][:, :], ST, None, ALU.mult),
                            reads=[K("ii", 0)], writes=[K("ii", 0)])
                        b.op("vector", lambda E: E.tensor_tensor(
                            ii_[2][:, :], ii_[2][:, :], ii_[0][:, :],
                            ALU.add),
                            reads=[K("ii", 2), K("ii", 0)], writes=[K("ii", 2)])
                        b.op("vector", lambda E, r=r: E.tensor_scalar(
                            pa_[r][:, :], pa_[r][:, :], float(CAPR), None,
                            ALU.is_ge),
                            reads=[K("pa", r)], writes=[K("pa", r)])
                        b.op("vector", lambda E, r=r: E.tensor_copy(
                            ii_[3][:, :], pa_[r][:, :]),
                            reads=[K("pa", r)], writes=[K("ii", 3)])
                        b.op("vector", lambda E: E.tensor_scalar(
                            ii_[3][:, :], ii_[3][:, :], SLOT_TOT, None,
                            ALU.mult),
                            reads=[K("ii", 3)], writes=[K("ii", 3)])
                        b.op("vector", lambda E: E.tensor_tensor(
                            ii_[2][:, :], ii_[2][:, :], ii_[3][:, :],
                            ALU.add),
                            reads=[K("ii", 2), K("ii", 3)], writes=[K("ii", 2)])
                        b.op("vector", lambda E, r=r: E.tensor_scalar_min(
                            si_[r][:, :], ii_[2][:, :], SLOT_TOT),
                            reads=[K("ii", 2)], writes=[K("si", r)])

                    b.op("sync", lambda E: E.dma_start(
                        out=igbd_t.ap()[0:SLOT_TOT, :].rearrange("(p a) v -> p (a v)", p=128),
                        in_=oobs[:, :]),
                        reads=["oobs"], writes=[("igbd",)], dma=True,
                        dsem="igi")
                    for tc in range(C8):
                        for r in range(2):
                            if DBG["no_igs"]:
                                continue
                            b.op("gpsimd", lambda E, tc=tc, r=r:
                                 E.indirect_dma_start(
                                     igbd_t.ap(), IndirectOffsetOnAxis(
                                         si_[r][:, tc:tc + 1], 0),
                                     sval[r][:, 2 * tc:2 * tc + 2], None),
                                 reads=[("igbd",), K("si", r), K("sval", r)],
                                 dma=True, dsem=f"igs{r}")
                    b.op("sync", lambda E: E.dma_start(
                        out=idg[:, :].rearrange("p (a v) -> p a v", v=2),
                        in_=igbd_t.ap()[0:SLOT_TOT, :].rearrange("(p a) v -> p a v", p=128)),
                        reads=[("igbd",)], writes=[("igbd",), K("idg")],
                        dma=True, dsem="igr")
                    b.op("vector", lambda E: E.tensor_copy(
                        ido[:, :].rearrange("p (a one) -> p a one", one=1),
                        idg[:, :].rearrange("p (a v) -> p a v", v=2)
                        [:, :, 0:1]),
                        reads=[K("idg")], writes=[K("ido")])

                    e = E8
                    nfc = DFF // FCC
                    mpc = FCC // 128
                    for c2 in range(nchn):
                        for fcc in range(nfc):
                            fb_ = fcb[fcc % 2]
                            fbk = K("fcb", fcc % 2)
                            stream_fc(e, fcc, fb_, fbk)
                            for m in range(mpc):
                                mpt = fcc * mpc + m
                                pp = mpt % 2
                                hPk = K("hP", pp)
                                for kpt in range(DP):
                                    b.op("tensor", lambda E, fb_=fb_, m=m,
                                         c2=c2, kpt=kpt, pp=pp: E.matmul(
                                             hP[pp][:, :],
                                             fb_[:, kpt * FCC + m * 128:
                                                 kpt * FCC + (m + 1) * 128],
                                             xln[:, kpt * TILE + c2 * NCH:
                                                 kpt * TILE + (c2 + 1) * NCH],
                                             start=(kpt == 0),
                                             stop=(kpt == DP - 1)),
                                         reads=[fbk, K("xln", kpt)],
                                         writes=[hPk])
                                hb = h1[pp]
                                b.op("scalar", lambda E, hb=hb, pp=pp:
                                     E.activation(hb[:, 0:NCH],
                                                  hP[pp][:, :], AF.Relu),
                                     reads=[hPk], writes=[K("h1", pp)])
                                b.op("vector", lambda E, hb=hb, mpt=mpt:
                                     E.tensor_tensor(
                                         hg[:, mpt * NCH:(mpt + 1) * NCH],
                                         hb[:, 0:NCH], hb[:, 0:NCH],
                                         ALU.mult),
                                     reads=[K("h1", pp)],
                                     writes=[K("hg", mpt)])
                        for mo in range(DP):
                            pp = mo % 2
                            ogPk = K("ogP", pp)
                            pb_ = pjb[pp]
                            pbk = K("pjb", pp)
                            stream_pj(e, mo, pb_, pbk, pp)
                            for kk in range(DFP):
                                b.op("tensor", lambda E, pb_=pb_, kk=kk,
                                     pp=pp: E.matmul(
                                         ogP[pp][:, 0:NCH],
                                         pb_[:, kk * 128:(kk + 1) * 128],
                                         hg[:, kk * NCH:(kk + 1) * NCH],
                                         start=(kk == 0),
                                         stop=(kk == DFP - 1)),
                                     reads=[pbk, K("hg", kk)],
                                     writes=[ogPk])
                            t0 = tt0 + c2 * NCH
                            b.op("vector", lambda E, mo=mo, t0=t0, pp=pp:
                                 E.scalar_tensor_tensor(
                                     xsl(mo, t0, t0 + NCH),
                                     ogP[pp][:, 0:NCH], 0.5,
                                     xsl(mo, t0, t0 + NCH),
                                     ALU.mult, ALU.add),
                                 reads=[ogPk, xk(mo, t0 // XBLK)], writes=[xk(mo, t0 // XBLK)])

                    for e in range(E8):
                        epar = e % 2
                        for g in range(GPE):
                            if DBG["no_gx"]:
                                continue
                            b.op("gpsimd", lambda E, e=e, g=g, epar=epar:
                                 E.indirect_dma_start(
                                     xgog[epar][:, g * D:(g + 1) * D], None,
                                     xtmd_t.ap(), IndirectOffsetOnAxis(
                                         ido[:, e * GPE + g:
                                             e * GPE + g + 1], 0)),
                                 reads=[("xtmd",), K("ido"),
                                        K("xgog", epar, g)] +
                                       [("xtmd", "dump", gg, off)
                                        for gg in range(NDUMP)
                                        for off in range(0, D,
                                                         max(SC, NCH))],
                                 writes=[K("xgog", epar, g)], dma=True,
                                 dsem=f"gx{epar}{g}")
                        for g in range(GPE):
                            for q in range(DP // TQ):
                                par = (g * (DP // TQ) + q) % 2
                                for j in range(TQ):
                                    pt = q * TQ + j
                                    b.op("tensor", lambda E, g=g, pt=pt, j=j,
                                         par=par, epar=epar: E.transpose(
                                             trP[par][:, j * 128:
                                                      (j + 1) * 128],
                                             xgog[epar][:, g * D + pt * 128:
                                                  g * D + (pt + 1) * 128],
                                             idb[:, :]),
                                         reads=[K("xgog", epar, g), "idb"],
                                         writes=[K("trP", par)])
                                b.op("scalar", lambda E, g=g, q=q, par=par:
                                     E.copy(
                                         xg[:, :].rearrange(
                                             "p (k s) -> p k s", k=DP)
                                         [:, q * TQ:(q + 1) * TQ,
                                          g * 128:(g + 1) * 128],
                                         trP[par][:, 0:TQ * 128].rearrange(
                                             "p (k s) -> p k s", k=TQ)),
                                     reads=[K("trP", par)],
                                     writes=[K("xg")])
                        for fcc in range(nfc):
                            fb_ = fcb[fcc % 2]
                            fbk = K("fcb", fcc % 2)
                            stream_fc(e, fcc, fb_, fbk)
                            for m in range(mpc):
                                mpt = fcc * mpc + m
                                pp = mpt % 2
                                hPk = K("hP", pp)
                                for kpt in range(DP):
                                    b.op("tensor", lambda E, fb_=fb_, m=m,
                                         kpt=kpt, pp=pp: E.matmul(
                                             hP[pp][:, 0:CAP2],
                                             fb_[:, kpt * FCC + m * 128:
                                                 kpt * FCC + (m + 1) * 128],
                                             xg[:, kpt * CAP2:
                                                (kpt + 1) * CAP2],
                                             start=(kpt == 0),
                                             stop=(kpt == DP - 1)),
                                         reads=[fbk, K("xg")],
                                         writes=[hPk])
                                hb = h1[pp]
                                b.op("scalar", lambda E, hb=hb, pp=pp:
                                     E.activation(hb[:, 0:CAP2],
                                                  hP[pp][:, 0:CAP2], AF.Relu),
                                     reads=[hPk], writes=[K("h1", pp)])
                                b.op("vector", lambda E, hb=hb, mpt=mpt:
                                     E.tensor_tensor(
                                         hg[:, mpt * NCH:mpt * NCH + CAP2],
                                         hb[:, 0:CAP2], hb[:, 0:CAP2],
                                         ALU.mult),
                                     reads=[K("h1", pp)],
                                     writes=[K("hg", mpt)])
                        for mo in range(DP):
                            pp = mo % 2
                            ogPk = K("ogP", pp)
                            pb_ = pjb[pp]
                            pbk = K("pjb", pp)
                            stream_pj(e, mo, pb_, pbk, pp)
                            for kk in range(DFP):
                                b.op("tensor", lambda E, pb_=pb_, kk=kk,
                                     pp=pp: E.matmul(
                                         ogP[pp][:, 0:CAP2],
                                         pb_[:, kk * 128:(kk + 1) * 128],
                                         hg[:, kk * NCH:kk * NCH + CAP2],
                                         start=(kk == 0),
                                         stop=(kk == DFP - 1)),
                                     reads=[pbk, K("hg", kk)],
                                     writes=[ogPk])
                            b.op("scalar", lambda E, mo=mo, pp=pp: E.copy(
                                ogf[:, mo * CAP2:(mo + 1) * CAP2],
                                ogP[pp][:, 0:CAP2]),
                                reads=[ogPk], writes=[K("xg")])
                        for g in range(GPE):
                            for q in range(DP // TQ):
                                par = (g * (DP // TQ) + q) % 2
                                for j in range(TQ):
                                    pt = q * TQ + j
                                    b.op("tensor", lambda E, g=g, pt=pt, j=j,
                                         par=par: E.transpose(
                                             trP[par][:, j * 128:
                                                      (j + 1) * 128],
                                             ogf[:, pt * CAP2 + g * 128:
                                                 pt * CAP2 + (g + 1) * 128],
                                             idb[:, :]),
                                         reads=[K("xg"), "idb"],
                                         writes=[K("trP", par)])
                                gc = idg[:, (e * GPE + g) * 2 + 1:
                                         (e * GPE + g) * 2 + 2]
                                b.op("vector", lambda E, g=g, q=q, par=par,
                                     gc=gc, epar=epar: E.tensor_scalar(
                                         xgog[epar][:, g * D + q * TQ * 128:
                                              g * D + (q + 1) * TQ * 128],
                                         trP[par][:, 0:TQ * 128],
                                         gc[:, 0:1], None, ALU.mult),
                                     reads=[K("trP", par), K("idg")],
                                     writes=[K("xgog", epar, g)])
                        for g in range(GPE):
                            if DBG["no_dsc"]:
                                continue
                            dst = dltA_t if g < NG else dltB_t
                            dk = ("dltA",) if g < NG else ("dltB",)
                            b.op("gpsimd", lambda E, e=e, g=g, dst=dst,
                                 epar=epar: E.indirect_dma_start(
                                     dst.ap(), IndirectOffsetOnAxis(
                                         ido[:, e * GPE + g:
                                             e * GPE + g + 1], 0),
                                     xgog[epar][:, g * D:(g + 1) * D], None),
                                 reads=[dk, K("ido"), K("xgog", epar, g)],
                                 dma=True, dsem=f"dsc{epar}{g}")

                    for dt_, dkk, dsm in ((dltA_t, ("dltA",), "dra"),
                                          (dltB_t, ("dltB",), "drb")):
                        b.op("sync", lambda E, dt_=dt_: E.dma_start(
                            out=xtm[:, :].rearrange("p (c d) -> p c d", c=C8),
                            in_=dt_.ap()[0:TILE, :].rearrange(
                                "(p c) d -> p c d", p=128)),
                            reads=[dkk], writes=[dkk, K("xtm")],
                            dma=True, dsem=dsm)
                        for pt in range(DP):
                            for cq in range(C8 // CQ):
                                par = (pt * (C8 // CQ) + cq) % 2
                                for j in range(CQ):
                                    c = cq * CQ + j
                                    b.op("tensor", lambda E, c=c, pt=pt, j=j,
                                         par=par: E.transpose(
                                             trP[par][:, j * 128:
                                                      (j + 1) * 128],
                                             xtm[:, c * D + pt * 128:
                                                 c * D + (pt + 1) * 128],
                                             idb[:, :]),
                                         reads=[K("xtm"), "idb"],
                                         writes=[K("trP", par)])
                                t0 = tt0 + cq * CQ * 128
                                b.op("vector", lambda E, pt=pt, t0=t0,
                                     par=par: E.scalar_tensor_tensor(
                                         xsl(pt, t0, t0 + CQ * 128),
                                         trP[par][:, 0:CQ * 128], 0.5,
                                         xsl(pt, t0, t0 + CQ * 128),
                                         ALU.mult, ALU.add),
                                     reads=[K("trP", par),
                                            xk(pt, t0 // XBLK)],
                                     writes=[xk(pt, t0 // XBLK)])
                b.barrier()

        def attn_phase(tag):
            at_ctx = ExitStack()
            with at_ctx:
                fs = lambda nm, sh, dt: at_ctx.enter_context(
                    nc.sbuf_tensor(tag + nm, sh, dt))
                fp = lambda nm, sh, dt: at_ctx.enter_context(
                    nc.psum_tensor(tag + nm, sh, dt))
                K = lambda *a: (tag,) + a

                ws = {}
                for wi, (nm, dram) in enumerate([("wq", wqd), ("wk", wkd),
                                                 ("wv", wvd), ("wo", wod)]):
                    ws[nm] = fs(nm, [128, DP * D], BF16)
                    ck = ("watt", nm)
                    if ck not in weight_cached:
                        weight_cached.add(ck)
                        b.op("gpsimd", lambda E, s=ws[nm], dram=dram,
                             nm=nm: E.dma_start(
                                 out=s[:, :].rearrange(
                                     "p (k m) -> p k m", k=DP),
                                 in_=dram.rearrange(
                                     "(k p) m -> p k m", p=128)),
                             writes=[K(nm)],
                             dma=True, dsem="wc_" + nm)
                        b.op("sync", lambda E, s=ws[nm], wi=wi: E.dma_start(
                            out=awtb[wi], in_=s[:, :]),
                            reads=[K(nm)], writes=[("awtb", wi)],
                            dma=True, dsem="wsw")
                    else:
                        b.op("sync", lambda E, s=ws[nm], wi=wi: E.dma_start(
                            out=s[:, :], in_=awtb[wi]),
                            reads=[("awtb", wi)], writes=[K(nm)],
                            dma=True, dsem="w_" + nm)

                xlnc = fs("xlnc", [128, DP * CH], BF16)
                bcMc = fs("bcMc", [128, CH], F32)
                bcRc = fs("bcRc", [128, CH], F32)
                qb = fs("qb", [128, DP * CH], BF16)
                fb = fs("fb", [128, DP * CH], BF16)
                vb = fs("vb", [128, DP * CH], BF16)
                ob = fs("ob", [128, DP * CH], BF16)
                lfb = fs("lfb", [128, DP * CH], F32)
                cb_ = fs("cb_", [128, DP * CH], F32)
                QB = fs("QB", [128, DP * CH], BF16)
                qRb = fs("qRb", [128, DP * CH], BF16)
                son = fs("son", [128, DP * CH], BF16)
                rbc = fs("rbc", [128, CH], F32)
                rrow = fs("rrow", [1, CH], F32)
                Sb_ = fs("Sb_", [128, DP * 128], BF16)
                qt = [fs(f"qt{i}", [128, SC], BF16) for i in range(2)]
                kt = [fs(f"kt{i}", [128, SC], BF16) for i in range(2)]
                kh = [fs(f"kh{i}", [128, SC], BF16) for i in range(2)]
                atm = [fs(f"atm{i}", [SC, SC], BF16) for i in range(2)]
                vk = [fs(f"vk{i}", [SC, 256], BF16) for i in range(2)]
                rcol = [fs(f"rcol{i}", [128, 1], F32) for i in range(2)]
                aevt = fs("aevt", [128, NCH], F32)

                qkvP = [fp(f"qkvP{i}", [128, max(CH, NCH)], F32)
                        for i in range(2)]
                glaA = [fp(f"glaA{i}", [128, 2 * SC], F32) for i in range(2)]
                glaT = [fp(f"glaT{i}", [SC, 2 * 128], BF16) for i in range(2)]
                glaS = [fp(f"glaS{i}", [128, 128], F32) for i in range(2)]

                nks = (NCH + CH - 1) // CH
                ln_stats(tag, [qb[:, 0:NCH], vb[:, 0:NCH]],
                         [fb[:, 0:NCH], ob[:, 0:NCH]],
                         [[K("qb", i) for i in range(nks)],
                          [K("vb", i) for i in range(nks)]],
                         [[K("fb", i) for i in range(nks)],
                          [K("ob", i) for i in range(nks)]],
                         qkvP[0], K("qkvP", 0), qkvP[1], K("qkvP", 1),
                         aevt, K("aevt"), eps5)

                b.op("gpsimd", lambda E: E.memset(Sb_[:, :], 0.0),
                     writes=[K("Sb_", h) for h in range(DP)])

                for ch in range(N // CH):
                    t0 = ch * CH
                    b.op("tensor", lambda E, t0=t0: E.matmul(
                        qkvP[0][:, 0:CH], orb[:, :], statA[0:1, t0:t0 + CH],
                        start=True, stop=True),
                        reads=["statA", "orb"], writes=[K("qkvP", 0)])
                    b.op("tensor", lambda E, t0=t0: E.matmul(
                        qkvP[1][:, 0:CH], orb[:, :], statB[0:1, t0:t0 + CH],
                        start=True, stop=True),
                        reads=["statB", "orb"], writes=[K("qkvP", 1)])
                    b.op("vector", lambda E: E.tensor_copy(
                        bcMc[:, :], qkvP[0][:, 0:CH]),
                        reads=[K("qkvP", 0)], writes=[K("bcMc")])
                    b.op("vector", lambda E: E.tensor_copy(
                        bcRc[:, :], qkvP[1][:, 0:CH]),
                        reads=[K("qkvP", 1)], writes=[K("bcRc")])
                    for pt in range(DP):
                        b.op("vector", lambda E, pt=pt, t0=t0:
                             E.tensor_tensor(aevt[:, 0:CH],
                                             xsl(pt, t0, t0 + CH),
                                             bcMc[:, :], ALU.subtract),
                             reads=xks(pt, t0, t0 + CH) + [K("bcMc")], writes=[K("aevt")])
                        b.op("vector", lambda E, pt=pt: E.tensor_tensor(
                            xlnc[:, pt * CH:(pt + 1) * CH], aevt[:, 0:CH],
                            bcRc[:, :], ALU.mult),
                            reads=[K("aevt"), K("bcRc")],
                            writes=[K("xlnc", pt)])
                    for nm, dst, dstk, act in [("wq", qb, "qb", AF.Silu),
                                               ("wk", fb, "fb", AF.Sigmoid),
                                               ("wv", vb, "vb", AF.Copy)]:
                        w = ws[nm]
                        for mo in range(DP):
                            pp = (mo // 2) % 2
                            co = (mo % 2) * CH
                            for kpt in range(DP):
                                b.op("tensor", lambda E, w=w, mo=mo, kpt=kpt,
                                     pp=pp, co=co: E.matmul(
                                         qkvP[pp][:, co:co + CH],
                                         w[:, kpt * D + mo * 128:
                                           kpt * D + (mo + 1) * 128],
                                         xlnc[:, kpt * CH:(kpt + 1) * CH],
                                         start=(kpt == 0),
                                         stop=(kpt == DP - 1)),
                                     reads=[K(nm), K("xlnc", kpt)],
                                     writes=[K("qkvP", pp)])
                            if mo % 2 == 1:
                                b.op("scalar", lambda E, dst=dst, mo=mo,
                                     pp=pp, act=act: E.activation(
                                         dst[:, (mo - 1) * CH:(mo + 1) * CH],
                                         qkvP[pp][:, 0:2 * CH], act),
                                     reads=[K("qkvP", pp)],
                                     writes=[K(dstk, mo - 1), K(dstk, mo)])
                    allf = [K("fb", mo) for mo in range(DP)]
                    alll = [K("lfb", mo) for mo in range(DP)]
                    b.op("scalar", lambda E: E.activation(
                        lfb[:, :], fb[:, :], AF.Sigmoid),
                        reads=allf, writes=alll)
                    b.op("scalar", lambda E: E.activation(
                        lfb[:, :], lfb[:, :], AF.Ln),
                        reads=alll, writes=alll)
                    mid = SC // 2
                    EMH = float(np.exp(-44.36))
                    allc = [K("cb", h) for h in range(DP)]
                    for h in range(DP):
                        for sc in range(NSC):
                            o0 = h * CH + sc * SC
                            b.op("vector", lambda E, o0=o0:
                                 E.tensor_tensor_scan(
                                     cb_[:, o0:o0 + SC], lfb[:, o0:o0 + SC],
                                     zcol[:, 0:SC], 0.0, ALU.add, ALU.add),
                                 reads=[K("lfb", h), "zcol"],
                                 writes=[K("cb", h)])
                    b.op("scalar", lambda E: E.activation(
                        QB[:, :], cb_[:, :], AF.Exp, scale=-1.0,
                        bias=hcolN[:, 0:1]),
                        reads=allc + ["hcolN"], writes=[K("QB")])
                    alll = [K("lfb", mo) for mo in range(DP)]
                    b.op("scalar", lambda E: E.activation(
                        lfb[:, :], cb_[:, :], AF.Exp, bias=hcolP[:, 0:1]),
                        reads=allc + ["hcolP"], writes=alll)
                    b.op("vector", lambda E: E.tensor_tensor(
                        qRb[:, :], qb[:, :], lfb[:, :], ALU.mult),
                        reads=[K("qb", h) for h in range(DP)] + alll,
                        writes=[K("qRb")])
                    b.op("vector", lambda E: E.tensor_tensor(
                        cb_[:, :], fb[:, :], QB[:, :], ALU.mult),
                        reads=[K("fb", h) for h in range(DP)] + [K("QB")],
                        writes=allc)
                    b.op("vector", lambda E: E.tensor_tensor(
                        QB[:, :], QB[:, :], cb_[:, :], ALU.subtract),
                        reads=allc + [K("QB")], writes=[K("QB")])
                    for sc in range(NSC):
                        for h in range(DP):
                            i = h % 2
                            o0 = h * CH + sc * SC
                            rmid = lfb[:, o0 + mid:o0 + mid + 1]
                            rend = lfb[:, o0 + SC - 1:o0 + SC]
                            b.op("vector", lambda E, rmid=rmid, i=i:
                                 E.reciprocal(rcol[i][:, :], rmid),
                                 reads=[K("lfb", h)], writes=[K("rcol", i)])
                            b.op("vector", lambda E, o0=o0, i=i:
                                 E.tensor_scalar(
                                     qt[i][:, :], qRb[:, o0:o0 + SC],
                                     rcol[i][:, 0:1], None, ALU.mult),
                                 reads=[K("qRb"), K("rcol", i)],
                                 writes=[K("qt", i)])
                            b.op("vector", lambda E, o0=o0, i=i, rmid=rmid:
                                 E.tensor_scalar(
                                     kt[i][:, :], QB[:, o0:o0 + SC],
                                     rmid, None, ALU.mult),
                                 reads=[K("QB"), K("lfb", h)],
                                 writes=[K("kt", i)])
                            b.op("vector", lambda E, o0=o0, i=i, rend=rend:
                                 E.tensor_scalar(
                                     kh[i][:, :], QB[:, o0:o0 + SC],
                                     rend, None, ALU.mult),
                                 reads=[K("QB"), K("lfb", h)],
                                 writes=[K("kh", i)])
                            b.op("tensor", lambda E, i=i: E.matmul(
                                glaA[i][:, 0:SC], kt[i][:, :], qt[i][:, :],
                                start=True, stop=True),
                                reads=[K("kt", i), K("qt", i)],
                                writes=[K("glaA", i)])
                            b.op("vector", lambda E, i=i: E.tensor_tensor(
                                atm[i][:, :], glaA[i][0:SC, 0:SC], mkb[:, :],
                                ALU.mult),
                                reads=[K("glaA", i), "mkb"],
                                writes=[K("atm", i)])
                            b.op("tensor", lambda E, o0=o0, i=i: E.transpose(
                                glaT[i][:, 0:128], vb[:, o0:o0 + SC],
                                idb[:, :]),
                                reads=[K("vb", h), "idb"],
                                writes=[K("glaT", i)])
                            b.op("tensor", lambda E, i=i: E.transpose(
                                glaT[i][:, 128:256], kh[i][:, :], idb[:, :]),
                                reads=[K("kh", i), "idb"],
                                writes=[K("glaT", i)])
                            b.op("scalar", lambda E, i=i: E.copy(
                                vk[i][:, :], glaT[i][0:SC, 0:256]),
                                reads=[K("glaT", i)], writes=[K("vk", i)])
                            b.op("tensor", lambda E, h=h, i=i, o0=o0:
                                 E.matmul(
                                     glaA[i][:, SC:2 * SC],
                                     Sb_[:, h * 128:(h + 1) * 128],
                                     qRb[:, o0:o0 + SC],
                                     start=True, stop=False),
                                 reads=[K("Sb_", h), K("qRb")],
                                 writes=[K("glaA", i)])
                            b.op("tensor", lambda E, i=i: E.matmul(
                                glaA[i][:, SC:2 * SC], vk[i][:, 0:128],
                                atm[i][:, :], start=False, stop=True),
                                reads=[K("vk", i), K("atm", i)],
                                writes=[K("glaA", i)])
                            b.op("scalar", lambda E, o0=o0, i=i: E.copy(
                                ob[:, o0:o0 + SC], glaA[i][:, SC:2 * SC]),
                                reads=[K("glaA", i)], writes=[K("ob", h)])
                            b.op("tensor", lambda E, i=i: E.matmul(
                                glaS[i][:, :], vk[i][:, 128:256],
                                vk[i][:, 0:128], start=True, stop=True),
                                reads=[K("vk", i)], writes=[K("glaS", i)])
                            b.op("vector", lambda E, h=h, i=i, rend=rend:
                                 E.tensor_scalar(
                                     Sb_[:, h * 128:(h + 1) * 128],
                                     Sb_[:, h * 128:(h + 1) * 128],
                                     rend, EMH, ALU.mult, ALU.mult),
                                 reads=[K("Sb_", h), K("lfb", h)],
                                 writes=[K("Sb_", h)])
                            b.op("vector", lambda E, h=h, i=i:
                                 E.scalar_tensor_tensor(
                                     Sb_[:, h * 128:(h + 1) * 128],
                                     glaS[i][:, :], EMH,
                                     Sb_[:, h * 128:(h + 1) * 128],
                                     ALU.mult, ALU.add),
                                 reads=[K("glaS", i), K("Sb_", h)],
                                 writes=[K("Sb_", h)])
                    b.op("vector", lambda E: E.tensor_tensor(
                        son[:, :], ob[:, :], ob[:, :], ALU.mult),
                        reads=[K("ob", hh) for hh in range(DP)],
                        writes=[K("son")])
                    for kpt in range(DP):
                        b.op("tensor", lambda E, kpt=kpt: E.matmul(
                            qkvP[0][0:1, 0:CH], ocb[:, :],
                            son[:, kpt * CH:(kpt + 1) * CH],
                            start=(kpt == 0), stop=(kpt == DP - 1)),
                            reads=[K("son"), "ocb"], writes=[K("qkvP", 0)])
                    b.op("scalar", lambda E: E.activation(
                        rrow[0:1, :], qkvP[0][0:1, 0:CH], AF.Ln,
                        scale=1.0 / D, bias=eps6[0:1, 0:1]),
                        reads=[K("qkvP", 0), "eps6"], writes=[K("rrow")])
                    b.op("scalar", lambda E: E.activation(
                        rrow[0:1, :], rrow[0:1, :], AF.Exp, scale=-0.5),
                        reads=[K("rrow")], writes=[K("rrow")])
                    b.op("tensor", lambda E: E.matmul(
                        qkvP[1][:, 0:CH], orf[:, :], rrow[0:1, :],
                        start=True, stop=True),
                        reads=[K("rrow"), "orf"], writes=[K("qkvP", 1)])
                    b.op("vector", lambda E: E.tensor_copy(
                        rbc[:, :], qkvP[1][:, 0:CH]),
                        reads=[K("qkvP", 1)], writes=[K("rbc")])
                    for mo in range(DP):
                        b.op("vector", lambda E, mo=mo: E.scalar_tensor_tensor(
                            son[:, mo * CH:(mo + 1) * CH],
                            ob[:, mo * CH:(mo + 1) * CH],
                            nws[:, mo:mo + 1], rbc[:, :],
                            ALU.mult, ALU.mult),
                            reads=[K("ob", mo), "nws", K("rbc")],
                            writes=[K("son")])
                    for mo in range(DP):
                        pp = (mo // 2) % 2
                        co = (mo % 2) * CH
                        for kpt in range(DP):
                            b.op("tensor", lambda E, mo=mo, kpt=kpt, pp=pp,
                                 co=co: E.matmul(
                                     qkvP[pp][:, co:co + CH],
                                     ws["wo"][:, kpt * D + mo * 128:
                                              kpt * D + (mo + 1) * 128],
                                     son[:, kpt * CH:(kpt + 1) * CH],
                                     start=(kpt == 0), stop=(kpt == DP - 1)),
                                 reads=[K("wo"), K("son")],
                                 writes=[K("qkvP", pp)])
                        if mo % 2 == 1:
                            xpair = xs[:, :].rearrange(
                                "p (k n) -> p k n", k=DP)[
                                :, mo - 1:mo + 1, t0:t0 + CH]
                            b.op("vector", lambda E, xpair=xpair, pp=pp:
                                 E.tensor_tensor(
                                     xpair, qkvP[pp][:, 0:2 * CH],
                                     xpair, ALU.add),
                                 reads=[K("qkvP", pp), xk(mo - 1, t0 // XBLK), xk(mo, t0 // XBLK)],
                                 writes=[xk(mo - 1, t0 // XBLK), xk(mo, t0 // XBLK)])
                b.barrier()

        for bb in range(B):
            b.op("sync", lambda E, bb=bb: E.dma_start(
                out=xs[:, :].rearrange("p (k n) -> p k n", k=DP),
                in_=xd.rearrange("b (k p) n -> b p k n", p=128)[bb]),
                writes=[k for pt in range(DP) for k in xk(pt)], dma=True, dsem="xload")
            ffn_phase(f"F{bb}a_")
            if not DBG["ffn_only"]:
                attn_phase(f"A{bb}_")
                ffn_phase(f"F{bb}b_")
            b.op("sync", lambda E, bb=bb: E.dma_start(
                out=outd.rearrange("b (k p) n -> b p k n", p=128)[bb],
                in_=xs[:, :].rearrange("p (k n) -> p k n", k=DP)),
                reads=[k for pt in range(DP) for k in xk(pt)], dma=True, dsem="xstore")
        b.barrier()
        b.play(top)
    return nc


def _oobs_const(cfg):
    E8 = cfg["E"]
    NG = cfg["CAPR"] // 128
    ST = 2 * E8 * NG
    a = np.arange(ST)
    epar = (a // (2 * NG)) % 2
    gg = epar * (2 * NG) + a % (2 * NG)
    ids = cfg["TILE"] + gg.astype(np.float32)
    pat = np.zeros((ST, 2), dtype=np.float32)
    pat[:, 0] = ids
    return np.tile(pat.reshape(1, ST * 2), (128, 1))


def prep_inputs(L0_attn_x, L0_attn_Wq, L0_attn_Wk, L0_attn_Wv, L0_attn_Wo,
                L0_attn_norm_w, L0_ffn_router_w, L0_ffn_w_fc, L0_ffn_w_proj,
                L0_ffn_shared_fc, L0_ffn_shared_proj, cfg, n_cores=8):
    B, N, D, DFF, E8 = cfg["B"], cfg["N"], cfg["D"], cfg["DFF"], cfg["E"]
    SC = cfg["SC"]
    cc = np.ascontiguousarray
    ascc = cc
    x = np.asarray(L0_attn_x, dtype=np.float32)
    fctT = np.concatenate(
        [np.transpose(np.asarray(L0_ffn_w_fc, np.float32), (0, 2, 1)),
         np.transpose(np.asarray(L0_ffn_shared_fc, np.float32))[None]],
        axis=0)
    pjtT = np.concatenate(
        [np.transpose(np.asarray(L0_ffn_w_proj, np.float32), (0, 2, 1)),
         np.transpose(np.asarray(L0_ffn_shared_proj, np.float32))[None]],
        axis=0)
    common = {
        "wq": cc(np.asarray(L0_attn_Wq, np.float32).T),
        "wk": cc(np.asarray(L0_attn_Wk, np.float32).T),
        "wv": cc(np.asarray(L0_attn_Wv, np.float32).T),
        "wo": cc(np.asarray(L0_attn_Wo, np.float32).T),
        "nw": cc(np.asarray(L0_attn_norm_w, np.float32).reshape(D // 128, 128).T),
        "rt": cc(np.concatenate(
            [np.asarray(L0_ffn_router_w, np.float32).T,
             np.ones((D, 1), np.float32)], axis=1)),
        "fct": cc(fctT),
        "pjt": cc(pjtT),
        "identb": np.eye(128, dtype=BFNP),
        "identf": np.eye(128, dtype=np.float32),
        "onescb": np.ones((128, 1), dtype=BFNP),
        "onescf": np.ones((128, 1), dtype=np.float32),
        "onesrf": np.ones((1, 128), dtype=np.float32),
        "halfrf": np.full((1, 128), 0.5, dtype=np.float32),
        "halfrb": np.full((1, 128), 0.5, dtype=BFNP),
        "maskb": np.triu(np.ones((SC, SC), dtype=np.float32)).astype(BFNP),
        "onesrb": np.ones((1, 128), dtype=BFNP),
        "ecols": np.tile(np.arange(E8, dtype=np.float32),
                         (128, cfg["TILE"] // 128)),
        "tvl": (np.arange(128, dtype=np.float32)[:, None] *
                (cfg["TILE"] // 128) +
                np.arange(cfg["TILE"] // 128, dtype=np.float32)[None, :]),
        "oobs": _oobs_const(cfg),
    }
    in_maps = []
    for c in range(n_cores):
        m = dict(common)
        m["x"] = cc(np.transpose(x[c * B:(c + 1) * B], (0, 2, 1)))
        in_maps.append(m)
    return in_maps


def kernel(**inputs):
    cfg = REAL_CFG
    n_cores = 8
    in_maps = prep_inputs(n_cores=n_cores, cfg=cfg, **inputs)
    nc = build_program(cfg)
    res = run_bass_kernel_spmd(nc, in_maps, core_ids=list(range(n_cores)))
    outs = [np.transpose(r["out"], (0, 2, 1)) for r in res.results]
    return np.concatenate(outs, axis=0)
